# version 03
# baseline (speedup 1.0000x reference)
import numpy as np
import ml_dtypes
import concourse.bass as bass
import concourse.mybir as mybir
from concourse.bass_utils import run_bass_kernel_spmd

F32 = mybir.dt.float32
F32R = mybir.dt.float32r
BF16 = mybir.dt.bfloat16
AF = mybir.ActivationFunctionType
OP = mybir.AluOpType
AX = mybir.AxisListType

D = 1024
E = 2048
NS = 16
RK = 64
T = 2048
TC = 512
NTCH = T // TC
H = 8
DH = 64
INNER = 512
HID = 2730
NIT1 = 22
WIN = 512
NW = T // WIN
NEG = -1.0e30
EPS = 1e-5
NCORES = 8
STOP = None
NOREG = False
ALEVEL = 99
W2LEVEL = 99
DBG = False
DBG2 = False
DBG3 = False
DBG4 = False
DBGL = 0
DBG3A = True
DBG3B = True

ENGINES = ("sync", "tensor", "vector", "scalar", "gpsimd")
E_T, E_V, E_S, E_G, E_SY = "tensor", "vector", "scalar", "gpsimd", "sync"


def bAP(t_ap, offset, dims):
    return bass.AP(tensor=t_ap.tensor, offset=offset,
                   ap=[list(d) for d in dims])


class Ctx:
    def __init__(self):
        self.nc = bass.Bass("TRN2", target_bir_lowering=False, debug=False)
        self.q = {e: [] for e in ENGINES}
        self.cnt = {}
        self.sems = {}
        self._sbuf_stack = []
        self.psg = {}

    def sem(self, name):
        h = self.nc.semaphore(name).__enter__()
        self.sems[name] = h
        self.cnt[name] = 0

    def emit(self, eng, fn, inc=None, by=1):
        if inc is not None:
            self.cnt[inc] += by
            sem = self.sems[inc]

            def run(e, fn=fn, sem=sem, by=by):
                fn(e).then_inc(sem, by)
            self.q[eng].append(run)
            return self.cnt[inc]
        self.q[eng].append(fn)
        return None

    def wait(self, eng, sem_name, val):
        if val is None or val <= 0:
            return
        sem = self.sems[sem_name]
        self.q[eng].append(lambda e, s=sem, v=val: e.wait_ge(s, v))

    def sbuf(self, name, shape, dt):
        cm = self.nc.sbuf_tensor(name, list(shape), dt)
        h = cm.__enter__()
        self._sbuf_stack.append((name, cm))
        return h.ap() if hasattr(h, "ap") else h

    def sbuf_mark(self):
        return len(self._sbuf_stack)

    def sbuf_free_to(self, mark):
        while len(self._sbuf_stack) > mark:
            _, cm = self._sbuf_stack.pop()
            cm.__exit__(None, None, None)

    def ps_wait(self, eng, name):
        g = self.psg.get(name)
        if g is not None:
            self.wait(eng, g[0], g[1])

    def ps_consume(self, name, sem, val):
        self.psg[name] = (sem, val)

    def dma(self, queue, out, in_, sem=None):
        sem = sem or ("DG" if queue == E_G else ("DA" if queue == E_S else "DS"))
        return self.emit(queue, lambda e, o=out, i=in_: e.dma_start(out=o, in_=i),
                         inc=sem, by=16)

    def mm(self, psum, lhsT, rhs, start, stop, inc=None):
        return self.emit(E_T, lambda e, p=psum, l=lhsT, r=rhs, st=start, sp=stop:
                         e.matmul(p, l, r, start=st, stop=sp), inc=inc)

    def barrier(self):
        marks = {s: self.cnt[s] for s in ("T", "V", "S", "P", "DS", "DG", "DA", "DO")}
        for eng in (E_T, E_V, E_S, E_G, E_SY):
            for s, v in marks.items():
                self.wait(eng, s, v)

    def build(self):
        nc = self.nc
        with nc.Block() as block:
            @block.sync
            def _(e):
                for fn in self.q["sync"]:
                    fn(e)

            @block.tensor
            def _(e):
                for fn in self.q["tensor"]:
                    fn(e)

            @block.vector
            def _(e):
                for fn in self.q["vector"]:
                    fn(e)

            @block.scalar
            def _(e):
                for fn in self.q["scalar"]:
                    fn(e)

            @block.gpsimd
            def _(e):
                for fn in self.q["gpsimd"]:
                    fn(e)
        return nc


def host_constants():
    c = {}
    ident = np.eye(128, dtype=np.float32)
    c["ident_b"] = ident.astype(ml_dtypes.bfloat16)
    c["ident_f"] = ident
    c["onesc_r"] = np.ones((128, 1), dtype=np.float32)
    c["ones1_r"] = np.ones((1, 128), dtype=np.float32)
    a = np.arange(128)
    c["m_ul"] = np.where(a[:, None] < a[None, :], NEG, 0.0).astype(ml_dtypes.bfloat16)
    c["m_ll"] = np.where(a[:, None] > a[None, :], NEG, 0.0).astype(ml_dtypes.bfloat16)
    c["m_full"] = np.full((128, 128), NEG, dtype=ml_dtypes.bfloat16)
    hpat = np.zeros((128, 32), dtype=np.float32)
    for it in range(4):
        for p_ in range(128):
            hpat[p_, 8 * it + 2 * it + (p_ // 64)] = 1.0
    c["hpat"] = hpat
    hm8 = np.zeros((8, 512), dtype=np.float32)
    hm1 = np.zeros((8, 512), dtype=np.float32)
    for it in range(4):
        for iloc in range(128):
            h = 2 * it + (iloc // 64)
            hm8[h, 128 * it + iloc] = 8.0
            hm1[h, 128 * it + iloc] = 1.0
    c["hm8"] = hm8
    c["hm1"] = hm1
    bcp = np.zeros((16, 16 * 128), dtype=np.float32)
    for n in range(16):
        bcp[n, n * 128:(n + 1) * 128] = 1.0
    c["bcp"] = bcp
    c["fconst"] = np.tile(np.array([[EPS, -0.5, 1.0, -1.0]], dtype=np.float32),
                          (128, 1))
    half = np.arange(0, DH, 2, dtype=np.float64)
    invf = 1.0 / 10000.0 ** (half / DH)
    t = np.arange(T, dtype=np.float64)
    freqs = t[:, None] * invf[None, :]
    base = (half + 0.4 * DH) / (1.4 * DH)
    power = (t - T // 2) / float(WIN // 2)
    scale = base[None, :] ** power[:, None]
    cosf = np.concatenate([np.cos(freqs)] * 2, axis=1)
    sinf = np.concatenate([np.sin(freqs)] * 2, axis=1)
    scf = np.concatenate([scale] * 2, axis=1)
    f1q = (cosf * scf).T
    f1k = (cosf / scf).T
    g2q = (sinf * scf).T.copy()
    g2k = (sinf / scf).T.copy()
    g2q[0:32, :] *= -1.0
    g2k[0:32, :] *= -1.0
    rep = lambda m_: np.concatenate([m_, m_], axis=0).astype(np.float32)
    c["rot_f1q"] = rep(f1q)
    c["rot_f1k"] = rep(f1k)
    c["rot_g2q"] = rep(g2q)
    c["rot_g2k"] = rep(g2k)
    return c


def reorg_params(inp):
    d = {}
    d["w_in"] = np.ascontiguousarray(inp["L0_attn_W_in"], dtype=np.float32)
    wx_ = np.asarray(inp["L0_attn_W_x"], dtype=np.float32)
    d["w_xa"] = np.ascontiguousarray(wx_[:, 0:80])
    d["w_xc"] = np.ascontiguousarray(wx_[:, 80:96])
    d["w_dt"] = np.ascontiguousarray(inp["L0_attn_W_dt"], dtype=np.float32)
    d["w_out"] = np.ascontiguousarray(inp["L0_attn_W_out"], dtype=np.float32)
    cw = np.asarray(inp["L0_attn_conv_w"], dtype=np.float32).reshape(E, 4)
    cwl = np.zeros((128, 64), dtype=np.float32)
    for pb in range(16):
        for k in range(4):
            cwl[:, pb * 4 + k] = cw[pb * 128:(pb + 1) * 128, k]
    d["convw_l"] = cwl
    pcol = lambda v, nb: np.ascontiguousarray(
        np.asarray(v, dtype=np.float32).reshape(nb, 128).T)
    d["convb_l"] = pcol(inp["L0_attn_conv_b"], 16)
    d["Dvec_l"] = pcol(inp["L0_attn_D"], 16)
    d["bdt_l"] = pcol(inp["L0_attn_b_dt"], 16)
    d["alog"] = np.asarray(inp["L0_attn_A_log"], dtype=np.float32).reshape(1, NS)
    for l in range(2):
        d[f"wqkv{l}"] = np.ascontiguousarray(inp["L0_ffn_wqkv"][l], dtype=np.float32)
        d[f"wo{l}"] = np.ascontiguousarray(inp["L0_ffn_wo"][l], dtype=np.float32)
        d[f"w1{l}"] = np.ascontiguousarray(inp["L0_ffn_w1"][l], dtype=np.float32)
        d[f"w2{l}"] = np.ascontiguousarray(inp["L0_ffn_w2"][l], dtype=np.float32)
        d[f"qs{l}"] = np.asarray(inp["L0_ffn_q_scale"][l], dtype=np.float32).reshape(1, DH)
        d[f"ks{l}"] = np.asarray(inp["L0_ffn_k_scale"][l], dtype=np.float32).reshape(1, DH)
        d[f"lng{l}"] = pcol(inp["L0_ffn_ln_g"][l], 8)
        d[f"lnb{l}"] = pcol(inp["L0_ffn_ln_b"][l], 8)
        b1 = np.asarray(inp["L0_ffn_b1"][l], dtype=np.float32)
        b1a = np.zeros((128, NIT1), dtype=np.float32)
        b1g = np.zeros((128, NIT1), dtype=np.float32)
        for it in range(NIT1):
            lo = it * 128
            hi = min(lo + 128, HID)
            b1a[0:hi - lo, it] = b1[lo:hi]
            b1g[0:hi - lo, it] = b1[HID + lo:HID + hi]
        d[f"b1a{l}"] = b1a
        d[f"b1g{l}"] = b1g
        d[f"b2{l}"] = pcol(inp["L0_ffn_b2"][l], 8)
    return d


def build_program():
    c = Ctx()
    nc = c.nc

    def din(name, shape, dt):
        return nc.dram_tensor(name, list(shape), dt, kind="ExternalInput").ap()

    x_in = din("x", (T, D), F32)
    w_in = din("w_in", (D, 2 * E), F32R)
    w_xa = din("w_xa", (E, 80), F32R)
    w_xc = din("w_xc", (E, 16), F32R)
    w_dt = din("w_dt", (RK, E), F32R)
    w_out = din("w_out", (E, D), F32R)
    convw_d = din("convw_l", (128, 64), F32)
    convb_d = din("convb_l", (128, 16), F32)
    Dvec_d = din("Dvec_l", (128, 16), F32)
    bdt_d = din("bdt_l", (128, 16), F32)
    alog_d = din("alog", (1, NS), F32R)
    wqkv_d = [din(f"wqkv{l}", (D, 3 * INNER), F32R) for l in range(2)]
    wo_d = [din(f"wo{l}", (INNER, D), F32R) for l in range(2)]
    w1_d = [din(f"w1{l}", (D, 2 * HID), F32R) for l in range(2)]
    w2_d = [din(f"w2{l}", (HID, D), F32R) for l in range(2)]
    qs_d = [din(f"qs{l}", (1, DH), F32R) for l in range(2)]
    ks_d = [din(f"ks{l}", (1, DH), F32R) for l in range(2)]
    lng_d = [din(f"lng{l}", (128, 8), F32) for l in range(2)]
    lnb_d = [din(f"lnb{l}", (128, 8), F32) for l in range(2)]
    b1a_d = [din(f"b1a{l}", (128, NIT1), F32) for l in range(2)]
    b1g_d = [din(f"b1g{l}", (128, NIT1), F32) for l in range(2)]
    b2_d = [din(f"b2{l}", (128, 8), F32) for l in range(2)]
    identb_d = din("ident_b", (128, 128), BF16)
    identf_d = din("ident_f", (128, 128), F32)
    onesc_d = din("onesc_r", (128, 1), F32R)
    ones1_d = din("ones1_r", (1, 128), F32R)
    mul_d = din("m_ul", (128, 128), BF16)
    mll_d = din("m_ll", (128, 128), BF16)
    mfull_d = din("m_full", (128, 128), BF16)
    hpat_d = din("hpat", (128, 32), F32R)
    hm8_d = din("hm8", (8, 512), F32R)
    hm1_d = din("hm1", (8, 512), F32R)
    bcp_d = din("bcp", (16, 16 * 128), F32R)
    fconst_d = din("fconst", (128, 4), F32)
    rotf_d = [din("rot_f1q", (128, T), F32), din("rot_f1k", (128, T), F32)]
    rotg_d = [din("rot_g2q", (128, T), F32), din("rot_g2k", (128, T), F32)]

    out_d = nc.dram_tensor("out", [T, D], F32, kind="ExternalOutput").ap()
    dbg_uc = dbg_dt = None
    dbg_xa = dbg_oT = dbg_hn = None
    dbg_qh = dbg_kh = None
    dbg_at = dbg_nm = None
    if DBG2:
        dbg_uc = nc.dram_tensor("dbg_uc", [128, NTCH * 16 * TC], F32R,
                                kind="ExternalOutput").ap()
        dbg_dt = nc.dram_tensor("dbg_dt", [128, NTCH * 16 * TC], F32,
                                kind="ExternalOutput").ap()
    if DBG3:
        dbg_xa = nc.dram_tensor("dbg_xa", [128, 8 * T], F32R, kind="ExternalOutput").ap()
        dbg_oT = nc.dram_tensor("dbg_oT", [128, 4 * T], F32R, kind="ExternalOutput").ap()
        dbg_hn = nc.dram_tensor("dbg_hn", [128, 8 * T], F32R, kind="ExternalOutput").ap()
        dbg_qh = nc.dram_tensor("dbg_qh", [128, 4 * T], BF16, kind="ExternalOutput").ap()
        dbg_kh = nc.dram_tensor("dbg_kh", [128, 4 * T], BF16, kind="ExternalOutput").ap()
    if DBG4:
        dbg_oT = nc.dram_tensor("dbg_oT", [128, 4 * T], F32R, kind="ExternalOutput").ap()
        dbg_at = nc.dram_tensor("dbg_at", [128, 4 * 640], BF16, kind="ExternalOutput").ap()
        dbg_nm = nc.dram_tensor("dbg_nm", [128, 64], F32, kind="ExternalOutput").ap()
        dbg_vt = nc.dram_tensor("dbg_vt", [128, 8 * WIN], BF16, kind="ExternalOutput").ap()
        dbg_att = nc.dram_tensor("dbg_att", [128, 5 * 128], BF16, kind="ExternalOutput").ap()
    kind_dbg = "ExternalOutput" if DBG else "Internal"
    xct_h = nc.dram_tensor("xct_h", [D, T], F32R, kind=kind_dbg).ap()
    sa_h = nc.dram_tensor("sa_h", [D, T + 8], F32R, kind=kind_dbg).ap()
    sb_h = nc.dram_tensor("sb_h", [D, T + 8], F32R, kind=kind_dbg).ap()

    identb_s = c.sbuf("identb_s", (128, 128), BF16)
    identf_s = c.sbuf("identf_s", (128, 128), F32)
    onesc_s = c.sbuf("onesc_s", (128, 1), F32R)
    ones1_s = c.sbuf("ones1_s", (1, 128), F32R)
    mul_s = c.sbuf("mul_s", (128, 128), BF16)
    mll_s = c.sbuf("mll_s", (128, 128), BF16)
    mfull_s = c.sbuf("mfull_s", (128, 128), BF16)
    hpat_s = c.sbuf("hpat_s", (128, 32), F32R)
    convw_s = c.sbuf("convw_s", (128, 64), F32)
    convb_s = c.sbuf("convb_s", (128, 16), F32)
    Dvec_s = c.sbuf("Dvec_s", (128, 16), F32)
    bdt_s = c.sbuf("bdt_s", (128, 16), F32)
    alog_s = c.sbuf("alog_s", (1, NS), F32R)
    acol_s = c.sbuf("acol_s", (128, NS), F32)
    lng_s = [c.sbuf(f"lng_s{l}", (128, 8), F32) for l in range(2)]
    lnb_s = [c.sbuf(f"lnb_s{l}", (128, 8), F32) for l in range(2)]
    b1a_s = [c.sbuf(f"b1a_s{l}", (128, NIT1), F32) for l in range(2)]
    b1g_s = [c.sbuf(f"b1g_s{l}", (128, NIT1), F32) for l in range(2)]
    b2_s = [c.sbuf(f"b2_s{l}", (128, 8), F32) for l in range(2)]
    qs_s = [c.sbuf(f"qs_s{l}", (1, DH), F32R) for l in range(2)]
    ks_s = [c.sbuf(f"ks_s{l}", (1, DH), F32R) for l in range(2)]
    zero_s = c.sbuf("zero_s", (128, 16), F32)
    fconst_s = c.sbuf("fconst_s", (128, 4), F32)
    fc_eps = fconst_s[:, 0:1]
    fc_mhalf = fconst_s[:, 1:2]
    fc_one = fconst_s[:, 2:3]
    fc_mone = fconst_s[:, 3:4]
    qsr_s = c.sbuf("qsr_s", (1, 512), F32R)
    stat_s = c.sbuf("stat_s", (1, 3 * TC), F32)
    prow_s = c.sbuf("prow_s", (1, TC), F32R)
    qrow_s = c.sbuf("qrow_s", (1, TC), F32R)
    lnsq_s = c.sbuf("lnsq_s", (128, TC), F32R)
    fence_s = c.sbuf("fence_s", (128, 8), F32)

    psD = nc.psum_tensor("psD", [128, 1024], F32).__enter__().ap()
    psD2 = nc.psum_tensor("psD2", [128, 1024], F32).__enter__().ap()
    p4 = nc.psum_tensor("p4", [128, 512], F32).__enter__().ap()
    p5 = nc.psum_tensor("p5", [128, 512], F32).__enter__().ap()
    p6 = nc.psum_tensor("p6", [128, 512], F32).__enter__().ap()
    p7 = nc.psum_tensor("p7", [128, 512], F32).__enter__().ap()
    psD_a, psD_b = psD[:, 0:512], psD[:, 512:1024]
    psD2_a, psD2_b = psD2[:, 0:512], psD2[:, 512:1024]

    for s in ("DS", "DG", "DA", "DO", "T", "V", "S", "P"):
        c.sem(s)

    def layer_norm(src_fn, dst_fn, npb, width, sqF, ready_V, ready_T=0, gb=None):
        sq = lnsq_s
        """src_fn(pb)->AP read, dst_fn(pb)->AP write; normalizes over npb*128
        channels.  Uses p6/p7 for sums, psD_a/psD_b for P/Q bcast.
        Returns final V mark."""
        nd = npb * 128
        c.wait(E_V, "S", c.cnt["S"])
        c.wait(E_S, "T", c.cnt["T"])
        if ready_T:
            c.wait(E_V, "T", ready_T)
        c.ps_wait(E_T, "b6")
        c.ps_wait(E_T, "b7")
        c.wait(E_T, "V", ready_V)
        if ready_T:
            c.wait(E_T, "T", ready_T)
        t1 = t2 = None
        for pb in range(npb):
            t1 = c.mm(p6[0:1, 0:width], onesc_s[:], src_fn(pb),
                      pb == 0, pb == npb - 1, inc="T" if pb == npb - 1 else None)
        t2m = 0
        for pb in range(npb):
            c.wait(E_S, "V", ready_V)
            c.wait(E_S, "T", t2m)
            sv = c.emit(E_S, lambda e, pb=pb: e.activation(
                out=sq[:, 0:width], in_=src_fn(pb),
                func=AF.Square), inc="S")
            c.wait(E_T, "S", sv)
            t2 = c.mm(p7[0:1, 0:width], onesc_s[:], sq[:, 0:width],
                      pb == 0, pb == npb - 1, inc="T")
            t2m = t2
        c.wait(E_V, "T", t2)
        c.emit(E_V, lambda e: e.tensor_scalar_mul(
            out=stat_s[:, 0:width], in0=p6[0:1, 0:width], scalar1=1.0 / nd))
        v = c.emit(E_V, lambda e: e.tensor_scalar_mul(
            out=stat_s[:, width:2 * width], in0=p7[0:1, 0:width],
            scalar1=1.0 / nd), inc="V")
        c.ps_consume("b6", "V", v)
        c.ps_consume("b7", "V", v)
        c.emit(E_V, lambda e: e.scalar_tensor_tensor(
            out=stat_s[:, 2 * width:3 * width], in0=stat_s[:, 0:width],
            scalar=-1.0, in1=stat_s[:, 0:width], op0=OP.mult, op1=OP.mult))
        v = c.emit(E_V, lambda e: e.tensor_tensor(
            out=stat_s[:, 2 * width:3 * width], in0=stat_s[:, width:2 * width],
            in1=stat_s[:, 2 * width:3 * width], op=OP.add), inc="V")
        c.wait(E_S, "V", v)
        c.emit(E_S, lambda e: e.activation(
            out=stat_s[:, 2 * width:3 * width],
            in_=stat_s[:, 2 * width:3 * width], func=AF.Ln, bias=fc_eps[0:1, :]))
        s = c.emit(E_S, lambda e: e.activation(
            out=prow_s[:, 0:width],
            in_=stat_s[:, 2 * width:3 * width], func=AF.Exp, scale=fc_mhalf[0:1, :]),
            inc="S")
        c.wait(E_V, "S", s)
        v = c.emit(E_V, lambda e: e.tensor_tensor(
            out=qrow_s[:, 0:width], in0=stat_s[:, 0:width],
            in1=prow_s[:, 0:width].bitcast(F32), op=OP.mult), inc="V")
        c.wait(E_T, "V", v)
        c.ps_wait(E_T, "b0")
        c.mm(psD_a[:, 0:width], ones1_s[:], prow_s[:, 0:width], True, True)
        c.ps_wait(E_T, "b1")
        tQ = c.mm(psD_b[:, 0:width], ones1_s[:], qrow_s[:, 0:width], True, True,
                  inc="T")
        c.wait(E_V, "T", tQ)
        out_v = 0
        for pb in range(npb):
            c.emit(E_V, lambda e, pb=pb: e.tensor_tensor(
                out=sqF[:, 0:width], in0=src_fn(pb),
                in1=psD_a[:, 0:width], op=OP.mult))
            if gb is None:
                out_v = c.emit(E_V, lambda e, pb=pb: e.tensor_tensor(
                    out=dst_fn(pb), in0=sqF[:, 0:width],
                    in1=psD_b[:, 0:width], op=OP.subtract), inc="V")
            else:
                g_ap, b_ap = gb
                c.emit(E_V, lambda e, pb=pb: e.tensor_tensor(
                    out=sqF[:, 0:width],
                    in0=sqF[:, 0:width],
                    in1=psD_b[:, 0:width], op=OP.subtract))
                out_v = c.emit(E_V, lambda e, pb=pb, g=g_ap, b=b_ap: e.tensor_scalar(
                    out=dst_fn(pb), in0=sqF[:, 0:width],
                    scalar1=g[:, pb:pb + 1], scalar2=b[:, pb:pb + 1],
                    op0=OP.mult, op1=OP.add), inc="V")
        c.ps_consume("b0", "V", out_v)
        c.ps_consume("b1", "V", out_v)
        return out_v

    consts_ld = None
    for (dst, src) in (
        (identb_s, identb_d), (identf_s, identf_d), (onesc_s, onesc_d),
        (ones1_s, ones1_d), (mul_s, mul_d), (mll_s, mll_d), (mfull_s, mfull_d),
        (hpat_s, hpat_d), (convw_s, convw_d), (convb_s, convb_d), (Dvec_s, Dvec_d),
        (bdt_s, bdt_d), (alog_s, alog_d), (fconst_s, fconst_d),
    ):
        consts_ld = c.dma(E_SY, dst[:], src[:])
    for l in range(2):
        for (dst, src) in ((lng_s[l], lng_d[l]), (lnb_s[l], lnb_d[l]),
                           (b1a_s[l], b1a_d[l]), (b1g_s[l], b1g_d[l]),
                           (b2_s[l], b2_d[l]), (qs_s[l], qs_d[l]),
                           (ks_s[l], ks_d[l])):
            consts_ld = c.dma(E_SY, dst[:], src[:])

    c.emit(E_V, lambda e: e.memset(zero_s[:], 0.0), inc="V")
    for eng in (E_T, E_V, E_S, E_G):
        c.wait(eng, "DS", consts_ld)

    c.wait(E_T, "DS", consts_ld)
    c.mm(psD[:, 0:NS], ones1_s[:], alog_s[:], True, True, inc="T")
    c.wait(E_S, "T", c.cnt["T"])
    c.emit(E_S, lambda e: e.activation(out=acol_s[:], in_=psD[:, 0:NS],
                                       func=AF.Exp), inc="S")
    c.wait(E_V, "S", c.cnt["S"])
    v = c.emit(E_V, lambda e: e.tensor_scalar_mul(out=acol_s[:], in0=acol_s[:],
                                                  scalar1=-1.0), inc="V")
    c.ps_consume("b0", "V", v)

    if STOP == "p0a":
        return c.build()
    p0mark = c.sbuf_mark()
    xin_s = c.sbuf("xin_s", (128, 2 * D), F32)
    otile_s = c.sbuf("otile_s", (128, 2 * D), F32R)
    tp_banks = [psD_a, psD_b, psD2_a, psD2_b]
    tp_ev = []
    tp_dma = []
    tp_tmk = {}
    c.ps_wait(E_T, "b0")
    c.ps_wait(E_T, "b1")
    for tt in range(16):
        ibuf = xin_s[:, (tt % 2) * D:(tt % 2 + 1) * D]
        obuf = otile_s[:, (tt % 2) * D:(tt % 2 + 1) * D]
        if tt >= 2:
            c.wait(E_SY, "T", tp_tmk[tt - 2])
        ld = c.dma(E_SY, ibuf, x_in[tt * 128:(tt + 1) * 128, :])
        if tt >= 2:
            c.wait(E_V, "DG", tp_dma[tt - 2])
        c.wait(E_T, "DS", ld)
        for dtile in range(8):
            slot = dtile % 4
            if len(tp_ev) >= 4:
                c.wait(E_T, "V", tp_ev[-4])
            mk = c.emit(E_T, lambda e, b=tp_banks[slot], i=ibuf, d=dtile:
                        e.transpose(b[:, 0:128], i[:, d * 128:(d + 1) * 128],
                                    identf_s[:]), inc="T")
            if dtile == 7:
                tp_tmk[tt] = mk
            c.wait(E_V, "T", mk)
            ev = c.emit(E_V, lambda e, b=tp_banks[slot], o=obuf, d=dtile:
                        e.tensor_copy(out=o[:, d * 128:(d + 1) * 128],
                                      in_=b[:, 0:128]), inc="V")
            tp_ev.append(ev)
        c.wait(E_G, "V", tp_ev[-1])
        dm = c.dma(E_G, bAP(xct_h, tt * 128, [[T, 128], [128 * T, 8], [1, 128]]),
                   obuf.rearrange("p (a b) -> p a b", b=128))
        tp_dma.append(dm)
    for bn in ("b0", "b1", "b2", "b3"):
        c.ps_consume(bn, "V", tp_ev[-1])

    if STOP == "p0b":
        return c.build()
    zv = c.cnt["V"]
    c.wait(E_G, "V", zv)
    c.dma(E_G, bAP(sa_h, 0, [[T + 8, 128], [128 * (T + 8), 8], [1, 8]]),
          bAP(zero_s, zero_s.offset, [zero_s.ap[0], [0, 8], [1, 8]]))
    c.dma(E_G, bAP(sb_h, 0, [[T + 8, 128], [128 * (T + 8), 8], [1, 8]]),
          bAP(zero_s, zero_s.offset, [zero_s.ap[0], [0, 8], [1, 8]]))

    if STOP == "p0c":
        return c.build()
    c.barrier()
    c.sbuf_free_to(p0mark)
    if STOP == "p0":
        return c.build()

    mamba_mark = c.sbuf_mark()
    bcp_s = c.sbuf("bcp_s", (16, 16 * 128), F32R)
    uc_s = c.sbuf("uc_s", (128, 16 * TC), F32R)
    dt_s = c.sbuf("dt_s", (128, 16 * TC), F32)
    sz_s = c.sbuf("sz_s", (128, 16 * TC), BF16)
    bbc_s = c.sbuf("bbc_s", (128, 8 * TC), BF16)
    cbc_s = c.sbuf("cbc_s", (128, 8 * TC), BF16)
    hall_s = c.sbuf("hall_s", (128, 16 * TC), BF16)
    abar_s = c.sbuf("abar_s", (128, 2 * TC), F32)
    bb_s = c.sbuf("bb_s", (128, 2 * TC), BF16)
    carry_s = c.sbuf("carry_s", (128, 16 * NS), F32)
    upre_s = c.sbuf("upre_s", (128, 3 + TC), F32)
    uhalo_s = c.sbuf("uhalo_s", (128, 16 * 3), F32)
    brow_s = c.sbuf("brow_s", (16, TC), F32R)
    crow_s = c.sbuf("crow_s", (16, TC), F32R)
    dtlr_s = c.sbuf("dtlr_s", (64, TC), F32R)
    g_s = c.sbuf("g_s", (128, TC), BF16)
    y2_s = c.sbuf("y2_s", (128, 2 * TC), F32)
    xw_s = c.sbuf("xw_s", (128, 8 * TC), F32R)
    wx_s = c.sbuf("wx_s", (128, 16 * 80), F32R)
    wxc_s = c.sbuf("wxc_s", (128, 16 * 16), F32R)
    wdt_s = c.sbuf("wdt_s", (64, E), F32R)
    wbufA = c.sbuf("wbufA", (128, 2048), F32R)
    wbufB = c.sbuf("wbufB", (128, 2048), F32R)
    sq_s = c.sbuf("sq_m", (128, TC), F32)

    c.emit(E_V, lambda e: e.memset(carry_s[:], 0.0))
    c.emit(E_V, lambda e: e.memset(uhalo_s[:], 0.0), inc="V")
    zinit = c.cnt["V"]

    c.dma(E_SY, bcp_s[:], bcp_d[:])
    wx_ld = None
    for pb in range(16):
        c.dma(E_SY, wx_s[:, pb * 80:(pb + 1) * 80],
              w_xa[pb * 128:(pb + 1) * 128, :])
        wx_ld = c.dma(E_SY, wxc_s[:, pb * 16:(pb + 1) * 16],
                      w_xc[pb * 128:(pb + 1) * 128, :])
    wdt_ld = c.dma(E_SY, wdt_s[:], w_dt[:])

    winbuf_T = [0, 0]
    uc_wr_S = [0] * 16
    abar_free = [0, 0]
    uc_rd_V = [0] * 16
    sz_rd_V = [0] * 16
    dt_rd_S = 0
    dt_rd_V = 0
    upre_rd_V = 0
    xw_free_V = 0
    xw_dma_G = 0
    hall_rd_T = 0
    bc_rd_V = 0
    sa_dma = []

    for tc in range(NTCH):
        t0 = tc * TC
        c.wait(E_SY, "DG", tp_dma[-1])
        if tc > 0:
            c.wait(E_SY, "V", xw_free_V)
            c.wait(E_SY, "DG", xw_dma_G)
        xld = c.dma(E_SY, xw_s[:].rearrange("p (a b) -> p a b", b=TC),
                    bAP(xct_h, t0, [[T, 128], [128 * T, 8], [1, TC]]))
        for it in range(32):
            wb = (wbufA, wbufB)[it % 2]
            q = (E_SY, E_S)[it % 2]
            qs_ = ("DS", "DA")[it % 2]
            c.wait(q, "T", winbuf_T[it % 2])
            wld = c.dma(q, wb[:, 0:1024].rearrange("p (a b) -> p a b", b=128),
                        bAP(w_in, it * 128, [[2 * E, 128], [128 * 2 * E, 8], [1, 128]]))
            c.wait(E_T, qs_, wld)
            c.wait(E_T, "DS", xld)
            bank = (p4, p5)[it % 2]
            c.ps_wait(E_T, f"b{4 + it % 2}")
            for kp in range(8):
                tm = c.mm(bank[:], wb[:, kp * 128:(kp + 1) * 128],
                          xw_s[:, kp * TC:(kp + 1) * TC],
                          kp == 0, kp == 7, inc="T" if kp == 7 else None)
            winbuf_T[it % 2] = tm
            if it < 16:
                dblk = it
                c.wait(E_S, "T", tm)
                c.wait(E_S, "V", upre_rd_V)
                sv = c.emit(E_S, lambda e, b=bank: e.copy(
                    out=upre_s[:, 3:3 + TC], in_=b[:]), inc="S")
                c.ps_consume(f"b{4 + it % 2}", "S", sv)
                c.wait(E_V, "V", zinit)
                c.wait(E_V, "S", sv)
                c.emit(E_V, lambda e, d=dblk: e.tensor_copy(
                    out=upre_s[:, 0:3], in_=uhalo_s[:, d * 3:d * 3 + 3]))
                c.emit(E_V, lambda e, d=dblk: e.tensor_copy(
                    out=uhalo_s[:, d * 3:d * 3 + 3], in_=upre_s[:, TC:TC + 3]))
                c.emit(E_V, lambda e, d=dblk: e.tensor_scalar(
                    out=sq_s[:], in0=upre_s[:, 0:TC],
                    scalar1=convw_s[:, d * 4:d * 4 + 1], scalar2=None,
                    op0=OP.mult))
                for k in range(1, 4):
                    v1 = c.emit(E_V, lambda e, k=k, d=dblk: e.scalar_tensor_tensor(
                        out=sq_s[:], in0=upre_s[:, k:k + TC],
                        scalar=convw_s[:, d * 4 + k:d * 4 + k + 1], in1=sq_s[:],
                        op0=OP.mult, op1=OP.add),
                        inc="V" if k == 3 else None)
                upre_rd_V = v1
                c.wait(E_S, "V", v1)
                c.wait(E_S, "V", uc_rd_V[dblk])
                sv = c.emit(E_S, lambda e, d=dblk: e.activation(
                    out=uc_s[:, d * TC:(d + 1) * TC], in_=sq_s[:],
                    func=AF.Silu, bias=convb_s[:, d:d + 1]), inc="S")
                uc_wr_S[dblk] = sv
            else:
                dblk = it - 16
                c.wait(E_S, "T", tm)
                c.wait(E_S, "V", sz_rd_V[dblk])
                sv = c.emit(E_S, lambda e, b=bank, d=dblk: e.activation(
                    out=sz_s[:, d * TC:(d + 1) * TC], in_=b[:], func=AF.Silu),
                    inc="S")
                c.ps_consume(f"b{4 + it % 2}", "S", sv)
        sz_wr_S = c.cnt["S"]

        if STOP == "m_pre":
            break
        c.wait(E_T, "DS", wx_ld)
        c.ps_wait(E_T, "b2")
        c.ps_wait(E_T, "b3")
        for pb in range(16):
            c.wait(E_T, "S", uc_wr_S[pb])
            c.mm(psD2_a[0:80, :], wx_s[:, pb * 80:(pb + 1) * 80],
                 uc_s[:, pb * TC:(pb + 1) * TC], pb == 0, pb == 15)
        for pb in range(16):
            tm = c.mm(psD2_b[0:16, :], wxc_s[:, pb * 16:(pb + 1) * 16],
                      uc_s[:, pb * TC:(pb + 1) * TC], pb == 0, pb == 15,
                      inc="T" if pb == 15 else None)
        c.wait(E_V, "T", tm)
        c.emit(E_V, lambda e: e.tensor_copy(out=dtlr_s[:], in_=psD2_a[0:64, :]))
        c.emit(E_V, lambda e: e.tensor_copy(out=brow_s[:], in_=psD2_a[64:80, :]))
        v2 = c.emit(E_V, lambda e: e.tensor_copy(out=crow_s[:],
                                                 in_=psD2_b[0:16, :]), inc="V")
        c.ps_consume("b2", "V", v2)
        c.ps_consume("b3", "V", v2)
        xdbl_V = v2

        c.wait(E_T, "DS", wdt_ld)
        c.wait(E_T, "V", xdbl_V)
        dt_wr_V = [0] * 16
        dt_sq_V = 0
        for dblk in range(16):
            bank = (p6, p7)[dblk % 2]
            c.ps_wait(E_T, f"b{6 + dblk % 2}")
            tm = c.mm(bank[:], wdt_s[:, dblk * 128:(dblk + 1) * 128],
                      dtlr_s[:], True, True, inc="T")
            c.wait(E_S, "T", tm)
            c.wait(E_S, "V", dt_rd_V)
            c.wait(E_S, "V", dt_sq_V)
            c.emit(E_S, lambda e, b=bank, d=dblk: e.activation(
                out=sq_s[:], in_=b[:], func=AF.Abs, bias=bdt_s[:, d:d + 1]))
            c.emit(E_S, lambda e: e.activation(
                out=sq_s[:], in_=sq_s[:], func=AF.Exp, scale=fc_mone))
            s3 = c.emit(E_S, lambda e: e.activation(
                out=sq_s[:], in_=sq_s[:], func=AF.Ln, bias=fc_one), inc="S")
            c.wait(E_V, "S", s3)
            v1 = c.emit(E_V, lambda e, b=bank, d=dblk: e.tensor_scalar(
                out=y2_s[:, 0:TC], in0=b[:], scalar1=bdt_s[:, d:d + 1],
                scalar2=0.0, op0=OP.add, op1=OP.max), inc="V")
            dt_wr_V[dblk] = c.emit(E_V, lambda e, d=dblk: e.tensor_tensor(
                out=dt_s[:, d * TC:(d + 1) * TC], in0=y2_s[:, 0:TC],
                in1=sq_s[:], op=OP.add), inc="V")
            dt_sq_V = dt_wr_V[dblk]
            c.ps_consume(f"b{6 + dblk % 2}", "V", v1)

        if DBG2:
            c.wait(E_G, "S", c.cnt["S"])
            c.wait(E_G, "V", c.cnt["V"])
            c.dma(E_G, dbg_uc[:, tc * 16 * TC:(tc + 1) * 16 * TC], uc_s[:], sem="DO")
            c.dma(E_G, dbg_dt[:, tc * 16 * TC:(tc + 1) * 16 * TC], dt_s[:], sem="DO")
            c.wait(E_G, "DO", c.cnt["DO"])
        if STOP == "m_dt_tap" and tc == NTCH - 1:
            break
        if STOP == "m_dt":
            break
        y3_wr_V = [0] * 16
        for half in range(2):
            for j, n in enumerate(range(8 * half, 8 * half + 8)):
                for (rows_s, dstbuf) in ((brow_s, bbc_s), (crow_s, cbc_s)):
                    c.ps_wait(E_T, "b3")
                    c.wait(E_T, "V", max(xdbl_V, bc_rd_V))
                    tm = c.mm(psD2_b[:], bcp_s[:, n * 128:(n + 1) * 128],
                              rows_s[:], True, True, inc="T")
                    c.wait(E_V, "T", tm)
                    ev = c.emit(E_V, lambda e, d=dstbuf, j=j: e.tensor_copy(
                        out=d[:, j * TC:(j + 1) * TC], in_=psD2_b[:]), inc="V")
                    c.ps_consume("b3", "V", ev)

            for dblk in range(16):
                c.wait(E_V, "S", uc_wr_S[dblk])
                gv = c.emit(E_V, lambda e, d=dblk: e.tensor_tensor(
                    out=g_s[:], in0=dt_s[:, d * TC:(d + 1) * TC],
                    in1=uc_s[:, d * TC:(d + 1) * TC], op=OP.mult), inc="V")
                dt_rd_V = gv
                ybank = (p4, p5)[dblk % 2]
                if half == 0:
                    c.ps_wait(E_T, f"b{4 + dblk % 2}")
                scan_mk = [0] * 8
                for j, n in enumerate(range(8 * half, 8 * half + 8)):
                    pp = j % 2
                    c.wait(E_S, "V", abar_free[pp])
                    se = c.emit(E_S, lambda e, d=dblk, n=n, pp=pp: e.activation(
                        out=abar_s[:, pp * TC:(pp + 1) * TC],
                        in_=dt_s[:, d * TC:(d + 1) * TC], func=AF.Exp,
                        scale=acol_s[:, n:n + 1]), inc="S")
                    bv = c.emit(E_V, lambda e, j=j, pp=pp: e.tensor_tensor(
                        out=bb_s[:, pp * TC:(pp + 1) * TC], in0=g_s[:],
                        in1=bbc_s[:, j * TC:(j + 1) * TC], op=OP.mult))
                    c.wait(E_V, "S", se)
                    c.wait(E_V, "T", hall_rd_T)
                    init = carry_s[:, dblk * NS + n:dblk * NS + n + 1] \
                        if tc > 0 else 0.0
                    sv = c.emit(E_V, lambda e, n=n, pp=pp, init=init:
                                e.tensor_tensor_scan(
                                    out=hall_s[:, n * TC:(n + 1) * TC],
                                    data0=abar_s[:, pp * TC:(pp + 1) * TC],
                                    data1=bb_s[:, pp * TC:(pp + 1) * TC],
                                    initial=init, op0=OP.mult, op1=OP.add),
                                inc="V")
                    scan_mk[j] = sv
                    abar_free[pp] = sv
                if tc < NTCH - 1:
                    n0 = 8 * half
                    c.emit(E_V, lambda e, d=dblk, n0=n0: e.tensor_copy(
                        out=carry_s[:, d * NS + n0:d * NS + n0 + 8],
                        in_=bAP(hall_s, hall_s.offset + n0 * TC + TC - 1,
                                [hall_s.ap[0], [TC, 8]])))
                hvs = []
                for j, n in enumerate(range(8 * half, 8 * half + 8)):
                    hv = c.emit(E_V, lambda e, n=n, j=j: e.tensor_tensor(
                        out=hall_s[:, n * TC:(n + 1) * TC],
                        in0=hall_s[:, n * TC:(n + 1) * TC],
                        in1=cbc_s[:, j * TC:(j + 1) * TC], op=OP.mult), inc="V")
                    hvs.append(hv)
                fence = c.emit(E_V, lambda e: e.memset(fence_s[:, 1:2], 0.0),
                               inc="V")
                for j, n in enumerate(range(8 * half, 8 * half + 8)):
                    c.wait(E_T, "V", min(hvs[j] + 1, fence))
                    tm = c.mm(ybank[:], identb_s[:],
                              hall_s[:, n * TC:(n + 1) * TC],
                              half == 0 and j == 0, half == 1 and j == 7,
                              inc="T")
                if half == 1:
                    hall_rd_T = tm
                    c.wait(E_V, "T", tm)
                    c.wait(E_V, "S", sz_wr_S)
                    c.emit(E_V, lambda e, d=dblk, b=ybank: e.scalar_tensor_tensor(
                        out=y2_s[:, (d % 2) * TC:(d % 2 + 1) * TC],
                        in0=uc_s[:, d * TC:(d + 1) * TC],
                        scalar=Dvec_s[:, d:d + 1], in1=b[:],
                        op0=OP.mult, op1=OP.add))
                    v2 = c.emit(E_V, lambda e, d=dblk: e.tensor_tensor(
                        out=uc_s[:, d * TC:(d + 1) * TC],
                        in0=y2_s[:, (d % 2) * TC:(d % 2 + 1) * TC],
                        in1=sz_s[:, d * TC:(d + 1) * TC], op=OP.mult), inc="V")
                    c.ps_consume(f"b{4 + dblk % 2}", "V", v2)
                    y3_wr_V[dblk] = v2
                    uc_rd_V[dblk] = v2
                    sz_rd_V[dblk] = v2
            bc_rd_V = c.cnt["V"]
        dt_rd_S = c.cnt["S"]
        dt_rd_V = c.cnt["V"]

        if STOP == "m_scan":
            break
        if STOP == "m_loop":
            continue
        for i in range(8):
            wb = (wbufA, wbufB)[i % 2]
            q = (E_SY, E_S)[i % 2]
            qs_ = ("DS", "DA")[i % 2]
            c.wait(q, "T", winbuf_T[i % 2])
            wld = c.dma(q, wb[:].rearrange("p (a b) -> p a b", b=128),
                        bAP(w_out, i * 128, [[D, 128], [128 * D, 16], [1, 128]]))
            c.wait(E_T, qs_, wld)
            bank = (p6, p7)[i % 2]
            c.ps_wait(E_T, f"b{6 + i % 2}")
            c.wait(E_T, "V", max(y3_wr_V))
            for kp in range(16):
                tm = c.mm(bank[:], wb[:, kp * 128:(kp + 1) * 128],
                          uc_s[:, kp * TC:(kp + 1) * TC], kp == 0, kp == 15,
                          inc="T" if kp == 15 else None)
            winbuf_T[i % 2] = tm
            c.wait(E_V, "T", tm)
            rv = c.emit(E_V, lambda e, i=i, b=bank: e.tensor_tensor(
                out=xw_s[:, i * TC:(i + 1) * TC],
                in0=xw_s[:, i * TC:(i + 1) * TC], in1=b[:], op=OP.add), inc="V")
            c.ps_consume(f"b{6 + i % 2}", "V", rv)
        x1pre_V = c.cnt["V"]

        if STOP == "m_wout":
            break
        lnm = layer_norm(lambda pb: xw_s[:, pb * TC:(pb + 1) * TC],
                         lambda pb: xw_s[:, pb * TC:(pb + 1) * TC],
                         8, TC, sq_s, x1pre_V)
        c.wait(E_G, "V", lnm)
        sd = c.dma(E_G, bAP(sa_h, 8 + t0, [[T + 8, 128], [128 * (T + 8), 8], [1, TC]]),
                   xw_s[:].rearrange("p (a b) -> p a b", b=TC))
        sa_dma.append(sd)
        xw_free_V = lnm
        xw_dma_G = sd

    if STOP in ("m_pre", "m_dt", "m_dt_tap", "m_scan", "m_loop", "m_wout", "mamba"):
        return c.build()
    c.barrier()
    if STOP == "mamba2":
        return c.build()
    c.sbuf_free_to(mamba_mark)

    xw2_s = c.sbuf("xw2_s", (128, 8 * (WIN + 1)), F32R)
    qraw_s = c.sbuf("qraw_s", (128, 4 * WIN), F32)
    qh_s = c.sbuf("qh_s", (128, 4 * WIN), BF16)
    kpair_s = c.sbuf("kpair_s", (128, 4 * 2 * WIN), BF16)
    vcur_s = c.sbuf("vcur_s", (128, 4 * WIN), BF16)
    vT_s = c.sbuf("vT_s", (128, 8 * WIN), BF16)
    attn_s = c.sbuf("attn_s", (128, 4 * 640), BF16)
    attnT_s = c.sbuf("attnT_s", (128, 2 * 640), BF16)
    oT_s = c.sbuf("oT_s", (128, 4 * WIN), F32R)
    hn_s = c.sbuf("hn_s", (128, 8 * WIN), F32R)
    h2_s = c.sbuf("h2_s", (128, NIT1 * WIN), BF16)
    map_s = c.sbuf("map_s", (128, 2 * WIN), F32)
    t1_s = c.sbuf("t1_s", (128, 2 * WIN), F32)
    rn_s = c.sbuf("rn_s", (8, WIN), F32R)
    rn2_s = c.sbuf("rn2_s", (8, WIN), F32)
    nm_s = c.sbuf("nm_s", (128, 32), F32)
    rs_s = c.sbuf("rs_s", (128, 32), F32)
    rr_s = c.sbuf("rr_s", (128, 32), F32)
    halo2_s = c.sbuf("halo2_s", (128, 16), F32)
    rotf_s = c.sbuf("rotf_s", (128, WIN), F32)
    rotg_s = c.sbuf("rotg_s", (128, WIN), F32)
    sq2_s = c.sbuf("sq2_s", (128, WIN), F32)
    wb1_s = c.sbuf("wb1_s", (128, 1024), F32R)
    wb2_s = c.sbuf("wb2_s", (128, 1024), F32R)
    w2bufA = c.sbuf("w2bufA", (128, 2816), BF16)
    w2bufB = c.sbuf("w2bufB", (128, 2816), BF16)
    hm8_s = c.sbuf("hm8_s", (8, 512), F32R)
    hm1_s = c.sbuf("hm1_s", (8, 512), F32R)
    qpat_s = [c.sbuf(f"qpat_s{l}", (8, 512), F32R) for l in range(2)]
    kpat_s = [c.sbuf(f"kpat_s{l}", (8, 512), F32R) for l in range(2)]
    qpatS_s = [c.sbuf(f"qpatS_s{l}", (8, 512), F32R) for l in range(2)]
    kpatS_s = [c.sbuf(f"kpatS_s{l}", (8, 512), F32R) for l in range(2)]
    x3_s = hn_s
    ot2_s = qraw_s

    pld = c.dma(E_SY, hm8_s[:], hm8_d[:])
    pld = c.dma(E_SY, hm1_s[:], hm1_d[:])
    c.wait(E_V, "DS", pld)
    for l in range(2):
        for (src_, dstN, dstS, hm) in (
            (qs_s[l], qpat_s[l], qpatS_s[l], hm8_s),
            (ks_s[l], kpat_s[l], kpatS_s[l], hm1_s),
        ):
            for swapped, dst in ((False, dstN), (True, dstS)):
                if not swapped:
                    vv = c.emit(E_V, lambda e, s=src_: e.tensor_copy(
                        out=qsr_s[:].rearrange("p (a b) -> p a b", b=DH),
                        in_=bAP(s, s.offset, [s.ap[0], [0, 8], [1, DH]])),
                        inc="V")
                else:
                    c.emit(E_V, lambda e, s=src_: e.tensor_copy(
                        out=bAP(qsr_s, qsr_s.offset, [qsr_s.ap[0], [64, 8], [1, 32]]),
                        in_=bAP(s, s.offset + 32, [s.ap[0], [0, 8], [1, 32]])))
                    vv = c.emit(E_V, lambda e, s=src_: e.tensor_copy(
                        out=bAP(qsr_s, qsr_s.offset + 32, [qsr_s.ap[0], [64, 8], [1, 32]]),
                        in_=bAP(s, s.offset, [s.ap[0], [0, 8], [1, 32]])),
                        inc="V")
                c.wait(E_T, "V", vv)
                c.ps_wait(E_T, "b0")
                tv = c.mm(psD_a[0:8, :], ones1_s[:, 0:8], qsr_s[:], True, True,
                          inc="T")
                c.wait(E_V, "T", tv)
                vv = c.emit(E_V, lambda e, d=dst, h=hm: e.tensor_tensor(
                    out=d[:], in0=psD_a[0:8, :], in1=h[:], op=OP.mult), inc="V")
                c.ps_consume("b0", "V", vv)

    out_dma = []

    def ffn_layer(l, src_h, dst_h, final):
        nonlocal out_dma
        kv_init = c.emit(E_V, lambda e: e.memset(kpair_s[:], 0.0))
        kv_init = c.emit(E_V, lambda e: e.memset(vT_s[:], 0.0))
        kv_init = c.emit(E_V, lambda e: e.memset(halo2_s[:], 0.0), inc="V")

        wbuf_T = [0, 0]
        w2buf_T = [0, 0]
        ev4 = [0, 0, 0, 0]
        odma = []
        xw2_free = kv_init
        qraw_free_V = 0
        qraw_free_S = 0
        vT_T = 0
        rot_rd_V = 0
        nm_free_S = 0
        qh_rd_T = 0
        kv_rd_T = 0
        attn_free = [0, 0, 0, 0]
        attnT_free = [0, 0]
        oT_rd_T = 0
        hn_rd_T = 0
        h2_rd_T = 0
        x3_free = [0, 0]

        for w in range(NW):
            t0 = w * WIN
            c.wait(E_SY, "V", xw2_free)
            if final and w > 0:
                c.wait(E_V, "DO", c.cnt["DO"])
            xld = c.dma(E_SY, xw2_s[:].rearrange("p (a b) -> p a b", b=WIN + 1),
                        bAP(src_h, t0 + 7, [[T + 8, 128], [128 * (T + 8), 8], [1, WIN + 1]]))
            shift_ap = lambda pb, w_=0: (
                xw2_s[:, pb * (WIN + 1) + 1: pb * (WIN + 1) + 1 + WIN] if pb < 4
                else xw2_s[:, (pb - 0) * (WIN + 1): (pb) * (WIN + 1) + WIN])

            c.wait(E_V, "T", kv_rd_T)
            for pb in range(4):
                c.emit(E_V, lambda e, pb=pb: e.tensor_copy(
                    out=kpair_s[:, pb * 1024:pb * 1024 + WIN],
                    in_=kpair_s[:, pb * 1024 + WIN:pb * 1024 + 2 * WIN]))
            kvs = c.emit(E_V, lambda e: e.tensor_copy(
                out=vT_s[:, 0:4 * WIN], in_=vT_s[:, 4 * WIN:8 * WIN]), inc="V")

            for part, patN, patS, rf_d, rg_d in (
                (0, qpat_s[l], qpatS_s[l], rotf_d[0], rotg_d[0]),
                (1, kpat_s[l], kpatS_s[l], rotf_d[1], rotg_d[1]),
            ):
                c.wait(E_SY, "V", rot_rd_V)
                rld1 = c.dma(E_SY, rotf_s[:], rf_d[:, t0:t0 + WIN])
                rld2 = c.dma(E_SY, rotg_s[:], rg_d[:, t0:t0 + WIN])
                c.ps_wait(E_T, "b6")
                tnq = 0
                for iloc in range(4):
                    it = part * 4 + iloc
                    wb = (wb1_s, wb2_s)[iloc % 2]
                    q = (E_SY, E_S)[iloc % 2]
                    qs_ = ("DS", "DA")[iloc % 2]
                    c.wait(q, "T", wbuf_T[iloc % 2])
                    wld = c.dma(q,
                                bAP(wb, 0, [[1024, 128], [128, 8], [1, 128]]),
                                bAP(wqkv_d[l], it * 128,
                                    [[3 * INNER, 128], [128 * 3 * INNER, 8], [1, 128]]))
                    c.wait(E_T, qs_, wld)
                    c.wait(E_T, "DS", xld)
                    bank = (p4, p5)[iloc % 2]
                    c.ps_wait(E_T, f"b{4 + iloc % 2}")
                    for kp in range(8):
                        tm = c.mm(bank[:], wb[:, kp * 128:(kp + 1) * 128],
                                  shift_ap(kp), kp == 0, kp == 7,
                                  inc="T" if kp == 7 else None)
                    wbuf_T[iloc % 2] = tm
                    c.wait(E_V, "V", qraw_free_V)
                    c.wait(E_V, "S", qraw_free_S)
                    c.wait(E_V, "T", tm)
                    ev = c.emit(E_V, lambda e, b=bank, iloc=iloc: e.tensor_copy(
                        out=qraw_s[:, iloc * WIN:(iloc + 1) * WIN], in_=b[:]),
                        inc="V")
                    c.ps_consume(f"b{4 + iloc % 2}", "V", ev)
                    c.wait(E_S, "V", ev)
                    c.wait(E_S, "T", tnq)
                    sv = c.emit(E_S, lambda e, iloc=iloc: e.activation(
                        out=lnsq_s[:, 0:WIN],
                        in_=qraw_s[:, iloc * WIN:(iloc + 1) * WIN],
                        func=AF.Square), inc="S")
                    c.wait(E_T, "S", sv)
                    tnq = c.mm(p6[0:8, :], hpat_s[:, iloc * 8:(iloc + 1) * 8],
                               lnsq_s[:, 0:WIN], iloc == 0, iloc == 3,
                               inc="T" if iloc == 3 else None)
                qkv_ev_V = c.cnt["V"]
                qraw_free_S = c.cnt["S"]
                c.wait(E_V, "T", tnq)
                c.wait(E_V, "S", c.cnt["S"])
                vv = c.emit(E_V, lambda e: e.tensor_scalar_max(
                    out=rn2_s[:], in0=p6[0:8, :], scalar1=1e-24),
                    inc="V")
                c.ps_consume("b6", "V", vv)
                c.wait(E_S, "V", vv)
                c.emit(E_S, lambda e: e.activation(
                    out=rn2_s[:], in_=rn2_s[:], func=AF.Ln))
                sv = c.emit(E_S, lambda e: e.activation(
                    out=rn_s[:], in_=rn2_s[:], func=AF.Exp,
                    scale=fc_mhalf[0:8, :]), inc="S")
                c.wait(E_T, "S", sv)
                c.wait(E_T, "DS", max(rld1, rld2))
                for iloc in range(4):
                    it = part * 4 + iloc
                    wb = (wb1_s, wb2_s)[iloc % 2]
                    q = (E_SY, E_S)[iloc % 2]
                    qs_ = ("DS", "DA")[iloc % 2]
                    c.wait(q, "T", wbuf_T[iloc % 2])
                    for (oofs, iofs) in ((0, 32), (32, 0), (64, 96), (96, 64)):
                        wldS = c.dma(q,
                                     bAP(wb, oofs, [[1024, 128], [128, 8], [1, 32]]),
                                     bAP(wqkv_d[l], it * 128 + iofs,
                                         [[3 * INNER, 128],
                                          [128 * 3 * INNER, 8], [1, 32]]))
                    c.wait(E_T, qs_, wldS)
                    c.ps_wait(E_T, "b6")
                    for kp in range(8):
                        tsw = c.mm(p6[:], wb[:, kp * 128:(kp + 1) * 128],
                                   shift_ap(kp), kp == 0, kp == 7,
                                   inc="T" if kp == 7 else None)
                    wbuf_T[iloc % 2] = tsw
                    c.ps_wait(E_T, "b7")
                    tS1 = c.mm(p7[:], patN[:, iloc * 128:(iloc + 1) * 128],
                               rn_s[:], True, True, inc="T")
                    c.wait(E_V, "T", tS1)
                    mv1 = c.emit(E_V, lambda e: e.tensor_tensor(
                        out=map_s[:, 0:WIN], in0=p7[:], in1=rotf_s[:],
                        op=OP.mult), inc="V")
                    c.ps_consume("b7", "V", mv1)
                    c.ps_wait(E_T, "b7")
                    tS2 = c.mm(p7[:], patS[:, iloc * 128:(iloc + 1) * 128],
                               rn_s[:], True, True, inc="T")
                    c.wait(E_V, "T", tS2)
                    mv2 = c.emit(E_V, lambda e: e.tensor_tensor(
                        out=map_s[:, WIN:2 * WIN], in0=p7[:], in1=rotg_s[:],
                        op=OP.mult), inc="V")
                    c.ps_consume("b7", "V", mv2)
                    c.emit(E_V, lambda e, iloc=iloc: e.tensor_tensor(
                        out=t1_s[:, 0:WIN],
                        in0=qraw_s[:, iloc * WIN:(iloc + 1) * WIN],
                        in1=map_s[:, 0:WIN], op=OP.mult))
                    c.wait(E_V, "T", tsw)
                    c.emit(E_V, lambda e: e.tensor_tensor(
                        out=t1_s[:, WIN:2 * WIN], in0=p6[:],
                        in1=map_s[:, WIN:2 * WIN], op=OP.mult))
                    if part == 0:
                        dst = qh_s[:, iloc * WIN:(iloc + 1) * WIN]
                    else:
                        dst = kpair_s[:, iloc * 1024 + WIN:iloc * 1024 + 2 * WIN]
                    c.wait(E_V, "T", qh_rd_T if part == 0 else kv_rd_T)
                    qv = c.emit(E_V, lambda e, d=dst: e.tensor_tensor(
                        out=d, in0=t1_s[:, 0:WIN], in1=t1_s[:, WIN:2 * WIN],
                        op=OP.add), inc="V")
                    c.ps_consume("b6", "V", qv)
                qraw_free_V = c.cnt["V"]
                rot_rd_V = c.cnt["V"]
            qk_done_V = c.cnt["V"]

            if STOP == "f_qkv":
                break
            if DBG3 and l == DBGL:
                c.wait(E_G, "V", qk_done_V)
                c.dma(E_G, dbg_qh[:, w * 4 * WIN:(w + 1) * 4 * WIN], qh_s[:], sem="DO")
                c.dma(E_G, bAP(dbg_kh, dbg_kh.offset + w * 4 * WIN,
                               [dbg_kh.ap[0], [WIN, 4], [1, WIN]]),
                      bAP(kpair_s, kpair_s.offset + WIN,
                          [kpair_s.ap[0], [1024, 4], [1, WIN]]), sem="DO")
                c.wait(E_G, "DO", c.cnt["DO"])
                for _e in (E_V, E_S, E_T, E_SY):
                    c.wait(_e, "DO", c.cnt["DO"])
            for iloc in range(4):
                it = 8 + iloc
                wb = (wb1_s, wb2_s)[iloc % 2]
                q = (E_SY, E_S)[iloc % 2]
                qs_ = ("DS", "DA")[iloc % 2]
                c.wait(q, "T", wbuf_T[iloc % 2])
                wld = c.dma(q, bAP(wb, 0, [[1024, 128], [128, 8], [1, 128]]),
                            bAP(wqkv_d[l], it * 128,
                                [[3 * INNER, 128], [128 * 3 * INNER, 8], [1, 128]]))
                c.wait(E_T, qs_, wld)
                c.wait(E_T, "DS", xld)
                bank = (p4, p5)[iloc % 2]
                c.ps_wait(E_T, f"b{4 + iloc % 2}")
                for kp in range(8):
                    tm = c.mm(bank[:], wb[:, kp * 128:(kp + 1) * 128],
                              shift_ap(kp), kp == 0, kp == 7,
                              inc="T" if kp == 7 else None)
                wbuf_T[iloc % 2] = tm
                c.wait(E_S, "T", tm)
                c.wait(E_S, "T", vT_T)
                c.wait(E_S, "V", kvs)
                sv = c.emit(E_S, lambda e, b=bank, iloc=iloc: e.copy(
                    out=vcur_s[:, iloc * WIN:(iloc + 1) * WIN], in_=b[:]), inc="S")
                c.ps_consume(f"b{4 + iloc % 2}", "S", sv)
            vcur_S = c.cnt["S"]
            c.wait(E_T, "S", vcur_S)
            for bn in ("b2", "b3", "b6", "b7"):
                c.ps_wait(E_T, bn)
            tp_ev2 = []
            for tch in range(4):
                for pb in range(4):
                    slot = (tch * 4 + pb) % 4
                    bankap = (psD2_a[:, 0:64], psD2_b[:, 0:64],
                              p6[:, 0:64], p7[:, 0:64])[slot]
                    if len(tp_ev2) >= 4:
                        c.wait(E_T, "V", tp_ev2[-4])
                    mk = c.emit(E_T, lambda e, b=bankap, tch=tch, pb=pb:
                                e.transpose(b.bitcast(BF16),
                                            vcur_s[:, pb * WIN + tch * 128:
                                                   pb * WIN + (tch + 1) * 128],
                                            identb_s[:]), inc="T")
                    c.wait(E_V, "T", mk)
                    ev = c.emit(E_V, lambda e, b=bankap, tch=tch, pb=pb:
                                e.tensor_copy(
                        out=vT_s[:, (4 + tch) * WIN + pb * 128:
                                 (4 + tch) * WIN + (pb + 1) * 128],
                        in_=b.bitcast(BF16)), inc="V")
                    tp_ev2.append(ev)
            vT_V = c.cnt["V"]
            vT_T = c.cnt["T"]
            for bn in ("b2", "b3", "b6", "b7"):
                c.ps_consume(bn, "V", vT_V)

            if STOP == "f_v":
                break
            if DBG4 and l == DBGL and w == 1:
                c.wait(E_G, "V", vT_V)
                c.dma(E_G, dbg_vt[:], vT_s[:], sem="DO")
                c.wait(E_G, "DO", c.cnt["DO"])
                for _e in (E_V, E_S, E_T, E_SY):
                    c.wait(_e, "DO", c.cnt["DO"])
            for h in range(H):
                hp = (h % 2) * 64
                hb = (h // 2)
                for qt in range(4):
                    gslot = qt
                    c.ps_wait(E_T, "b0")
                    c.ps_wait(E_T, "b1")
                    c.wait(E_T, "V", qk_done_V)
                    c.wait(E_T, "V", attn_free[gslot])
                    lq = qh_s[hp:hp + 64, hb * WIN + qt * 128:hb * WIN + (qt + 1) * 128]
                    rk = kpair_s[hp:hp + 64, hb * 1024 + qt * 128:hb * 1024 + qt * 128 + 640]
                    nfull = max(0, (512 - 128 * qt) // 128) if w == 0 else 0
                    ops0 = [(psD[:, 0:512], lq,
                             kpair_s[hp:hp + 64,
                                     hb * 1024 + qt * 128:hb * 1024 + qt * 128 + 512]),
                            (psD[:, 0:128], mul_s[:], identb_s[:])]
                    for ch in range(min(nfull, 4)):
                        ops0.append((psD[:, ch * 128:(ch + 1) * 128], mfull_s[:],
                                     identb_s[:]))
                    for oi, (po, la, ra) in enumerate(ops0):
                        c.mm(po, la, ra, oi == 0, oi == len(ops0) - 1)
                    ops1 = [(psD[:, 512:640], lq,
                             kpair_s[hp:hp + 64,
                                     hb * 1024 + qt * 128 + 512:
                                     hb * 1024 + qt * 128 + 640]),
                            (psD[:, 512:640], mll_s[:], identb_s[:])]
                    if nfull > 4:
                        ops1.append((psD[:, 512:640], mfull_s[:], identb_s[:]))
                    tm = None
                    for oi, (po, la, ra) in enumerate(ops1):
                        tm = c.mm(po, la, ra, oi == 0, oi == len(ops1) - 1,
                                  inc="T" if oi == len(ops1) - 1 else None)
                    col = h * 4 + qt
                    if ALEVEL < 1:
                        continue
                    c.wait(E_V, "T", tm)
                    c.wait(E_V, "S", nm_free_S)
                    vv = c.emit(E_V, lambda e, col=col: e.tensor_reduce(
                        out=nm_s[:, col:col + 1].rearrange("p (a o) -> p a o", o=1),
                        in_=psD[:, 0:640].rearrange("p (a o) -> p a o", o=640),
                        axis=AX.X, op=OP.max, negate=True), inc="V")
                    if ALEVEL < 2:
                        c.ps_consume("b0", "V", vv)
                        c.ps_consume("b1", "V", vv)
                        continue
                    c.wait(E_S, "V", vv)
                    sv = c.emit(E_S, lambda e, col=col, g=gslot: e.activation(
                        out=attn_s[:, g * 640:(g + 1) * 640], in_=psD[:, 0:640],
                        func=AF.Exp, bias=nm_s[:, col:col + 1],
                        accum_out=rs_s[:, col:col + 1]), inc="S")
                    c.ps_consume("b0", "S", sv)
                    c.ps_consume("b1", "S", sv)
                if ALEVEL < 3:
                    continue
                c.wait(E_V, "S", c.cnt["S"])
                rv = c.emit(E_V, lambda e, h=h: e.reciprocal(
                    out=rr_s[:, h * 4:h * 4 + 4], in_=rs_s[:, h * 4:h * 4 + 4]),
                    inc="V")
                c.emit(E_V, lambda e: e.memset(fence_s[:, 0:1], 0.0), inc="V")
                avbank = (p4, p5)[h % 2]
                c.ps_wait(E_T, f"b{4 + h % 2}")
                for qt in range(4):
                    gslot = qt
                    col = h * 4 + qt
                    nv = c.emit(E_V, lambda e, g=gslot, col=col: e.tensor_scalar_mul(
                        out=attn_s[:, g * 640:(g + 1) * 640],
                        in0=attn_s[:, g * 640:(g + 1) * 640],
                        scalar1=rr_s[:, col:col + 1]), inc="V")
                    if ALEVEL < 4:
                        continue
                    ppT = qt % 2
                    c.wait(E_T, "V", nv)
                    c.wait(E_V, "T", attnT_free[ppT])
                    tev = []
                    for bn in ("b2", "b3", "b6", "b7"):
                        c.ps_wait(E_T, bn)
                    for cc in range(5):
                        slotb = (psD2_a[:, 0:64], psD2_b[:, 0:64],
                                 p6[:, 0:64], p7[:, 0:64],
                                 psD2_a[:, 0:64])[cc]
                        if cc == 4:
                            c.wait(E_T, "V", tev[0])
                        mk = c.emit(E_T, lambda e, b=slotb, g=gslot, cc=cc:
                                    e.transpose(b.bitcast(BF16),
                                                attn_s[:, g * 640 + cc * 128:
                                                       g * 640 + (cc + 1) * 128],
                                                identb_s[:]), inc="T")
                        c.wait(E_V, "T", mk)
                        ev = c.emit(E_V, lambda e, b=slotb, ppT=ppT, cc=cc:
                                    e.tensor_copy(
                            out=attnT_s[:, ppT * 640 + cc * 128:
                                        ppT * 640 + (cc + 1) * 128],
                            in_=b.bitcast(BF16)), inc="V")
                        tev.append(ev)
                    if DBG4 and l == DBGL and w == 1 and h == 0 and qt == 0:
                        c.wait(E_G, "V", tev[-1])
                        c.dma(E_G, dbg_att[:], attnT_s[:, ppT * 640:(ppT + 1) * 640], sem="DO")
                        c.dma(E_G, dbg_at[:, 0:640], attn_s[:, 0:640], sem="DO")
                        c.dma(E_G, dbg_nm[:, 0:32], nm_s[:], sem="DO")
                        c.dma(E_G, dbg_nm[:, 32:64], rs_s[:], sem="DO")
                        c.wait(E_G, "DO", c.cnt["DO"])
                        for _e in (E_V, E_S, E_T, E_SY):
                            c.wait(_e, "DO", c.cnt["DO"])
                    attn_free[gslot] = tev[-1]
                    for bn in ("b2", "b3", "b6", "b7"):
                        c.ps_consume(bn, "V", tev[-1])
                    if ALEVEL < 5:
                        continue
                    if qt == 3:
                        tfence = c.emit(E_V, lambda e: e.memset(
                            fence_s[:, 2:3], 0.0), inc="V")
                        c.wait(E_T, "V", tfence)
                    else:
                        c.wait(E_T, "V", tev[-1] + 1)
                    c.wait(E_T, "V", vT_V)
                    for cc in range(5):
                        tm = c.mm(avbank[0:64, qt * 128:(qt + 1) * 128],
                                  vT_s[:, (qt + cc) * WIN + 64 * h:
                                       (qt + cc) * WIN + 64 * h + 64],
                                  attnT_s[:, ppT * 640 + cc * 128:
                                          ppT * 640 + (cc + 1) * 128],
                                  cc == 0, cc == 4, inc="T" if cc == 4 else None)
                    attnT_free[ppT] = tm
                if ALEVEL < 6:
                    continue
                c.wait(E_V, "T", tm)
                c.wait(E_V, "T", oT_rd_T)
                ov = c.emit(E_V, lambda e, b=avbank, hp=hp, hb=hb: e.tensor_copy(
                    out=oT_s[hp:hp + 64, hb * WIN:(hb + 1) * WIN],
                    in_=b[0:64, :]), inc="V")
                c.ps_consume(f"b{4 + h % 2}", "V", ov)
            c.emit(E_V, lambda e: e.memset(fence_s[:, 3:4], 0.0), inc="V")
            oT_V = c.cnt["V"]
            nm_free_S = c.cnt["S"]

            if STOP == "f_attn":
                break
            for i in range(8):
                wb = (wb1_s, wb2_s)[i % 2]
                q = (E_SY, E_S)[i % 2]
                qs_ = ("DS", "DA")[i % 2]
                c.wait(q, "T", wbuf_T[i % 2])
                wld = c.dma(q, bAP(wb, 0, [[1024, 128], [128, 4], [1, 128]]),
                            bAP(wo_d[l], i * 128, [[D, 128], [128 * D, 4], [1, 128]]))
                c.wait(E_T, qs_, wld)
                bank = (p6, p7)[i % 2]
                c.ps_wait(E_T, f"b{6 + i % 2}")
                c.wait(E_T, "V", oT_V)
                for kp in range(4):
                    tm = c.mm(bank[:], wb[:, kp * 128:(kp + 1) * 128],
                              oT_s[:, kp * WIN:(kp + 1) * WIN], kp == 0, kp == 3,
                              inc="T" if kp == 3 else None)
                wbuf_T[i % 2] = tm
                oT_rd_T = tm
                c.wait(E_V, "T", tm)
                rv = c.emit(E_V, lambda e, i=i, b=bank: e.tensor_tensor(
                    out=xw2_s[:, i * (WIN + 1) + 1:i * (WIN + 1) + 1 + WIN],
                    in0=xw2_s[:, i * (WIN + 1) + 1:i * (WIN + 1) + 1 + WIN],
                    in1=b[:], op=OP.add), inc="V")
                c.ps_consume(f"b{6 + i % 2}", "V", rv)
            x2_V = c.cnt["V"]
            if DBG4 and l == DBGL and DBG3A:
                c.wait(E_G, "V", oT_V)
                c.dma(E_G, dbg_oT[:, w * 4 * WIN:(w + 1) * 4 * WIN], oT_s[:], sem="DO")
                c.wait(E_G, "DO", c.cnt["DO"])
                for _e in (E_V, E_S, E_T, E_SY):
                    c.wait(_e, "DO", c.cnt["DO"])
            if DBG3 and l == DBGL and DBG3A:
                c.wait(E_G, "V", x2_V)
                c.dma(E_G, dbg_oT[:, w * 4 * WIN:(w + 1) * 4 * WIN], oT_s[:], sem="DO")
                c.dma(E_G, bAP(dbg_xa, dbg_xa.offset + w * 8 * WIN,
                               [dbg_xa.ap[0], [8 * T // 8, 0]][0:1] + [[WIN, 8], [1, WIN]]),
                      bAP(xw2_s, xw2_s.offset + 1, [xw2_s.ap[0], [WIN + 1, 8], [1, WIN]]),
                      sem="DO")
                c.wait(E_G, "DO", c.cnt["DO"])
                for _e in (E_V, E_S, E_T, E_SY):
                    c.wait(_e, "DO", c.cnt["DO"])

            hs = (w % 2) * 8
            hp_ = ((w + 1) % 2) * 8
            c.emit(E_V, lambda e, hs=hs: e.tensor_copy(
                out=halo2_s[:, hs:hs + 8],
                in_=bAP(xw2_s, xw2_s.offset + WIN, [xw2_s.ap[0], [WIN + 1, 8], [1, 1]])))
            hv = c.emit(E_V, lambda e, hp_=hp_: e.tensor_copy(
                out=bAP(xw2_s, xw2_s.offset, [xw2_s.ap[0], [WIN + 1, 8], [1, 1]]),
                in_=halo2_s[:, hp_:hp_ + 8]), inc="V")

            if STOP == "f_wo":
                break
            lnm = layer_norm(lambda pb: shift_ap(pb),
                             lambda pb: hn_s[:, pb * WIN:(pb + 1) * WIN],
                             8, WIN, sq2_s, hv, ready_T=hn_rd_T,
                             gb=(lng_s[l], lnb_s[l]))

            if STOP == "f_ln":
                break
            if DBG3 and l == DBGL:
                c.wait(E_G, "V", lnm)
                c.dma(E_G, bAP(dbg_hn, dbg_hn.offset + w * 8 * WIN,
                               [dbg_hn.ap[0], [WIN, 8], [1, WIN]]),
                      hn_s[:].rearrange("p (a b) -> p a b", b=WIN), sem="DO")
                c.wait(E_G, "DO", c.cnt["DO"])
                for _e in (E_V, E_S, E_T, E_SY):
                    c.wait(_e, "DO", c.cnt["DO"])
            gelu_free_V = 0
            for it in range(NIT1):
                na = min(128, HID - it * 128)
                wbA, wbB = (wb1_s, wb2_s)
                c.wait(E_SY, "T", wbuf_T[0])
                wldA = c.dma(E_SY, bAP(wbA, 0, [[1024, 128], [128, 8], [1, na]]),
                             bAP(w1_d[l], it * 128,
                                 [[2 * HID, 128], [128 * 2 * HID, 8], [1, na]]))
                c.wait(E_S, "T", wbuf_T[1])
                wldB = c.dma(E_S, bAP(wbB, 0, [[1024, 128], [128, 8], [1, na]]),
                             bAP(w1_d[l], HID + it * 128,
                                 [[2 * HID, 128], [128 * 2 * HID, 8], [1, na]]))
                c.wait(E_T, "DS", wldA)
                c.wait(E_T, "DA", wldB)
                c.wait(E_T, "V", lnm)
                c.ps_wait(E_T, "b4")
                c.ps_wait(E_T, "b5")
                for kp in range(8):
                    tma = c.mm(p4[0:na, :], wbA[:, kp * 128:kp * 128 + na],
                               hn_s[:, kp * WIN:(kp + 1) * WIN], kp == 0, kp == 7,
                               inc="T" if kp == 7 else None)
                for kp in range(8):
                    tmb = c.mm(p5[0:na, :], wbB[:, kp * 128:kp * 128 + na],
                               hn_s[:, kp * WIN:(kp + 1) * WIN], kp == 0, kp == 7,
                               inc="T" if kp == 7 else None)
                wbuf_T[0] = tma
                wbuf_T[1] = tmb
                hn_rd_T = tmb
                c.wait(E_S, "T", tmb)
                c.wait(E_S, "V", gelu_free_V)
                sv = c.emit(E_S, lambda e, it=it, na=na: e.activation(
                    out=sq2_s[0:na, :], in_=p5[0:na, :], func=AF.Gelu,
                    bias=b1g_s[l][0:na, it:it + 1]), inc="S")
                c.ps_consume("b5", "S", sv)
                c.wait(E_V, "S", sv)
                c.wait(E_V, "T", h2_rd_T)
                gv = c.emit(E_V, lambda e, it=it, na=na: e.scalar_tensor_tensor(
                    out=h2_s[0:na, it * WIN:(it + 1) * WIN], in0=p4[0:na, :],
                    scalar=b1a_s[l][0:na, it:it + 1], in1=sq2_s[0:na, :],
                    op0=OP.add, op1=OP.mult), inc="V")
                gelu_free_V = gv
                c.ps_consume("b4", "V", gv)
            h2_V = c.cnt["V"]

            if STOP == "f_w1":
                break
            for i in range(8):
                wb = (w2bufA, w2bufB)[i % 2]
                c.wait(E_G, "T", w2buf_T[i % 2])
                wld = c.dma(E_G, bAP(wb, 0, [[2816, 128], [128, 21], [1, 128]]),
                            bAP(w2_d[l], i * 128, [[D, 128], [128 * D, 21], [1, 128]]))
                wld = c.dma(E_G, bAP(wb, 21 * 128, [[2816, 42], [1, 128]]),
                            bAP(w2_d[l], 21 * 128 * D + i * 128, [[D, 42], [1, 128]]))
                if W2LEVEL < 2:
                    continue
                c.wait(E_T, "DG", wld)
                bank = (p6, p7)[i % 2]
                c.ps_wait(E_T, f"b{6 + i % 2}")
                c.wait(E_T, "V", h2_V)
                for kp in range(NIT1):
                    na = min(128, HID - kp * 128)
                    tm = c.mm(bank[:], wb[0:na, kp * 128:(kp + 1) * 128],
                              h2_s[0:na, kp * WIN:(kp + 1) * WIN],
                              kp == 0, kp == NIT1 - 1,
                              inc="T" if kp == NIT1 - 1 else None)
                w2buf_T[i % 2] = tm
                h2_rd_T = tm
                if W2LEVEL < 3:
                    c.wait(E_S, "T", tm)
                    dmy = c.emit(E_S, lambda e, b=bank: e.copy(
                        out=sq2_s[:], in_=b[:]), inc="S")
                    c.ps_consume(f"b{6 + i % 2}", "S", dmy)
                    continue
                c.wait(E_V, "T", tm)
                rv = c.emit(E_V, lambda e, i=i, b=bank: e.scalar_tensor_tensor(
                    out=xw2_s[:, i * (WIN + 1) + 1:i * (WIN + 1) + 1 + WIN],
                    in0=b[:], scalar=b2_s[l][:, i:i + 1],
                    in1=xw2_s[:, i * (WIN + 1) + 1:i * (WIN + 1) + 1 + WIN],
                    op0=OP.add, op1=OP.add), inc="V")
                c.ps_consume(f"b{6 + i % 2}", "V", rv)
            xout_V = c.cnt["V"]

            kv_rd_T = c.cnt["T"]
            qh_rd_T = c.cnt["T"]

            if not final and W2LEVEL >= 4:
                c.wait(E_G, "V", xout_V)
                sd = c.dma(E_G, bAP(dst_h, 8 + t0,
                                    [[T + 8, 128], [128 * (T + 8), 8], [1, WIN]]),
                           bAP(xw2_s, xw2_s.offset + 1,
                               [xw2_s.ap[0], [WIN + 1, 8], [1, WIN]]))
                c.wait(E_SY, "DG", sd)
                xw2_free = xout_V
            elif not final:
                xw2_free = xout_V if W2LEVEL >= 3 else c.cnt["V"]
            else:
                c.wait(E_SY, "V", c.cnt["V"])
                c.wait(E_SY, "T", c.cnt["T"])
                x1ld = c.dma(E_SY, x3_s[:].rearrange("p (a b) -> p a b", b=WIN),
                             bAP(sa_h, 8 + t0,
                                 [[T + 8, 128], [128 * (T + 8), 8], [1, WIN]]))
                c.wait(E_V, "DS", x1ld)
                av = 0
                for pb in range(8):
                    av = c.emit(E_V, lambda e, pb=pb: e.tensor_tensor(
                        out=x3_s[:, pb * WIN:(pb + 1) * WIN],
                        in0=x3_s[:, pb * WIN:(pb + 1) * WIN],
                        in1=xw2_s[:, pb * (WIN + 1) + 1:pb * (WIN + 1) + 1 + WIN],
                        op=OP.add), inc="V")
                if DBG3 and DBG3B:
                    c.wait(E_G, "V", av)
                    c.dma(E_G, bAP(dbg_hn, dbg_hn.offset + w * 8 * WIN,
                                   [dbg_hn.ap[0], [WIN, 8], [1, WIN]]),
                          x3_s[:].rearrange("p (a b) -> p a b", b=WIN), sem="DO")
                    c.wait(E_G, "DO", c.cnt["DO"])
                    for _e in (E_V, E_S, E_T, E_SY):
                        c.wait(_e, "DO", c.cnt["DO"])
                lnf = layer_norm(lambda pb: x3_s[:, pb * WIN:(pb + 1) * WIN],
                                 lambda pb: x3_s[:, pb * WIN:(pb + 1) * WIN],
                                 8, WIN, sq2_s, av)
                xw2_free = av
                c.ps_wait(E_T, "psD2a")
                c.ps_wait(E_T, "b3")
                c.wait(E_T, "V", lnf)
                for tch in range(4):
                    ob = ot2_s[:, (tch % 2) * D:(tch % 2 + 1) * D]
                    if len(odma) >= 2:
                        c.wait(E_V, "DO", odma[-2])
                    evl = None
                    for pb in range(8):
                        slot = pb % 4
                        bankap = (psD_a[:, 0:128], psD_b[:, 0:128],
                                  psD2_a[:, 0:128], psD2_b[:, 0:128])[slot]
                        if ev4[slot]:
                            c.wait(E_T, "V", ev4[slot])
                        mk = c.emit(E_T, lambda e, b=bankap, pb=pb, tch=tch:
                                    e.transpose(b, x3_s[:, pb * WIN + tch * 128:
                                                        pb * WIN + (tch + 1) * 128]
                                                .bitcast(F32), identf_s[:]),
                                    inc="T")
                        c.wait(E_V, "T", mk)
                        evl = c.emit(E_V, lambda e, b=bankap, ob=ob, pb=pb:
                                     e.tensor_copy(out=ob[:, pb * 128:(pb + 1) * 128],
                                                   in_=b), inc="V")
                        ev4[slot] = evl
                    c.wait(E_G, "V", evl)
                    dm = c.dma(E_G, out_d[t0 + tch * 128:t0 + (tch + 1) * 128, :],
                               ob, sem="DO")
                    odma.append(dm)
                for bn in ("b0", "b1", "b2", "b3"):
                    c.ps_consume(bn, "V", ev4[3])
                hn_rd_T = c.cnt["T"]
                x3_free[w % 2] = lnf
        return

    ffn_layer(0, sa_h, sb_h, final=False)
    if STOP in ("f_qkv", "f_v", "f_attn", "f_wo", "f_ln", "f_w1"):
        return c.build()
    c.barrier()
    if STOP == "ffn0":
        return c.build()
    ffn_layer(1, sb_h, None, final=True)

    c.wait(E_G, "DO", c.cnt["DO"])
    c.wait(E_SY, "DO", c.cnt["DO"])

    nc = c.build()
    return nc


_CACHED = {}


def kernel(**inputs):
    x_full = np.asarray(inputs["L0_attn_x"], dtype=np.float32)
    params = reorg_params(inputs)
    consts = host_constants()
    if "nc" not in _CACHED:
        _CACHED["nc"] = build_program()
    nc = _CACHED["nc"]
    base = {}
    base.update(params)
    base.update(consts)
    in_maps = []
    for cid in range(NCORES):
        m = dict(base)
        m["x"] = np.ascontiguousarray(x_full[cid])
        in_maps.append(m)
    res = run_bass_kernel_spmd(nc, in_maps, core_ids=list(range(NCORES)))
    out = np.stack([np.asarray(r["out"]) for r in res.results], axis=0)
    return out.astype(np.float32)


# version 05
# speedup vs baseline: 1.0752x; 1.0752x over previous
import numpy as np
import ml_dtypes
import concourse.bass as bass
import concourse.mybir as mybir
from concourse.bass_utils import run_bass_kernel_spmd

F32 = mybir.dt.float32
F32R = mybir.dt.float32r
BF16 = mybir.dt.bfloat16
AF = mybir.ActivationFunctionType
OP = mybir.AluOpType
AX = mybir.AxisListType

D = 1024
E = 2048
NS = 16
RK = 64
T = 2048
TC = 512
NTCH = T // TC
H = 8
DH = 64
INNER = 512
HID = 2730
NIT1 = 22
WIN = 512
NW = T // WIN
NEG = -1.0e30
EPS = 1e-5
NCORES = 8
STOP = None
NOREG = False
ALEVEL = 99
W2LEVEL = 99
DBG = False
DBG2 = False
DBG3 = False
DBG4 = False
DBGL = 0
DBG3A = True
DBG3B = True

ENGINES = ("sync", "tensor", "vector", "scalar", "gpsimd")
E_T, E_V, E_S, E_G, E_SY = "tensor", "vector", "scalar", "gpsimd", "sync"


def bAP(t_ap, offset, dims):
    return bass.AP(tensor=t_ap.tensor, offset=offset,
                   ap=[list(d) for d in dims])


class Ctx:
    def __init__(self):
        self.nc = bass.Bass("TRN2", target_bir_lowering=False, debug=False)
        self.q = {e: [] for e in ENGINES}
        self.cnt = {}
        self.sems = {}
        self._sbuf_stack = []
        self.psg = {}

    def sem(self, name):
        h = self.nc.semaphore(name).__enter__()
        self.sems[name] = h
        self.cnt[name] = 0

    def emit(self, eng, fn, inc=None, by=1):
        if inc is not None:
            self.cnt[inc] += by
            sem = self.sems[inc]

            def run(e, fn=fn, sem=sem, by=by):
                fn(e).then_inc(sem, by)
            self.q[eng].append(run)
            return self.cnt[inc]
        self.q[eng].append(fn)
        return None

    def wait(self, eng, sem_name, val):
        if val is None or val <= 0:
            return
        sem = self.sems[sem_name]
        self.q[eng].append(lambda e, s=sem, v=val: e.wait_ge(s, v))

    def sbuf(self, name, shape, dt):
        cm = self.nc.sbuf_tensor(name, list(shape), dt)
        h = cm.__enter__()
        self._sbuf_stack.append((name, cm))
        return h.ap() if hasattr(h, "ap") else h

    def sbuf_mark(self):
        return len(self._sbuf_stack)

    def sbuf_free_to(self, mark):
        while len(self._sbuf_stack) > mark:
            _, cm = self._sbuf_stack.pop()
            cm.__exit__(None, None, None)

    def ps_wait(self, eng, name):
        g = self.psg.get(name)
        if g is not None:
            self.wait(eng, g[0], g[1])

    def ps_consume(self, name, sem, val):
        self.psg[name] = (sem, val)

    def dma(self, queue, out, in_, sem=None):
        sem = sem or ("DG" if queue == E_G else ("DA" if queue == E_S else "DS"))
        return self.emit(queue, lambda e, o=out, i=in_: e.dma_start(out=o, in_=i),
                         inc=sem, by=16)

    def mm(self, psum, lhsT, rhs, start, stop, inc=None):
        return self.emit(E_T, lambda e, p=psum, l=lhsT, r=rhs, st=start, sp=stop:
                         e.matmul(p, l, r, start=st, stop=sp), inc=inc)

    def barrier(self):
        marks = {s: self.cnt[s] for s in ("T", "V", "S", "P", "DS", "DG", "DA", "DO")}
        for eng in (E_T, E_V, E_S, E_G, E_SY):
            for s, v in marks.items():
                self.wait(eng, s, v)

    def build(self):
        nc = self.nc
        with nc.Block() as block:
            @block.sync
            def _(e):
                for fn in self.q["sync"]:
                    fn(e)

            @block.tensor
            def _(e):
                for fn in self.q["tensor"]:
                    fn(e)

            @block.vector
            def _(e):
                for fn in self.q["vector"]:
                    fn(e)

            @block.scalar
            def _(e):
                for fn in self.q["scalar"]:
                    fn(e)

            @block.gpsimd
            def _(e):
                for fn in self.q["gpsimd"]:
                    fn(e)
        return nc


def host_constants():
    c = {}
    ident = np.eye(128, dtype=np.float32)
    c["ident_b"] = ident.astype(ml_dtypes.bfloat16)
    c["ident_f"] = ident
    c["onesc_r"] = np.ones((128, 1), dtype=np.float32)
    c["ones1_r"] = np.ones((1, 128), dtype=np.float32)
    a = np.arange(128)
    c["m_ul"] = np.where(a[:, None] < a[None, :], NEG, 0.0).astype(ml_dtypes.bfloat16)
    c["m_ll"] = np.where(a[:, None] > a[None, :], NEG, 0.0).astype(ml_dtypes.bfloat16)
    c["m_full"] = np.full((128, 128), NEG, dtype=ml_dtypes.bfloat16)
    hpat = np.zeros((128, 32), dtype=np.float32)
    for it in range(4):
        for p_ in range(128):
            hpat[p_, 8 * it + 2 * it + (p_ // 64)] = 1.0
    c["hpat"] = hpat
    hm8 = np.zeros((8, 512), dtype=np.float32)
    hm1 = np.zeros((8, 512), dtype=np.float32)
    for it in range(4):
        for iloc in range(128):
            h = 2 * it + (iloc // 64)
            hm8[h, 128 * it + iloc] = 8.0
            hm1[h, 128 * it + iloc] = 1.0
    c["hm8"] = hm8
    c["hm1"] = hm1
    bcp = np.zeros((16, 16 * 128), dtype=np.float32)
    for n in range(16):
        bcp[n, n * 128:(n + 1) * 128] = 1.0
    c["bcp"] = bcp
    c["fconst"] = np.tile(np.array([[EPS, -0.5, 1.0, -1.0]], dtype=np.float32),
                          (128, 1))
    half = np.arange(0, DH, 2, dtype=np.float64)
    invf = 1.0 / 10000.0 ** (half / DH)
    t = np.arange(T, dtype=np.float64)
    freqs = t[:, None] * invf[None, :]
    base = (half + 0.4 * DH) / (1.4 * DH)
    power = (t - T // 2) / float(WIN // 2)
    scale = base[None, :] ** power[:, None]
    cosf = np.concatenate([np.cos(freqs)] * 2, axis=1)
    sinf = np.concatenate([np.sin(freqs)] * 2, axis=1)
    scf = np.concatenate([scale] * 2, axis=1)
    f1q = (cosf * scf).T
    f1k = (cosf / scf).T
    g2q = (sinf * scf).T.copy()
    g2k = (sinf / scf).T.copy()
    g2q[0:32, :] *= -1.0
    g2k[0:32, :] *= -1.0
    rep = lambda m_: np.concatenate([m_, m_], axis=0).astype(np.float32)
    c["rot_f1q"] = rep(f1q)
    c["rot_f1k"] = rep(f1k)
    c["rot_g2q"] = rep(g2q)
    c["rot_g2k"] = rep(g2k)
    return c


def reorg_params(inp):
    d = {}
    d["w_in"] = np.ascontiguousarray(inp["L0_attn_W_in"], dtype=np.float32)
    wx_ = np.asarray(inp["L0_attn_W_x"], dtype=np.float32)
    d["w_xa"] = np.ascontiguousarray(wx_[:, 0:80])
    d["w_xc"] = np.ascontiguousarray(wx_[:, 80:96])
    d["w_dt"] = np.ascontiguousarray(inp["L0_attn_W_dt"], dtype=np.float32)
    d["w_out"] = np.ascontiguousarray(inp["L0_attn_W_out"], dtype=np.float32)
    cw = np.asarray(inp["L0_attn_conv_w"], dtype=np.float32).reshape(E, 4)
    cwl = np.zeros((128, 64), dtype=np.float32)
    for pb in range(16):
        for k in range(4):
            cwl[:, pb * 4 + k] = cw[pb * 128:(pb + 1) * 128, k]
    d["convw_l"] = cwl
    pcol = lambda v, nb: np.ascontiguousarray(
        np.asarray(v, dtype=np.float32).reshape(nb, 128).T)
    d["convb_l"] = pcol(inp["L0_attn_conv_b"], 16)
    d["Dvec_l"] = pcol(inp["L0_attn_D"], 16)
    d["bdt_l"] = pcol(inp["L0_attn_b_dt"], 16)
    d["alog"] = np.asarray(inp["L0_attn_A_log"], dtype=np.float32).reshape(1, NS)
    for l in range(2):
        d[f"wqkv{l}"] = np.ascontiguousarray(inp["L0_ffn_wqkv"][l], dtype=np.float32)
        d[f"wo{l}"] = np.ascontiguousarray(inp["L0_ffn_wo"][l], dtype=np.float32)
        d[f"w1{l}"] = np.ascontiguousarray(inp["L0_ffn_w1"][l], dtype=np.float32)
        d[f"w2{l}"] = np.ascontiguousarray(inp["L0_ffn_w2"][l], dtype=np.float32)
        d[f"qs{l}"] = np.asarray(inp["L0_ffn_q_scale"][l], dtype=np.float32).reshape(1, DH)
        d[f"ks{l}"] = np.asarray(inp["L0_ffn_k_scale"][l], dtype=np.float32).reshape(1, DH)
        d[f"lng{l}"] = pcol(inp["L0_ffn_ln_g"][l], 8)
        d[f"lnb{l}"] = pcol(inp["L0_ffn_ln_b"][l], 8)
        b1 = np.asarray(inp["L0_ffn_b1"][l], dtype=np.float32)
        b1a = np.zeros((128, NIT1), dtype=np.float32)
        b1g = np.zeros((128, NIT1), dtype=np.float32)
        for it in range(NIT1):
            lo = it * 128
            hi = min(lo + 128, HID)
            b1a[0:hi - lo, it] = b1[lo:hi]
            b1g[0:hi - lo, it] = b1[HID + lo:HID + hi]
        d[f"b1a{l}"] = b1a
        d[f"b1g{l}"] = b1g
        d[f"b2{l}"] = pcol(inp["L0_ffn_b2"][l], 8)
    return d


def build_program():
    c = Ctx()
    nc = c.nc

    def din(name, shape, dt):
        return nc.dram_tensor(name, list(shape), dt, kind="ExternalInput").ap()

    x_in = din("x", (T, D), F32)
    w_in = din("w_in", (D, 2 * E), F32R)
    w_xa = din("w_xa", (E, 80), F32R)
    w_xc = din("w_xc", (E, 16), F32R)
    w_dt = din("w_dt", (RK, E), F32R)
    w_out = din("w_out", (E, D), F32R)
    convw_d = din("convw_l", (128, 64), F32)
    convb_d = din("convb_l", (128, 16), F32)
    Dvec_d = din("Dvec_l", (128, 16), F32)
    bdt_d = din("bdt_l", (128, 16), F32)
    alog_d = din("alog", (1, NS), F32R)
    wqkv_d = [din(f"wqkv{l}", (D, 3 * INNER), F32R) for l in range(2)]
    wo_d = [din(f"wo{l}", (INNER, D), F32R) for l in range(2)]
    w1_d = [din(f"w1{l}", (D, 2 * HID), F32R) for l in range(2)]
    w2_d = [din(f"w2{l}", (HID, D), F32R) for l in range(2)]
    qs_d = [din(f"qs{l}", (1, DH), F32R) for l in range(2)]
    ks_d = [din(f"ks{l}", (1, DH), F32R) for l in range(2)]
    lng_d = [din(f"lng{l}", (128, 8), F32) for l in range(2)]
    lnb_d = [din(f"lnb{l}", (128, 8), F32) for l in range(2)]
    b1a_d = [din(f"b1a{l}", (128, NIT1), F32) for l in range(2)]
    b1g_d = [din(f"b1g{l}", (128, NIT1), F32) for l in range(2)]
    b2_d = [din(f"b2{l}", (128, 8), F32) for l in range(2)]
    identb_d = din("ident_b", (128, 128), BF16)
    identf_d = din("ident_f", (128, 128), F32)
    onesc_d = din("onesc_r", (128, 1), F32R)
    ones1_d = din("ones1_r", (1, 128), F32R)
    mul_d = din("m_ul", (128, 128), BF16)
    mll_d = din("m_ll", (128, 128), BF16)
    mfull_d = din("m_full", (128, 128), BF16)
    hpat_d = din("hpat", (128, 32), F32R)
    hm8_d = din("hm8", (8, 512), F32R)
    hm1_d = din("hm1", (8, 512), F32R)
    bcp_d = din("bcp", (16, 16 * 128), F32R)
    fconst_d = din("fconst", (128, 4), F32)
    rotf_d = [din("rot_f1q", (128, T), F32), din("rot_f1k", (128, T), F32)]
    rotg_d = [din("rot_g2q", (128, T), F32), din("rot_g2k", (128, T), F32)]

    out_d = nc.dram_tensor("out", [T, D], F32, kind="ExternalOutput").ap()
    dbg_uc = dbg_dt = None
    dbg_xa = dbg_oT = dbg_hn = None
    dbg_qh = dbg_kh = None
    dbg_at = dbg_nm = None
    if DBG2:
        dbg_uc = nc.dram_tensor("dbg_uc", [128, NTCH * 16 * TC], F32R,
                                kind="ExternalOutput").ap()
        dbg_dt = nc.dram_tensor("dbg_dt", [128, NTCH * 16 * TC], F32,
                                kind="ExternalOutput").ap()
    if DBG3:
        dbg_xa = nc.dram_tensor("dbg_xa", [128, 8 * T], F32R, kind="ExternalOutput").ap()
        dbg_oT = nc.dram_tensor("dbg_oT", [128, 4 * T], F32R, kind="ExternalOutput").ap()
        dbg_hn = nc.dram_tensor("dbg_hn", [128, 8 * T], F32R, kind="ExternalOutput").ap()
        dbg_qh = nc.dram_tensor("dbg_qh", [128, 4 * T], BF16, kind="ExternalOutput").ap()
        dbg_kh = nc.dram_tensor("dbg_kh", [128, 4 * T], BF16, kind="ExternalOutput").ap()
    if DBG4:
        dbg_oT = nc.dram_tensor("dbg_oT", [128, 4 * T], F32R, kind="ExternalOutput").ap()
        dbg_at = nc.dram_tensor("dbg_at", [128, 4 * 640], BF16, kind="ExternalOutput").ap()
        dbg_nm = nc.dram_tensor("dbg_nm", [128, 64], F32, kind="ExternalOutput").ap()
        dbg_vt = nc.dram_tensor("dbg_vt", [128, 8 * WIN], BF16, kind="ExternalOutput").ap()
        dbg_att = nc.dram_tensor("dbg_att", [128, 5 * 128], BF16, kind="ExternalOutput").ap()
    kind_dbg = "ExternalOutput" if DBG else "Internal"
    xct_h = nc.dram_tensor("xct_h", [D, T], F32R, kind=kind_dbg).ap()
    sa_h = nc.dram_tensor("sa_h", [D, T + 8], F32R, kind=kind_dbg).ap()
    sb_h = nc.dram_tensor("sb_h", [D, T + 8], F32R, kind=kind_dbg).ap()

    identb_s = c.sbuf("identb_s", (128, 128), BF16)
    identf_s = c.sbuf("identf_s", (128, 128), F32)
    onesc_s = c.sbuf("onesc_s", (128, 1), F32R)
    ones1_s = c.sbuf("ones1_s", (1, 128), F32R)
    mul_s = c.sbuf("mul_s", (128, 128), BF16)
    mll_s = c.sbuf("mll_s", (128, 128), BF16)
    mfull_s = c.sbuf("mfull_s", (128, 128), BF16)
    hpat_s = c.sbuf("hpat_s", (128, 32), F32R)
    convw_s = c.sbuf("convw_s", (128, 64), F32)
    convb_s = c.sbuf("convb_s", (128, 16), F32)
    Dvec_s = c.sbuf("Dvec_s", (128, 16), F32)
    bdt_s = c.sbuf("bdt_s", (128, 16), F32)
    alog_s = c.sbuf("alog_s", (1, NS), F32R)
    acol_s = c.sbuf("acol_s", (128, NS), F32)
    lng_s = [c.sbuf(f"lng_s{l}", (128, 8), F32) for l in range(2)]
    lnb_s = [c.sbuf(f"lnb_s{l}", (128, 8), F32) for l in range(2)]
    b1a_s = [c.sbuf(f"b1a_s{l}", (128, NIT1), F32) for l in range(2)]
    b1g_s = [c.sbuf(f"b1g_s{l}", (128, NIT1), F32) for l in range(2)]
    b2_s = [c.sbuf(f"b2_s{l}", (128, 8), F32) for l in range(2)]
    qs_s = [c.sbuf(f"qs_s{l}", (1, DH), F32R) for l in range(2)]
    ks_s = [c.sbuf(f"ks_s{l}", (1, DH), F32R) for l in range(2)]
    zero_s = c.sbuf("zero_s", (128, 16), F32)
    fconst_s = c.sbuf("fconst_s", (128, 4), F32)
    fc_eps = fconst_s[:, 0:1]
    fc_mhalf = fconst_s[:, 1:2]
    fc_one = fconst_s[:, 2:3]
    fc_mone = fconst_s[:, 3:4]
    qsr_s = c.sbuf("qsr_s", (1, 512), F32R)
    stat_s = c.sbuf("stat_s", (1, 3 * TC), F32)
    prow_s = c.sbuf("prow_s", (1, TC), F32R)
    qrow_s = c.sbuf("qrow_s", (1, TC), F32R)
    lnsq_s = c.sbuf("lnsq_s", (128, TC), F32R)
    fence_s = c.sbuf("fence_s", (128, 8), F32)

    psD = nc.psum_tensor("psD", [128, 1024], F32).__enter__().ap()
    psD2 = nc.psum_tensor("psD2", [128, 1024], F32).__enter__().ap()
    p4 = nc.psum_tensor("p4", [128, 512], F32).__enter__().ap()
    p5 = nc.psum_tensor("p5", [128, 512], F32).__enter__().ap()
    p6 = nc.psum_tensor("p6", [128, 512], F32).__enter__().ap()
    p7 = nc.psum_tensor("p7", [128, 512], F32).__enter__().ap()
    psD_a, psD_b = psD[:, 0:512], psD[:, 512:1024]
    psD2_a, psD2_b = psD2[:, 0:512], psD2[:, 512:1024]

    for s in ("DS", "DG", "DA", "DO", "T", "V", "S", "P"):
        c.sem(s)

    def layer_norm(src_fn, dst_fn, npb, width, sqF, ready_V, ready_T=0, gb=None):
        sq = lnsq_s
        """src_fn(pb)->AP read, dst_fn(pb)->AP write; normalizes over npb*128
        channels.  Uses p6/p7 for sums, psD_a/psD_b for P/Q bcast.
        Returns final V mark."""
        nd = npb * 128
        c.wait(E_V, "S", c.cnt["S"])
        c.wait(E_S, "T", c.cnt["T"])
        if ready_T:
            c.wait(E_V, "T", ready_T)
        c.ps_wait(E_T, "b6")
        c.ps_wait(E_T, "b7")
        c.wait(E_T, "V", ready_V)
        if ready_T:
            c.wait(E_T, "T", ready_T)
        t1 = t2 = None
        for pb in range(npb):
            t1 = c.mm(p6[0:1, 0:width], onesc_s[:], src_fn(pb),
                      pb == 0, pb == npb - 1, inc="T" if pb == npb - 1 else None)
        t2m = 0
        for pb in range(npb):
            c.wait(E_S, "V", ready_V)
            c.wait(E_S, "T", t2m)
            sv = c.emit(E_S, lambda e, pb=pb: e.activation(
                out=sq[:, 0:width], in_=src_fn(pb),
                func=AF.Square), inc="S")
            c.wait(E_T, "S", sv)
            t2 = c.mm(p7[0:1, 0:width], onesc_s[:], sq[:, 0:width],
                      pb == 0, pb == npb - 1, inc="T")
            t2m = t2
        c.wait(E_V, "T", t2)
        c.emit(E_V, lambda e: e.tensor_scalar_mul(
            out=stat_s[:, 0:width], in0=p6[0:1, 0:width], scalar1=1.0 / nd))
        v = c.emit(E_V, lambda e: e.tensor_scalar_mul(
            out=stat_s[:, width:2 * width], in0=p7[0:1, 0:width],
            scalar1=1.0 / nd), inc="V")
        c.ps_consume("b6", "V", v)
        c.ps_consume("b7", "V", v)
        c.emit(E_V, lambda e: e.scalar_tensor_tensor(
            out=stat_s[:, 2 * width:3 * width], in0=stat_s[:, 0:width],
            scalar=-1.0, in1=stat_s[:, 0:width], op0=OP.mult, op1=OP.mult))
        v = c.emit(E_V, lambda e: e.tensor_tensor(
            out=stat_s[:, 2 * width:3 * width], in0=stat_s[:, width:2 * width],
            in1=stat_s[:, 2 * width:3 * width], op=OP.add), inc="V")
        c.wait(E_S, "V", v)
        c.emit(E_S, lambda e: e.activation(
            out=stat_s[:, 2 * width:3 * width],
            in_=stat_s[:, 2 * width:3 * width], func=AF.Ln, bias=fc_eps[0:1, :]))
        s = c.emit(E_S, lambda e: e.activation(
            out=prow_s[:, 0:width],
            in_=stat_s[:, 2 * width:3 * width], func=AF.Exp, scale=fc_mhalf[0:1, :]),
            inc="S")
        c.wait(E_V, "S", s)
        v = c.emit(E_V, lambda e: e.tensor_tensor(
            out=qrow_s[:, 0:width], in0=stat_s[:, 0:width],
            in1=prow_s[:, 0:width].bitcast(F32), op=OP.mult), inc="V")
        c.wait(E_T, "V", v)
        c.ps_wait(E_T, "b0")
        c.mm(psD_a[:, 0:width], ones1_s[:], prow_s[:, 0:width], True, True)
        c.ps_wait(E_T, "b1")
        tQ = c.mm(psD_b[:, 0:width], ones1_s[:], qrow_s[:, 0:width], True, True,
                  inc="T")
        c.wait(E_V, "T", tQ)
        out_v = 0
        for pb in range(npb):
            c.emit(E_V, lambda e, pb=pb: e.tensor_tensor(
                out=sqF[:, 0:width], in0=src_fn(pb),
                in1=psD_a[:, 0:width], op=OP.mult))
            if gb is None:
                out_v = c.emit(E_V, lambda e, pb=pb: e.tensor_tensor(
                    out=dst_fn(pb), in0=sqF[:, 0:width],
                    in1=psD_b[:, 0:width], op=OP.subtract), inc="V")
            else:
                g_ap, b_ap = gb
                c.emit(E_V, lambda e, pb=pb: e.tensor_tensor(
                    out=sqF[:, 0:width],
                    in0=sqF[:, 0:width],
                    in1=psD_b[:, 0:width], op=OP.subtract))
                out_v = c.emit(E_V, lambda e, pb=pb, g=g_ap, b=b_ap: e.tensor_scalar(
                    out=dst_fn(pb), in0=sqF[:, 0:width],
                    scalar1=g[:, pb:pb + 1], scalar2=b[:, pb:pb + 1],
                    op0=OP.mult, op1=OP.add), inc="V")
        c.ps_consume("b0", "V", out_v)
        c.ps_consume("b1", "V", out_v)
        return out_v

    consts_ld = None
    for (dst, src) in (
        (identb_s, identb_d), (identf_s, identf_d), (onesc_s, onesc_d),
        (ones1_s, ones1_d), (mul_s, mul_d), (mll_s, mll_d), (mfull_s, mfull_d),
        (hpat_s, hpat_d), (convw_s, convw_d), (convb_s, convb_d), (Dvec_s, Dvec_d),
        (bdt_s, bdt_d), (alog_s, alog_d), (fconst_s, fconst_d),
    ):
        consts_ld = c.dma(E_SY, dst[:], src[:])
    for l in range(2):
        for (dst, src) in ((lng_s[l], lng_d[l]), (lnb_s[l], lnb_d[l]),
                           (b1a_s[l], b1a_d[l]), (b1g_s[l], b1g_d[l]),
                           (b2_s[l], b2_d[l]), (qs_s[l], qs_d[l]),
                           (ks_s[l], ks_d[l])):
            consts_ld = c.dma(E_SY, dst[:], src[:])

    c.emit(E_V, lambda e: e.memset(zero_s[:], 0.0), inc="V")
    for eng in (E_T, E_V, E_S, E_G):
        c.wait(eng, "DS", consts_ld)

    c.wait(E_T, "DS", consts_ld)
    c.mm(psD[:, 0:NS], ones1_s[:], alog_s[:], True, True, inc="T")
    c.wait(E_S, "T", c.cnt["T"])
    c.emit(E_S, lambda e: e.activation(out=acol_s[:], in_=psD[:, 0:NS],
                                       func=AF.Exp), inc="S")
    c.wait(E_V, "S", c.cnt["S"])
    v = c.emit(E_V, lambda e: e.tensor_scalar_mul(out=acol_s[:], in0=acol_s[:],
                                                  scalar1=-1.0), inc="V")
    c.ps_consume("b0", "V", v)

    if STOP == "p0a":
        return c.build()
    p0mark = c.sbuf_mark()
    xin_s = c.sbuf("xin_s", (128, 2 * D), F32)
    otile_s = c.sbuf("otile_s", (128, 2 * D), F32R)
    tp_banks = [psD_a, psD_b, psD2_a, psD2_b]
    tp_ev = []
    tp_dma = []
    tp_tmk = {}
    c.ps_wait(E_T, "b0")
    c.ps_wait(E_T, "b1")
    for tt in range(16):
        ibuf = xin_s[:, (tt % 2) * D:(tt % 2 + 1) * D]
        obuf = otile_s[:, (tt % 2) * D:(tt % 2 + 1) * D]
        if tt >= 2:
            c.wait(E_SY, "T", tp_tmk[tt - 2])
        ld = c.dma(E_SY, ibuf, x_in[tt * 128:(tt + 1) * 128, :])
        if tt >= 2:
            c.wait(E_V, "DG", tp_dma[tt - 2])
        c.wait(E_T, "DS", ld)
        for dtile in range(8):
            slot = dtile % 4
            if len(tp_ev) >= 4:
                c.wait(E_T, "V", tp_ev[-4])
            mk = c.emit(E_T, lambda e, b=tp_banks[slot], i=ibuf, d=dtile:
                        e.transpose(b[:, 0:128], i[:, d * 128:(d + 1) * 128],
                                    identf_s[:]), inc="T")
            if dtile == 7:
                tp_tmk[tt] = mk
            c.wait(E_V, "T", mk)
            ev = c.emit(E_V, lambda e, b=tp_banks[slot], o=obuf, d=dtile:
                        e.tensor_copy(out=o[:, d * 128:(d + 1) * 128],
                                      in_=b[:, 0:128]), inc="V")
            tp_ev.append(ev)
        c.wait(E_G, "V", tp_ev[-1])
        dm = c.dma(E_G, bAP(xct_h, tt * 128, [[T, 128], [128 * T, 8], [1, 128]]),
                   obuf.rearrange("p (a b) -> p a b", b=128))
        tp_dma.append(dm)
    for bn in ("b0", "b1", "b2", "b3"):
        c.ps_consume(bn, "V", tp_ev[-1])

    if STOP == "p0b":
        return c.build()
    zv = c.cnt["V"]
    c.wait(E_G, "V", zv)
    c.dma(E_G, bAP(sa_h, 0, [[T + 8, 128], [128 * (T + 8), 8], [1, 8]]),
          bAP(zero_s, zero_s.offset, [zero_s.ap[0], [0, 8], [1, 8]]))
    c.dma(E_G, bAP(sb_h, 0, [[T + 8, 128], [128 * (T + 8), 8], [1, 8]]),
          bAP(zero_s, zero_s.offset, [zero_s.ap[0], [0, 8], [1, 8]]))

    if STOP == "p0c":
        return c.build()
    c.barrier()
    c.sbuf_free_to(p0mark)
    if STOP == "p0":
        return c.build()

    mamba_mark = c.sbuf_mark()
    bcp_s = c.sbuf("bcp_s", (16, 16 * 128), F32R)
    uc_s = c.sbuf("uc_s", (128, 16 * TC), F32R)
    dt_s = c.sbuf("dt_s", (128, 16 * TC), F32)
    sz_s = c.sbuf("sz_s", (128, 16 * TC), BF16)
    bbc_s = c.sbuf("bbc_s", (128, 8 * TC), BF16)
    cbc_s = c.sbuf("cbc_s", (128, 8 * TC), BF16)
    hall_s = c.sbuf("hall_s", (128, 16 * TC), BF16)
    abar_s = c.sbuf("abar_s", (128, 2 * TC), F32)
    bb_s = c.sbuf("bb_s", (128, 2 * TC), BF16)
    carry_s = c.sbuf("carry_s", (128, 16 * NS), F32)
    upre_s = c.sbuf("upre_s", (128, 3 + TC), F32)
    uhalo_s = c.sbuf("uhalo_s", (128, 16 * 3), F32)
    brow_s = c.sbuf("brow_s", (16, TC), F32R)
    crow_s = c.sbuf("crow_s", (16, TC), F32R)
    dtlr_s = c.sbuf("dtlr_s", (64, TC), F32R)
    g_s = c.sbuf("g_s", (128, TC), BF16)
    y2_s = c.sbuf("y2_s", (128, 2 * TC), F32)
    xw_s = c.sbuf("xw_s", (128, 8 * TC), F32R)
    wx_s = c.sbuf("wx_s", (128, 16 * 80), F32R)
    wxc_s = c.sbuf("wxc_s", (128, 16 * 16), F32R)
    wdt_s = c.sbuf("wdt_s", (64, E), F32R)
    wbufA = c.sbuf("wbufA", (128, 2048), F32R)
    wbufB = c.sbuf("wbufB", (128, 2048), F32R)
    sq_s = c.sbuf("sq_m", (128, TC), F32)

    c.emit(E_V, lambda e: e.memset(carry_s[:], 0.0))
    c.emit(E_V, lambda e: e.memset(uhalo_s[:], 0.0), inc="V")
    zinit = c.cnt["V"]

    c.dma(E_SY, bcp_s[:], bcp_d[:])
    wx_ld = None
    for pb in range(16):
        c.dma(E_SY, wx_s[:, pb * 80:(pb + 1) * 80],
              w_xa[pb * 128:(pb + 1) * 128, :])
        wx_ld = c.dma(E_SY, wxc_s[:, pb * 16:(pb + 1) * 16],
                      w_xc[pb * 128:(pb + 1) * 128, :])
    wdt_ld = c.dma(E_SY, wdt_s[:], w_dt[:])

    winbuf_T = [0, 0]
    uc_wr_S = [0] * 16
    abar_free = [0, 0]
    uc_rd_V = [0] * 16
    sz_rd_V = [0] * 16
    dt_rd_S = 0
    dt_rd_V = 0
    upre_rd_V = 0
    xw_free_V = 0
    xw_dma_G = 0
    hall_rd_T = 0
    bc_rd_V = 0
    sa_dma = []

    for tc in range(NTCH):
        t0 = tc * TC
        c.wait(E_SY, "DG", tp_dma[-1])
        if tc > 0:
            c.wait(E_SY, "V", xw_free_V)
            c.wait(E_SY, "DG", xw_dma_G)
        xld = c.dma(E_SY, xw_s[:].rearrange("p (a b) -> p a b", b=TC),
                    bAP(xct_h, t0, [[T, 128], [128 * T, 8], [1, TC]]))
        for it in range(32):
            wb = (wbufA, wbufB)[it % 2]
            c.wait(E_SY, "T", winbuf_T[it % 2])
            wld = c.dma(E_SY, wb[:, 0:1024].rearrange("p (a b) -> p a b", b=128),
                        bAP(w_in, it * 128, [[2 * E, 128], [128 * 2 * E, 8], [1, 128]]))
            c.wait(E_T, "DS", max(wld, xld))
            bank = (p4, p5)[it % 2]
            c.ps_wait(E_T, f"b{4 + it % 2}")
            for kp in range(8):
                tm = c.mm(bank[:], wb[:, kp * 128:(kp + 1) * 128],
                          xw_s[:, kp * TC:(kp + 1) * TC],
                          kp == 0, kp == 7, inc="T" if kp == 7 else None)
            winbuf_T[it % 2] = tm
            if it < 16:
                dblk = it
                c.wait(E_S, "T", tm)
                c.wait(E_S, "V", upre_rd_V)
                sv = c.emit(E_S, lambda e, b=bank: e.copy(
                    out=upre_s[:, 3:3 + TC], in_=b[:]), inc="S")
                c.ps_consume(f"b{4 + it % 2}", "S", sv)
                c.wait(E_V, "V", zinit)
                c.wait(E_V, "S", sv)
                c.emit(E_V, lambda e, d=dblk: e.tensor_copy(
                    out=upre_s[:, 0:3], in_=uhalo_s[:, d * 3:d * 3 + 3]))
                c.emit(E_V, lambda e, d=dblk: e.tensor_copy(
                    out=uhalo_s[:, d * 3:d * 3 + 3], in_=upre_s[:, TC:TC + 3]))
                c.emit(E_V, lambda e, d=dblk: e.tensor_scalar(
                    out=sq_s[:], in0=upre_s[:, 0:TC],
                    scalar1=convw_s[:, d * 4:d * 4 + 1], scalar2=None,
                    op0=OP.mult))
                for k in range(1, 4):
                    v1 = c.emit(E_V, lambda e, k=k, d=dblk: e.scalar_tensor_tensor(
                        out=sq_s[:], in0=upre_s[:, k:k + TC],
                        scalar=convw_s[:, d * 4 + k:d * 4 + k + 1], in1=sq_s[:],
                        op0=OP.mult, op1=OP.add),
                        inc="V" if k == 3 else None)
                upre_rd_V = v1
                c.wait(E_S, "V", v1)
                c.wait(E_S, "V", uc_rd_V[dblk])
                sv = c.emit(E_S, lambda e, d=dblk: e.activation(
                    out=uc_s[:, d * TC:(d + 1) * TC], in_=sq_s[:],
                    func=AF.Silu, bias=convb_s[:, d:d + 1]), inc="S")
                uc_wr_S[dblk] = sv
            else:
                dblk = it - 16
                c.wait(E_S, "T", tm)
                c.wait(E_S, "V", sz_rd_V[dblk])
                sv = c.emit(E_S, lambda e, b=bank, d=dblk: e.activation(
                    out=sz_s[:, d * TC:(d + 1) * TC], in_=b[:], func=AF.Silu),
                    inc="S")
                c.ps_consume(f"b{4 + it % 2}", "S", sv)
        sz_wr_S = c.cnt["S"]

        if STOP == "m_pre":
            break
        c.wait(E_T, "DS", wx_ld)
        c.ps_wait(E_T, "b2")
        c.ps_wait(E_T, "b3")
        for pb in range(16):
            c.wait(E_T, "S", uc_wr_S[pb])
            c.mm(psD2_a[0:80, :], wx_s[:, pb * 80:(pb + 1) * 80],
                 uc_s[:, pb * TC:(pb + 1) * TC], pb == 0, pb == 15)
        for pb in range(16):
            tm = c.mm(psD2_b[0:16, :], wxc_s[:, pb * 16:(pb + 1) * 16],
                      uc_s[:, pb * TC:(pb + 1) * TC], pb == 0, pb == 15,
                      inc="T" if pb == 15 else None)
        c.wait(E_V, "T", tm)
        c.emit(E_V, lambda e: e.tensor_copy(out=dtlr_s[:], in_=psD2_a[0:64, :]))
        c.emit(E_V, lambda e: e.tensor_copy(out=brow_s[:], in_=psD2_a[64:80, :]))
        v2 = c.emit(E_V, lambda e: e.tensor_copy(out=crow_s[:],
                                                 in_=psD2_b[0:16, :]), inc="V")
        c.ps_consume("b2", "V", v2)
        c.ps_consume("b3", "V", v2)
        xdbl_V = v2

        c.wait(E_T, "DS", wdt_ld)
        c.wait(E_T, "V", xdbl_V)
        dt_wr_V = [0] * 16
        dt_sq_V = 0
        for dblk in range(16):
            bank = (p6, p7)[dblk % 2]
            c.ps_wait(E_T, f"b{6 + dblk % 2}")
            tm = c.mm(bank[:], wdt_s[:, dblk * 128:(dblk + 1) * 128],
                      dtlr_s[:], True, True, inc="T")
            c.wait(E_S, "T", tm)
            c.wait(E_S, "V", dt_rd_V)
            c.wait(E_S, "V", dt_sq_V)
            c.emit(E_S, lambda e, b=bank, d=dblk: e.activation(
                out=sq_s[:], in_=b[:], func=AF.Abs, bias=bdt_s[:, d:d + 1]))
            c.emit(E_S, lambda e: e.activation(
                out=sq_s[:], in_=sq_s[:], func=AF.Exp, scale=fc_mone))
            s3 = c.emit(E_S, lambda e: e.activation(
                out=sq_s[:], in_=sq_s[:], func=AF.Ln, bias=fc_one), inc="S")
            c.wait(E_V, "S", s3)
            v1 = c.emit(E_V, lambda e, b=bank, d=dblk: e.tensor_scalar(
                out=y2_s[:, 0:TC], in0=b[:], scalar1=bdt_s[:, d:d + 1],
                scalar2=0.0, op0=OP.add, op1=OP.max), inc="V")
            dt_wr_V[dblk] = c.emit(E_V, lambda e, d=dblk: e.tensor_tensor(
                out=dt_s[:, d * TC:(d + 1) * TC], in0=y2_s[:, 0:TC],
                in1=sq_s[:], op=OP.add), inc="V")
            dt_sq_V = dt_wr_V[dblk]
            c.ps_consume(f"b{6 + dblk % 2}", "V", v1)

        if DBG2:
            c.wait(E_G, "S", c.cnt["S"])
            c.wait(E_G, "V", c.cnt["V"])
            c.dma(E_G, dbg_uc[:, tc * 16 * TC:(tc + 1) * 16 * TC], uc_s[:], sem="DO")
            c.dma(E_G, dbg_dt[:, tc * 16 * TC:(tc + 1) * 16 * TC], dt_s[:], sem="DO")
            c.wait(E_G, "DO", c.cnt["DO"])
        if STOP == "m_dt_tap" and tc == NTCH - 1:
            break
        if STOP == "m_dt":
            break
        y3_wr_V = [0] * 16
        for half in range(2):
            for j, n in enumerate(range(8 * half, 8 * half + 8)):
                for (rows_s, dstbuf) in ((brow_s, bbc_s), (crow_s, cbc_s)):
                    c.ps_wait(E_T, "b3")
                    c.wait(E_T, "V", max(xdbl_V, bc_rd_V))
                    tm = c.mm(psD2_b[:], bcp_s[:, n * 128:(n + 1) * 128],
                              rows_s[:], True, True, inc="T")
                    c.wait(E_V, "T", tm)
                    ev = c.emit(E_V, lambda e, d=dstbuf, j=j: e.tensor_copy(
                        out=d[:, j * TC:(j + 1) * TC], in_=psD2_b[:]), inc="V")
                    c.ps_consume("b3", "V", ev)

            for dblk in range(16):
                c.wait(E_V, "S", uc_wr_S[dblk])
                gv = c.emit(E_V, lambda e, d=dblk: e.tensor_tensor(
                    out=g_s[:], in0=dt_s[:, d * TC:(d + 1) * TC],
                    in1=uc_s[:, d * TC:(d + 1) * TC], op=OP.mult), inc="V")
                dt_rd_V = gv
                ybank = (p4, p5)[dblk % 2]
                if half == 0:
                    c.ps_wait(E_T, f"b{4 + dblk % 2}")
                scan_mk = [0] * 8
                for j, n in enumerate(range(8 * half, 8 * half + 8)):
                    pp = j % 2
                    c.wait(E_S, "V", abar_free[pp])
                    se = c.emit(E_S, lambda e, d=dblk, n=n, pp=pp: e.activation(
                        out=abar_s[:, pp * TC:(pp + 1) * TC],
                        in_=dt_s[:, d * TC:(d + 1) * TC], func=AF.Exp,
                        scale=acol_s[:, n:n + 1]), inc="S")
                    bv = c.emit(E_V, lambda e, j=j, pp=pp: e.tensor_tensor(
                        out=bb_s[:, pp * TC:(pp + 1) * TC], in0=g_s[:],
                        in1=bbc_s[:, j * TC:(j + 1) * TC], op=OP.mult))
                    c.wait(E_V, "S", se)
                    c.wait(E_V, "T", hall_rd_T)
                    init = carry_s[:, dblk * NS + n:dblk * NS + n + 1] \
                        if tc > 0 else 0.0
                    sv = c.emit(E_V, lambda e, n=n, pp=pp, init=init:
                                e.tensor_tensor_scan(
                                    out=hall_s[:, n * TC:(n + 1) * TC],
                                    data0=abar_s[:, pp * TC:(pp + 1) * TC],
                                    data1=bb_s[:, pp * TC:(pp + 1) * TC],
                                    initial=init, op0=OP.mult, op1=OP.add),
                                inc="V")
                    scan_mk[j] = sv
                    abar_free[pp] = sv
                if tc < NTCH - 1:
                    n0 = 8 * half
                    c.emit(E_V, lambda e, d=dblk, n0=n0: e.tensor_copy(
                        out=carry_s[:, d * NS + n0:d * NS + n0 + 8],
                        in_=bAP(hall_s, hall_s.offset + n0 * TC + TC - 1,
                                [hall_s.ap[0], [TC, 8]])))
                hvs = []
                for j, n in enumerate(range(8 * half, 8 * half + 8)):
                    hv = c.emit(E_V, lambda e, n=n, j=j: e.tensor_tensor(
                        out=hall_s[:, n * TC:(n + 1) * TC],
                        in0=hall_s[:, n * TC:(n + 1) * TC],
                        in1=cbc_s[:, j * TC:(j + 1) * TC], op=OP.mult), inc="V")
                    hvs.append(hv)
                fence = c.emit(E_V, lambda e: e.memset(fence_s[:, 1:2], 0.0),
                               inc="V")
                for j, n in enumerate(range(8 * half, 8 * half + 8)):
                    c.wait(E_T, "V", min(hvs[j] + 1, fence))
                    tm = c.mm(ybank[:], identb_s[:],
                              hall_s[:, n * TC:(n + 1) * TC],
                              half == 0 and j == 0, half == 1 and j == 7,
                              inc="T")
                if half == 1:
                    hall_rd_T = tm
                    c.wait(E_V, "T", tm)
                    c.wait(E_V, "S", sz_wr_S)
                    c.emit(E_V, lambda e, d=dblk, b=ybank: e.scalar_tensor_tensor(
                        out=y2_s[:, (d % 2) * TC:(d % 2 + 1) * TC],
                        in0=uc_s[:, d * TC:(d + 1) * TC],
                        scalar=Dvec_s[:, d:d + 1], in1=b[:],
                        op0=OP.mult, op1=OP.add))
                    v2 = c.emit(E_V, lambda e, d=dblk: e.tensor_tensor(
                        out=uc_s[:, d * TC:(d + 1) * TC],
                        in0=y2_s[:, (d % 2) * TC:(d % 2 + 1) * TC],
                        in1=sz_s[:, d * TC:(d + 1) * TC], op=OP.mult), inc="V")
                    c.ps_consume(f"b{4 + dblk % 2}", "V", v2)
                    y3_wr_V[dblk] = v2
                    uc_rd_V[dblk] = v2
                    sz_rd_V[dblk] = v2
            bc_rd_V = c.cnt["V"]
        dt_rd_S = c.cnt["S"]
        dt_rd_V = c.cnt["V"]

        if STOP == "m_scan":
            break
        if STOP == "m_loop":
            continue
        for i in range(8):
            wb = (wbufA, wbufB)[i % 2]
            c.wait(E_SY, "T", winbuf_T[i % 2])
            wld = c.dma(E_SY, wb[:].rearrange("p (a b) -> p a b", b=128),
                        bAP(w_out, i * 128, [[D, 128], [128 * D, 16], [1, 128]]))
            c.wait(E_T, "DS", wld)
            bank = (p6, p7)[i % 2]
            c.ps_wait(E_T, f"b{6 + i % 2}")
            c.wait(E_T, "V", max(y3_wr_V))
            for kp in range(16):
                tm = c.mm(bank[:], wb[:, kp * 128:(kp + 1) * 128],
                          uc_s[:, kp * TC:(kp + 1) * TC], kp == 0, kp == 15,
                          inc="T" if kp == 15 else None)
            winbuf_T[i % 2] = tm
            c.wait(E_V, "T", tm)
            rv = c.emit(E_V, lambda e, i=i, b=bank: e.tensor_tensor(
                out=xw_s[:, i * TC:(i + 1) * TC],
                in0=xw_s[:, i * TC:(i + 1) * TC], in1=b[:], op=OP.add), inc="V")
            c.ps_consume(f"b{6 + i % 2}", "V", rv)
        x1pre_V = c.cnt["V"]

        if STOP == "m_wout":
            break
        lnm = layer_norm(lambda pb: xw_s[:, pb * TC:(pb + 1) * TC],
                         lambda pb: xw_s[:, pb * TC:(pb + 1) * TC],
                         8, TC, sq_s, x1pre_V)
        c.wait(E_G, "V", lnm)
        sd = c.dma(E_G, bAP(sa_h, 8 + t0, [[T + 8, 128], [128 * (T + 8), 8], [1, TC]]),
                   xw_s[:].rearrange("p (a b) -> p a b", b=TC))
        sa_dma.append(sd)
        xw_free_V = lnm
        xw_dma_G = sd

    if STOP in ("m_pre", "m_dt", "m_dt_tap", "m_scan", "m_loop", "m_wout", "mamba"):
        return c.build()
    c.barrier()
    if STOP == "mamba2":
        return c.build()
    c.sbuf_free_to(mamba_mark)

    xw2_s = c.sbuf("xw2_s", (128, 8 * (WIN + 1)), F32R)
    qraw_s = c.sbuf("qraw_s", (128, 4 * WIN), F32)
    qh_s = c.sbuf("qh_s", (128, 4 * WIN), BF16)
    kpair_s = c.sbuf("kpair_s", (128, 4 * 2 * WIN), BF16)
    vcur_s = c.sbuf("vcur_s", (128, 4 * WIN), BF16)
    vT_s = c.sbuf("vT_s", (128, 8 * WIN), BF16)
    attn_s = c.sbuf("attn_s", (128, 4 * 640), BF16)
    attnT_s = c.sbuf("attnT_s", (128, 2 * 640), BF16)
    oT_s = c.sbuf("oT_s", (128, 4 * WIN), F32R)
    hn_s = c.sbuf("hn_s", (128, 8 * WIN), F32R)
    h2_s = c.sbuf("h2_s", (128, NIT1 * WIN), BF16)
    map_s = c.sbuf("map_s", (128, 2 * WIN), F32)
    t1_s = c.sbuf("t1_s", (128, 2 * WIN), F32)
    rn_s = c.sbuf("rn_s", (8, WIN), F32R)
    rn2_s = c.sbuf("rn2_s", (8, WIN), F32)
    nm_s = c.sbuf("nm_s", (128, 32), F32)
    rs_s = c.sbuf("rs_s", (128, 32), F32)
    rr_s = c.sbuf("rr_s", (128, 32), F32)
    halo2_s = c.sbuf("halo2_s", (128, 16), F32)
    rotf_s = c.sbuf("rotf_s", (128, WIN), F32)
    rotg_s = c.sbuf("rotg_s", (128, WIN), F32)
    sq2_s = c.sbuf("sq2_s", (128, WIN), F32)
    wb1_s = c.sbuf("wb1_s", (128, 1024), F32R)
    wb2_s = c.sbuf("wb2_s", (128, 1024), F32R)
    w2bufA = c.sbuf("w2bufA", (128, 2816), BF16)
    w2bufB = c.sbuf("w2bufB", (128, 2816), BF16)
    hm8_s = c.sbuf("hm8_s", (8, 512), F32R)
    hm1_s = c.sbuf("hm1_s", (8, 512), F32R)
    qpat_s = [c.sbuf(f"qpat_s{l}", (8, 512), F32R) for l in range(2)]
    kpat_s = [c.sbuf(f"kpat_s{l}", (8, 512), F32R) for l in range(2)]
    qpatS_s = [c.sbuf(f"qpatS_s{l}", (8, 512), F32R) for l in range(2)]
    kpatS_s = [c.sbuf(f"kpatS_s{l}", (8, 512), F32R) for l in range(2)]
    x3_s = hn_s
    ot2_s = qraw_s

    pld = c.dma(E_SY, hm8_s[:], hm8_d[:])
    pld = c.dma(E_SY, hm1_s[:], hm1_d[:])
    c.wait(E_V, "DS", pld)
    for l in range(2):
        for (src_, dstN, dstS, hm) in (
            (qs_s[l], qpat_s[l], qpatS_s[l], hm8_s),
            (ks_s[l], kpat_s[l], kpatS_s[l], hm1_s),
        ):
            for swapped, dst in ((False, dstN), (True, dstS)):
                if not swapped:
                    vv = c.emit(E_V, lambda e, s=src_: e.tensor_copy(
                        out=qsr_s[:].rearrange("p (a b) -> p a b", b=DH),
                        in_=bAP(s, s.offset, [s.ap[0], [0, 8], [1, DH]])),
                        inc="V")
                else:
                    c.emit(E_V, lambda e, s=src_: e.tensor_copy(
                        out=bAP(qsr_s, qsr_s.offset, [qsr_s.ap[0], [64, 8], [1, 32]]),
                        in_=bAP(s, s.offset + 32, [s.ap[0], [0, 8], [1, 32]])))
                    vv = c.emit(E_V, lambda e, s=src_: e.tensor_copy(
                        out=bAP(qsr_s, qsr_s.offset + 32, [qsr_s.ap[0], [64, 8], [1, 32]]),
                        in_=bAP(s, s.offset, [s.ap[0], [0, 8], [1, 32]])),
                        inc="V")
                c.wait(E_T, "V", vv)
                c.ps_wait(E_T, "b0")
                tv = c.mm(psD_a[0:8, :], ones1_s[:, 0:8], qsr_s[:], True, True,
                          inc="T")
                c.wait(E_V, "T", tv)
                vv = c.emit(E_V, lambda e, d=dst, h=hm: e.tensor_tensor(
                    out=d[:], in0=psD_a[0:8, :], in1=h[:], op=OP.mult), inc="V")
                c.ps_consume("b0", "V", vv)

    out_dma = []

    def ffn_layer(l, src_h, dst_h, final):
        nonlocal out_dma
        kv_init = c.emit(E_V, lambda e: e.memset(kpair_s[:], 0.0))
        kv_init = c.emit(E_V, lambda e: e.memset(vT_s[:], 0.0))
        kv_init = c.emit(E_V, lambda e: e.memset(halo2_s[:], 0.0), inc="V")

        wbuf_T = [0, 0]
        w2buf_T = [0, 0]
        ev4 = [0, 0, 0, 0]
        odma = []
        xw2_free = kv_init
        qraw_free_V = 0
        qraw_free_S = 0
        vT_T = 0
        rot_rd_V = 0
        nm_free_S = 0
        qh_rd_T = 0
        kv_rd_T = 0
        attn_free = [0, 0, 0, 0]
        attnT_free = [0, 0]
        oT_rd_T = 0
        hn_rd_T = 0
        h2_rd_T = 0
        x3_free = [0, 0]

        for w in range(NW):
            t0 = w * WIN
            c.wait(E_SY, "V", xw2_free)
            if final and w > 0:
                c.wait(E_V, "DO", c.cnt["DO"])
            xld = c.dma(E_SY, xw2_s[:].rearrange("p (a b) -> p a b", b=WIN + 1),
                        bAP(src_h, t0 + 7, [[T + 8, 128], [128 * (T + 8), 8], [1, WIN + 1]]))
            shift_ap = lambda pb, w_=0: (
                xw2_s[:, pb * (WIN + 1) + 1: pb * (WIN + 1) + 1 + WIN] if pb < 4
                else xw2_s[:, (pb - 0) * (WIN + 1): (pb) * (WIN + 1) + WIN])

            c.wait(E_V, "T", kv_rd_T)
            for pb in range(4):
                c.emit(E_V, lambda e, pb=pb: e.tensor_copy(
                    out=kpair_s[:, pb * 1024:pb * 1024 + WIN],
                    in_=kpair_s[:, pb * 1024 + WIN:pb * 1024 + 2 * WIN]))
            kvs = c.emit(E_V, lambda e: e.tensor_copy(
                out=vT_s[:, 0:4 * WIN], in_=vT_s[:, 4 * WIN:8 * WIN]), inc="V")

            for part, patN, patS, rf_d, rg_d in (
                (0, qpat_s[l], qpatS_s[l], rotf_d[0], rotg_d[0]),
                (1, kpat_s[l], kpatS_s[l], rotf_d[1], rotg_d[1]),
            ):
                c.wait(E_SY, "V", rot_rd_V)
                rld1 = c.dma(E_SY, rotf_s[:], rf_d[:, t0:t0 + WIN])
                rld2 = c.dma(E_SY, rotg_s[:], rg_d[:, t0:t0 + WIN])
                c.ps_wait(E_T, "b6")
                tnq = 0
                for iloc in range(4):
                    it = part * 4 + iloc
                    wb = (wb1_s, wb2_s)[iloc % 2]
                    c.wait(E_SY, "T", wbuf_T[iloc % 2])
                    wld = c.dma(E_SY,
                                bAP(wb, 0, [[1024, 128], [128, 8], [1, 128]]),
                                bAP(wqkv_d[l], it * 128,
                                    [[3 * INNER, 128], [128 * 3 * INNER, 8], [1, 128]]))
                    c.wait(E_T, "DS", max(wld, xld))
                    bank = (p4, p5)[iloc % 2]
                    c.ps_wait(E_T, f"b{4 + iloc % 2}")
                    for kp in range(8):
                        tm = c.mm(bank[:], wb[:, kp * 128:(kp + 1) * 128],
                                  shift_ap(kp), kp == 0, kp == 7,
                                  inc="T" if kp == 7 else None)
                    wbuf_T[iloc % 2] = tm
                    c.wait(E_V, "V", qraw_free_V)
                    c.wait(E_V, "S", qraw_free_S)
                    c.wait(E_V, "T", tm)
                    ev = c.emit(E_V, lambda e, b=bank, iloc=iloc: e.tensor_copy(
                        out=qraw_s[:, iloc * WIN:(iloc + 1) * WIN], in_=b[:]),
                        inc="V")
                    c.ps_consume(f"b{4 + iloc % 2}", "V", ev)
                    c.wait(E_S, "V", ev)
                    c.wait(E_S, "T", tnq)
                    sv = c.emit(E_S, lambda e, iloc=iloc: e.activation(
                        out=lnsq_s[:, 0:WIN],
                        in_=qraw_s[:, iloc * WIN:(iloc + 1) * WIN],
                        func=AF.Square), inc="S")
                    c.wait(E_T, "S", sv)
                    tnq = c.mm(p6[0:8, :], hpat_s[:, iloc * 8:(iloc + 1) * 8],
                               lnsq_s[:, 0:WIN], iloc == 0, iloc == 3,
                               inc="T" if iloc == 3 else None)
                qkv_ev_V = c.cnt["V"]
                qraw_free_S = c.cnt["S"]
                c.wait(E_V, "T", tnq)
                c.wait(E_V, "S", c.cnt["S"])
                vv = c.emit(E_V, lambda e: e.tensor_scalar_max(
                    out=rn2_s[:], in0=p6[0:8, :], scalar1=1e-24),
                    inc="V")
                c.ps_consume("b6", "V", vv)
                c.wait(E_S, "V", vv)
                c.emit(E_S, lambda e: e.activation(
                    out=rn2_s[:], in_=rn2_s[:], func=AF.Ln))
                sv = c.emit(E_S, lambda e: e.activation(
                    out=rn_s[:], in_=rn2_s[:], func=AF.Exp,
                    scale=fc_mhalf[0:8, :]), inc="S")
                c.wait(E_T, "S", sv)
                c.wait(E_T, "DS", max(rld1, rld2))
                for iloc in range(4):
                    it = part * 4 + iloc
                    wb = (wb1_s, wb2_s)[iloc % 2]
                    c.wait(E_SY, "T", wbuf_T[iloc % 2])
                    for (oofs, iofs) in ((0, 32), (32, 0), (64, 96), (96, 64)):
                        wldS = c.dma(E_SY,
                                     bAP(wb, oofs, [[1024, 128], [128, 8], [1, 32]]),
                                     bAP(wqkv_d[l], it * 128 + iofs,
                                         [[3 * INNER, 128],
                                          [128 * 3 * INNER, 8], [1, 32]]))
                    c.wait(E_T, "DS", wldS)
                    c.ps_wait(E_T, "b6")
                    for kp in range(8):
                        tsw = c.mm(p6[:], wb[:, kp * 128:(kp + 1) * 128],
                                   shift_ap(kp), kp == 0, kp == 7,
                                   inc="T" if kp == 7 else None)
                    wbuf_T[iloc % 2] = tsw
                    c.ps_wait(E_T, "b7")
                    tS1 = c.mm(p7[:], patN[:, iloc * 128:(iloc + 1) * 128],
                               rn_s[:], True, True, inc="T")
                    c.wait(E_V, "T", tS1)
                    mv1 = c.emit(E_V, lambda e: e.tensor_tensor(
                        out=map_s[:, 0:WIN], in0=p7[:], in1=rotf_s[:],
                        op=OP.mult), inc="V")
                    c.ps_consume("b7", "V", mv1)
                    c.ps_wait(E_T, "b7")
                    tS2 = c.mm(p7[:], patS[:, iloc * 128:(iloc + 1) * 128],
                               rn_s[:], True, True, inc="T")
                    c.wait(E_V, "T", tS2)
                    mv2 = c.emit(E_V, lambda e: e.tensor_tensor(
                        out=map_s[:, WIN:2 * WIN], in0=p7[:], in1=rotg_s[:],
                        op=OP.mult), inc="V")
                    c.ps_consume("b7", "V", mv2)
                    c.emit(E_V, lambda e, iloc=iloc: e.tensor_tensor(
                        out=t1_s[:, 0:WIN],
                        in0=qraw_s[:, iloc * WIN:(iloc + 1) * WIN],
                        in1=map_s[:, 0:WIN], op=OP.mult))
                    c.wait(E_V, "T", tsw)
                    c.emit(E_V, lambda e: e.tensor_tensor(
                        out=t1_s[:, WIN:2 * WIN], in0=p6[:],
                        in1=map_s[:, WIN:2 * WIN], op=OP.mult))
                    if part == 0:
                        dst = qh_s[:, iloc * WIN:(iloc + 1) * WIN]
                    else:
                        dst = kpair_s[:, iloc * 1024 + WIN:iloc * 1024 + 2 * WIN]
                    c.wait(E_V, "T", qh_rd_T if part == 0 else kv_rd_T)
                    qv = c.emit(E_V, lambda e, d=dst: e.tensor_tensor(
                        out=d, in0=t1_s[:, 0:WIN], in1=t1_s[:, WIN:2 * WIN],
                        op=OP.add), inc="V")
                    c.ps_consume("b6", "V", qv)
                qraw_free_V = c.cnt["V"]
                rot_rd_V = c.cnt["V"]
            qk_done_V = c.cnt["V"]

            if STOP == "f_qkv":
                break
            if DBG3 and l == DBGL:
                c.wait(E_G, "V", qk_done_V)
                c.dma(E_G, dbg_qh[:, w * 4 * WIN:(w + 1) * 4 * WIN], qh_s[:], sem="DO")
                c.dma(E_G, bAP(dbg_kh, dbg_kh.offset + w * 4 * WIN,
                               [dbg_kh.ap[0], [WIN, 4], [1, WIN]]),
                      bAP(kpair_s, kpair_s.offset + WIN,
                          [kpair_s.ap[0], [1024, 4], [1, WIN]]), sem="DO")
                c.wait(E_G, "DO", c.cnt["DO"])
                for _e in (E_V, E_S, E_T, E_SY):
                    c.wait(_e, "DO", c.cnt["DO"])
            for iloc in range(4):
                it = 8 + iloc
                wb = (wb1_s, wb2_s)[iloc % 2]
                c.wait(E_SY, "T", wbuf_T[iloc % 2])
                wld = c.dma(E_SY, bAP(wb, 0, [[1024, 128], [128, 8], [1, 128]]),
                            bAP(wqkv_d[l], it * 128,
                                [[3 * INNER, 128], [128 * 3 * INNER, 8], [1, 128]]))
                c.wait(E_T, "DS", max(wld, xld))
                bank = (p4, p5)[iloc % 2]
                c.ps_wait(E_T, f"b{4 + iloc % 2}")
                for kp in range(8):
                    tm = c.mm(bank[:], wb[:, kp * 128:(kp + 1) * 128],
                              shift_ap(kp), kp == 0, kp == 7,
                              inc="T" if kp == 7 else None)
                wbuf_T[iloc % 2] = tm
                c.wait(E_S, "T", tm)
                c.wait(E_S, "T", vT_T)
                c.wait(E_S, "V", kvs)
                sv = c.emit(E_S, lambda e, b=bank, iloc=iloc: e.copy(
                    out=vcur_s[:, iloc * WIN:(iloc + 1) * WIN], in_=b[:]), inc="S")
                c.ps_consume(f"b{4 + iloc % 2}", "S", sv)
            vcur_S = c.cnt["S"]
            c.wait(E_T, "S", vcur_S)
            for bn in ("b2", "b3", "b6", "b7"):
                c.ps_wait(E_T, bn)
            tp_ev2 = []
            for tch in range(4):
                for pb in range(4):
                    slot = (tch * 4 + pb) % 4
                    bankap = (psD2_a[:, 0:64], psD2_b[:, 0:64],
                              p6[:, 0:64], p7[:, 0:64])[slot]
                    if len(tp_ev2) >= 4:
                        c.wait(E_T, "V", tp_ev2[-4])
                    mk = c.emit(E_T, lambda e, b=bankap, tch=tch, pb=pb:
                                e.transpose(b.bitcast(BF16),
                                            vcur_s[:, pb * WIN + tch * 128:
                                                   pb * WIN + (tch + 1) * 128],
                                            identb_s[:]), inc="T")
                    c.wait(E_V, "T", mk)
                    ev = c.emit(E_V, lambda e, b=bankap, tch=tch, pb=pb:
                                e.tensor_copy(
                        out=vT_s[:, (4 + tch) * WIN + pb * 128:
                                 (4 + tch) * WIN + (pb + 1) * 128],
                        in_=b.bitcast(BF16)), inc="V")
                    tp_ev2.append(ev)
            vT_V = c.cnt["V"]
            vT_T = c.cnt["T"]
            for bn in ("b2", "b3", "b6", "b7"):
                c.ps_consume(bn, "V", vT_V)

            if STOP == "f_v":
                break
            if DBG4 and l == DBGL and w == 1:
                c.wait(E_G, "V", vT_V)
                c.dma(E_G, dbg_vt[:], vT_s[:], sem="DO")
                c.wait(E_G, "DO", c.cnt["DO"])
                for _e in (E_V, E_S, E_T, E_SY):
                    c.wait(_e, "DO", c.cnt["DO"])
            for h in range(H):
                hp = (h % 2) * 64
                hb = (h // 2)
                for qt in range(4):
                    gslot = qt
                    c.ps_wait(E_T, "b0")
                    c.ps_wait(E_T, "b1")
                    c.wait(E_T, "V", qk_done_V)
                    c.wait(E_T, "V", attn_free[gslot])
                    lq = qh_s[hp:hp + 64, hb * WIN + qt * 128:hb * WIN + (qt + 1) * 128]
                    rk = kpair_s[hp:hp + 64, hb * 1024 + qt * 128:hb * 1024 + qt * 128 + 640]
                    nfull = max(0, (512 - 128 * qt) // 128) if w == 0 else 0
                    ops0 = [(psD[:, 0:512], lq,
                             kpair_s[hp:hp + 64,
                                     hb * 1024 + qt * 128:hb * 1024 + qt * 128 + 512]),
                            (psD[:, 0:128], mul_s[:], identb_s[:])]
                    for ch in range(min(nfull, 4)):
                        ops0.append((psD[:, ch * 128:(ch + 1) * 128], mfull_s[:],
                                     identb_s[:]))
                    for oi, (po, la, ra) in enumerate(ops0):
                        c.mm(po, la, ra, oi == 0, oi == len(ops0) - 1)
                    ops1 = [(psD[:, 512:640], lq,
                             kpair_s[hp:hp + 64,
                                     hb * 1024 + qt * 128 + 512:
                                     hb * 1024 + qt * 128 + 640]),
                            (psD[:, 512:640], mll_s[:], identb_s[:])]
                    if nfull > 4:
                        ops1.append((psD[:, 512:640], mfull_s[:], identb_s[:]))
                    tm = None
                    for oi, (po, la, ra) in enumerate(ops1):
                        tm = c.mm(po, la, ra, oi == 0, oi == len(ops1) - 1,
                                  inc="T" if oi == len(ops1) - 1 else None)
                    col = h * 4 + qt
                    if ALEVEL < 1:
                        continue
                    c.wait(E_V, "T", tm)
                    c.wait(E_V, "S", nm_free_S)
                    vv = c.emit(E_V, lambda e, col=col: e.tensor_reduce(
                        out=nm_s[:, col:col + 1].rearrange("p (a o) -> p a o", o=1),
                        in_=psD[:, 0:640].rearrange("p (a o) -> p a o", o=640),
                        axis=AX.X, op=OP.max, negate=True), inc="V")
                    if ALEVEL < 2:
                        c.ps_consume("b0", "V", vv)
                        c.ps_consume("b1", "V", vv)
                        continue
                    c.wait(E_S, "V", vv)
                    sv = c.emit(E_S, lambda e, col=col, g=gslot: e.activation(
                        out=attn_s[:, g * 640:(g + 1) * 640], in_=psD[:, 0:640],
                        func=AF.Exp, bias=nm_s[:, col:col + 1],
                        accum_out=rs_s[:, col:col + 1]), inc="S")
                    c.ps_consume("b0", "S", sv)
                    c.ps_consume("b1", "S", sv)
                if ALEVEL < 3:
                    continue
                c.wait(E_V, "S", c.cnt["S"])
                rv = c.emit(E_V, lambda e, h=h: e.reciprocal(
                    out=rr_s[:, h * 4:h * 4 + 4], in_=rs_s[:, h * 4:h * 4 + 4]),
                    inc="V")
                c.emit(E_V, lambda e: e.memset(fence_s[:, 0:1], 0.0), inc="V")
                avbank = (p4, p5)[h % 2]
                c.ps_wait(E_T, f"b{4 + h % 2}")
                for qt in range(4):
                    gslot = qt
                    col = h * 4 + qt
                    nv = c.emit(E_V, lambda e, g=gslot, col=col: e.tensor_scalar_mul(
                        out=attn_s[:, g * 640:(g + 1) * 640],
                        in0=attn_s[:, g * 640:(g + 1) * 640],
                        scalar1=rr_s[:, col:col + 1]), inc="V")
                    if ALEVEL < 4:
                        continue
                    ppT = qt % 2
                    c.wait(E_T, "V", nv)
                    c.wait(E_V, "T", attnT_free[ppT])
                    tev = []
                    for bn in ("b2", "b3", "b6", "b7"):
                        c.ps_wait(E_T, bn)
                    for cc in range(5):
                        slotb = (psD2_a[:, 0:64], psD2_b[:, 0:64],
                                 p6[:, 0:64], p7[:, 0:64],
                                 psD2_a[:, 0:64])[cc]
                        if cc == 4:
                            c.wait(E_T, "V", tev[0])
                        mk = c.emit(E_T, lambda e, b=slotb, g=gslot, cc=cc:
                                    e.transpose(b.bitcast(BF16),
                                                attn_s[:, g * 640 + cc * 128:
                                                       g * 640 + (cc + 1) * 128],
                                                identb_s[:]), inc="T")
                        if cc in (1, 3):
                            c.wait(E_S, "T", mk)
                            c.wait(E_S, "T", attnT_free[ppT])
                            c.emit(E_S, lambda e, b=slotb, ppT=ppT, cc=cc:
                                   e.copy(
                                out=attnT_s[:, ppT * 640 + cc * 128:
                                            ppT * 640 + (cc + 1) * 128],
                                in_=b.bitcast(BF16)), inc="S")
                            continue
                        c.wait(E_V, "T", mk)
                        ev = c.emit(E_V, lambda e, b=slotb, ppT=ppT, cc=cc:
                                    e.tensor_copy(
                            out=attnT_s[:, ppT * 640 + cc * 128:
                                        ppT * 640 + (cc + 1) * 128],
                            in_=b.bitcast(BF16)), inc="V")
                        tev.append(ev)
                    sfence = c.emit(E_S, lambda e: e.copy(
                        out=fence_s[:, 4:5], in_=fence_s[:, 0:1]), inc="S")
                    if DBG4 and l == DBGL and w == 1 and h == 0 and qt == 0:
                        c.wait(E_G, "V", tev[-1])
                        c.dma(E_G, dbg_att[:], attnT_s[:, ppT * 640:(ppT + 1) * 640], sem="DO")
                        c.dma(E_G, dbg_at[:, 0:640], attn_s[:, 0:640], sem="DO")
                        c.dma(E_G, dbg_nm[:, 0:32], nm_s[:], sem="DO")
                        c.dma(E_G, dbg_nm[:, 32:64], rs_s[:], sem="DO")
                        c.wait(E_G, "DO", c.cnt["DO"])
                        for _e in (E_V, E_S, E_T, E_SY):
                            c.wait(_e, "DO", c.cnt["DO"])
                    attn_free[gslot] = tev[-1]
                    for bn in ("b2", "b6"):
                        c.ps_consume(bn, "V", tev[-1])
                    for bn in ("b3", "b7"):
                        c.ps_consume(bn, "S", sfence)
                    if ALEVEL < 5:
                        continue
                    if qt == 3:
                        tfence = c.emit(E_V, lambda e: e.memset(
                            fence_s[:, 2:3], 0.0), inc="V")
                        c.wait(E_T, "V", tfence)
                    else:
                        c.wait(E_T, "V", tev[-1] + 1)
                    c.wait(E_T, "S", sfence)
                    c.wait(E_T, "V", vT_V)
                    for cc in range(5):
                        tm = c.mm(avbank[0:64, qt * 128:(qt + 1) * 128],
                                  vT_s[:, (qt + cc) * WIN + 64 * h:
                                       (qt + cc) * WIN + 64 * h + 64],
                                  attnT_s[:, ppT * 640 + cc * 128:
                                          ppT * 640 + (cc + 1) * 128],
                                  cc == 0, cc == 4, inc="T" if cc == 4 else None)
                    attnT_free[ppT] = tm
                if ALEVEL < 6:
                    continue
                c.wait(E_V, "T", tm)
                c.wait(E_V, "T", oT_rd_T)
                ov = c.emit(E_V, lambda e, b=avbank, hp=hp, hb=hb: e.tensor_copy(
                    out=oT_s[hp:hp + 64, hb * WIN:(hb + 1) * WIN],
                    in_=b[0:64, :]), inc="V")
                c.ps_consume(f"b{4 + h % 2}", "V", ov)
            c.emit(E_V, lambda e: e.memset(fence_s[:, 3:4], 0.0), inc="V")
            oT_V = c.cnt["V"]
            nm_free_S = c.cnt["S"]

            if STOP == "f_attn":
                break
            for i in range(8):
                wb = (wb1_s, wb2_s)[i % 2]
                c.wait(E_SY, "T", wbuf_T[i % 2])
                wld = c.dma(E_SY, bAP(wb, 0, [[1024, 128], [128, 4], [1, 128]]),
                            bAP(wo_d[l], i * 128, [[D, 128], [128 * D, 4], [1, 128]]))
                c.wait(E_T, "DS", wld)
                bank = (p6, p7)[i % 2]
                c.ps_wait(E_T, f"b{6 + i % 2}")
                c.wait(E_T, "V", oT_V)
                for kp in range(4):
                    tm = c.mm(bank[:], wb[:, kp * 128:(kp + 1) * 128],
                              oT_s[:, kp * WIN:(kp + 1) * WIN], kp == 0, kp == 3,
                              inc="T" if kp == 3 else None)
                wbuf_T[i % 2] = tm
                oT_rd_T = tm
                c.wait(E_V, "T", tm)
                rv = c.emit(E_V, lambda e, i=i, b=bank: e.tensor_tensor(
                    out=xw2_s[:, i * (WIN + 1) + 1:i * (WIN + 1) + 1 + WIN],
                    in0=xw2_s[:, i * (WIN + 1) + 1:i * (WIN + 1) + 1 + WIN],
                    in1=b[:], op=OP.add), inc="V")
                c.ps_consume(f"b{6 + i % 2}", "V", rv)
            x2_V = c.cnt["V"]
            if DBG4 and l == DBGL and DBG3A:
                c.wait(E_G, "V", oT_V)
                c.dma(E_G, dbg_oT[:, w * 4 * WIN:(w + 1) * 4 * WIN], oT_s[:], sem="DO")
                c.wait(E_G, "DO", c.cnt["DO"])
                for _e in (E_V, E_S, E_T, E_SY):
                    c.wait(_e, "DO", c.cnt["DO"])
            if DBG3 and l == DBGL and DBG3A:
                c.wait(E_G, "V", x2_V)
                c.dma(E_G, dbg_oT[:, w * 4 * WIN:(w + 1) * 4 * WIN], oT_s[:], sem="DO")
                c.dma(E_G, bAP(dbg_xa, dbg_xa.offset + w * 8 * WIN,
                               [dbg_xa.ap[0], [8 * T // 8, 0]][0:1] + [[WIN, 8], [1, WIN]]),
                      bAP(xw2_s, xw2_s.offset + 1, [xw2_s.ap[0], [WIN + 1, 8], [1, WIN]]),
                      sem="DO")
                c.wait(E_G, "DO", c.cnt["DO"])
                for _e in (E_V, E_S, E_T, E_SY):
                    c.wait(_e, "DO", c.cnt["DO"])

            hs = (w % 2) * 8
            hp_ = ((w + 1) % 2) * 8
            c.emit(E_V, lambda e, hs=hs: e.tensor_copy(
                out=halo2_s[:, hs:hs + 8],
                in_=bAP(xw2_s, xw2_s.offset + WIN, [xw2_s.ap[0], [WIN + 1, 8], [1, 1]])))
            hv = c.emit(E_V, lambda e, hp_=hp_: e.tensor_copy(
                out=bAP(xw2_s, xw2_s.offset, [xw2_s.ap[0], [WIN + 1, 8], [1, 1]]),
                in_=halo2_s[:, hp_:hp_ + 8]), inc="V")

            if STOP == "f_wo":
                break
            lnm = layer_norm(lambda pb: shift_ap(pb),
                             lambda pb: hn_s[:, pb * WIN:(pb + 1) * WIN],
                             8, WIN, sq2_s, hv, ready_T=hn_rd_T,
                             gb=(lng_s[l], lnb_s[l]))

            if STOP == "f_ln":
                break
            if DBG3 and l == DBGL:
                c.wait(E_G, "V", lnm)
                c.dma(E_G, bAP(dbg_hn, dbg_hn.offset + w * 8 * WIN,
                               [dbg_hn.ap[0], [WIN, 8], [1, WIN]]),
                      hn_s[:].rearrange("p (a b) -> p a b", b=WIN), sem="DO")
                c.wait(E_G, "DO", c.cnt["DO"])
                for _e in (E_V, E_S, E_T, E_SY):
                    c.wait(_e, "DO", c.cnt["DO"])
            gelu_free_V = 0
            for it in range(NIT1):
                na = min(128, HID - it * 128)
                wbA, wbB = (wb1_s, wb2_s)
                c.wait(E_SY, "T", wbuf_T[0])
                wldA = c.dma(E_SY, bAP(wbA, 0, [[1024, 128], [128, 8], [1, na]]),
                             bAP(w1_d[l], it * 128,
                                 [[2 * HID, 128], [128 * 2 * HID, 8], [1, na]]))
                c.wait(E_S, "T", wbuf_T[1])
                wldB = c.dma(E_S, bAP(wbB, 0, [[1024, 128], [128, 8], [1, na]]),
                             bAP(w1_d[l], HID + it * 128,
                                 [[2 * HID, 128], [128 * 2 * HID, 8], [1, na]]))
                c.wait(E_T, "DS", wldA)
                c.wait(E_T, "DA", wldB)
                c.wait(E_T, "V", lnm)
                c.ps_wait(E_T, "b4")
                c.ps_wait(E_T, "b5")
                for kp in range(8):
                    tma = c.mm(p4[0:na, :], wbA[:, kp * 128:kp * 128 + na],
                               hn_s[:, kp * WIN:(kp + 1) * WIN], kp == 0, kp == 7,
                               inc="T" if kp == 7 else None)
                for kp in range(8):
                    tmb = c.mm(p5[0:na, :], wbB[:, kp * 128:kp * 128 + na],
                               hn_s[:, kp * WIN:(kp + 1) * WIN], kp == 0, kp == 7,
                               inc="T" if kp == 7 else None)
                wbuf_T[0] = tma
                wbuf_T[1] = tmb
                hn_rd_T = tmb
                c.wait(E_S, "T", tmb)
                c.wait(E_S, "V", gelu_free_V)
                sv = c.emit(E_S, lambda e, it=it, na=na: e.activation(
                    out=sq2_s[0:na, :], in_=p5[0:na, :], func=AF.Gelu,
                    bias=b1g_s[l][0:na, it:it + 1]), inc="S")
                c.ps_consume("b5", "S", sv)
                c.wait(E_V, "S", sv)
                c.wait(E_V, "T", h2_rd_T)
                gv = c.emit(E_V, lambda e, it=it, na=na: e.scalar_tensor_tensor(
                    out=h2_s[0:na, it * WIN:(it + 1) * WIN], in0=p4[0:na, :],
                    scalar=b1a_s[l][0:na, it:it + 1], in1=sq2_s[0:na, :],
                    op0=OP.add, op1=OP.mult), inc="V")
                gelu_free_V = gv
                c.ps_consume("b4", "V", gv)
            h2_V = c.cnt["V"]

            if STOP == "f_w1":
                break
            for i in range(8):
                wb = (w2bufA, w2bufB)[i % 2]
                c.wait(E_G, "T", w2buf_T[i % 2])
                wld = c.dma(E_G, bAP(wb, 0, [[2816, 128], [128, 21], [1, 128]]),
                            bAP(w2_d[l], i * 128, [[D, 128], [128 * D, 21], [1, 128]]))
                wld = c.dma(E_G, bAP(wb, 21 * 128, [[2816, 42], [1, 128]]),
                            bAP(w2_d[l], 21 * 128 * D + i * 128, [[D, 42], [1, 128]]))
                if W2LEVEL < 2:
                    continue
                c.wait(E_T, "DG", wld)
                bank = (p6, p7)[i % 2]
                c.ps_wait(E_T, f"b{6 + i % 2}")
                c.wait(E_T, "V", h2_V)
                for kp in range(NIT1):
                    na = min(128, HID - kp * 128)
                    tm = c.mm(bank[:], wb[0:na, kp * 128:(kp + 1) * 128],
                              h2_s[0:na, kp * WIN:(kp + 1) * WIN],
                              kp == 0, kp == NIT1 - 1,
                              inc="T" if kp == NIT1 - 1 else None)
                w2buf_T[i % 2] = tm
                h2_rd_T = tm
                if W2LEVEL < 3:
                    c.wait(E_S, "T", tm)
                    dmy = c.emit(E_S, lambda e, b=bank: e.copy(
                        out=sq2_s[:], in_=b[:]), inc="S")
                    c.ps_consume(f"b{6 + i % 2}", "S", dmy)
                    continue
                c.wait(E_V, "T", tm)
                rv = c.emit(E_V, lambda e, i=i, b=bank: e.scalar_tensor_tensor(
                    out=xw2_s[:, i * (WIN + 1) + 1:i * (WIN + 1) + 1 + WIN],
                    in0=b[:], scalar=b2_s[l][:, i:i + 1],
                    in1=xw2_s[:, i * (WIN + 1) + 1:i * (WIN + 1) + 1 + WIN],
                    op0=OP.add, op1=OP.add), inc="V")
                c.ps_consume(f"b{6 + i % 2}", "V", rv)
            xout_V = c.cnt["V"]

            kv_rd_T = c.cnt["T"]
            qh_rd_T = c.cnt["T"]

            if not final and W2LEVEL >= 4:
                c.wait(E_G, "V", xout_V)
                sd = c.dma(E_G, bAP(dst_h, 8 + t0,
                                    [[T + 8, 128], [128 * (T + 8), 8], [1, WIN]]),
                           bAP(xw2_s, xw2_s.offset + 1,
                               [xw2_s.ap[0], [WIN + 1, 8], [1, WIN]]))
                c.wait(E_SY, "DG", sd)
                xw2_free = xout_V
            elif not final:
                xw2_free = xout_V if W2LEVEL >= 3 else c.cnt["V"]
            else:
                c.wait(E_SY, "V", c.cnt["V"])
                c.wait(E_SY, "T", c.cnt["T"])
                x1ld = c.dma(E_SY, x3_s[:].rearrange("p (a b) -> p a b", b=WIN),
                             bAP(sa_h, 8 + t0,
                                 [[T + 8, 128], [128 * (T + 8), 8], [1, WIN]]))
                c.wait(E_V, "DS", x1ld)
                av = 0
                for pb in range(8):
                    av = c.emit(E_V, lambda e, pb=pb: e.tensor_tensor(
                        out=x3_s[:, pb * WIN:(pb + 1) * WIN],
                        in0=x3_s[:, pb * WIN:(pb + 1) * WIN],
                        in1=xw2_s[:, pb * (WIN + 1) + 1:pb * (WIN + 1) + 1 + WIN],
                        op=OP.add), inc="V")
                if DBG3 and DBG3B:
                    c.wait(E_G, "V", av)
                    c.dma(E_G, bAP(dbg_hn, dbg_hn.offset + w * 8 * WIN,
                                   [dbg_hn.ap[0], [WIN, 8], [1, WIN]]),
                          x3_s[:].rearrange("p (a b) -> p a b", b=WIN), sem="DO")
                    c.wait(E_G, "DO", c.cnt["DO"])
                    for _e in (E_V, E_S, E_T, E_SY):
                        c.wait(_e, "DO", c.cnt["DO"])
                lnf = layer_norm(lambda pb: x3_s[:, pb * WIN:(pb + 1) * WIN],
                                 lambda pb: x3_s[:, pb * WIN:(pb + 1) * WIN],
                                 8, WIN, sq2_s, av)
                xw2_free = av
                c.ps_wait(E_T, "psD2a")
                c.ps_wait(E_T, "b3")
                c.wait(E_T, "V", lnf)
                for tch in range(4):
                    ob = ot2_s[:, (tch % 2) * D:(tch % 2 + 1) * D]
                    if len(odma) >= 2:
                        c.wait(E_V, "DO", odma[-2])
                    evl = None
                    for pb in range(8):
                        slot = pb % 4
                        bankap = (psD_a[:, 0:128], psD_b[:, 0:128],
                                  psD2_a[:, 0:128], psD2_b[:, 0:128])[slot]
                        if ev4[slot]:
                            c.wait(E_T, "V", ev4[slot])
                        mk = c.emit(E_T, lambda e, b=bankap, pb=pb, tch=tch:
                                    e.transpose(b, x3_s[:, pb * WIN + tch * 128:
                                                        pb * WIN + (tch + 1) * 128]
                                                .bitcast(F32), identf_s[:]),
                                    inc="T")
                        c.wait(E_V, "T", mk)
                        evl = c.emit(E_V, lambda e, b=bankap, ob=ob, pb=pb:
                                     e.tensor_copy(out=ob[:, pb * 128:(pb + 1) * 128],
                                                   in_=b), inc="V")
                        ev4[slot] = evl
                    c.wait(E_G, "V", evl)
                    dm = c.dma(E_G, out_d[t0 + tch * 128:t0 + (tch + 1) * 128, :],
                               ob, sem="DO")
                    odma.append(dm)
                for bn in ("b0", "b1", "b2", "b3"):
                    c.ps_consume(bn, "V", ev4[3])
                hn_rd_T = c.cnt["T"]
                x3_free[w % 2] = lnf
        return

    ffn_layer(0, sa_h, sb_h, final=False)
    if STOP in ("f_qkv", "f_v", "f_attn", "f_wo", "f_ln", "f_w1"):
        return c.build()
    c.barrier()
    if STOP == "ffn0":
        return c.build()
    ffn_layer(1, sb_h, None, final=True)

    c.wait(E_G, "DO", c.cnt["DO"])
    c.wait(E_SY, "DO", c.cnt["DO"])

    nc = c.build()
    return nc


_CACHED = {}


def kernel(**inputs):
    x_full = np.asarray(inputs["L0_attn_x"], dtype=np.float32)
    params = reorg_params(inputs)
    consts = host_constants()
    if "nc" not in _CACHED:
        _CACHED["nc"] = build_program()
    nc = _CACHED["nc"]
    base = {}
    base.update(params)
    base.update(consts)
    in_maps = []
    for cid in range(NCORES):
        m = dict(base)
        m["x"] = np.ascontiguousarray(x_full[cid])
        in_maps.append(m)
    res = run_bass_kernel_spmd(nc, in_maps, core_ids=list(range(NCORES)))
    out = np.stack([np.asarray(r["out"]) for r in res.results], axis=0)
    return out.astype(np.float32)


# version 06 (final)
# speedup vs baseline: 1.1135x; 1.0356x over previous
import numpy as np
import ml_dtypes
import concourse.bass as bass
import concourse.mybir as mybir
from concourse.bass_utils import run_bass_kernel_spmd

F32 = mybir.dt.float32
F32R = mybir.dt.float32r
BF16 = mybir.dt.bfloat16
AF = mybir.ActivationFunctionType
OP = mybir.AluOpType
AX = mybir.AxisListType

D = 1024
E = 2048
NS = 16
RK = 64
T = 2048
TC = 512
NTCH = T // TC
H = 8
DH = 64
INNER = 512
HID = 2730
NIT1 = 22
WIN = 512
NW = T // WIN
NEG = -1.0e30
EPS = 1e-5
NCORES = 8
STOP = None
NOREG = False
ALEVEL = 99
W2LEVEL = 99
DBG = False
DBG2 = False
DBG3 = False
DBG4 = False
DBGL = 0
DBG3A = True
DBG3B = True

ENGINES = ("sync", "tensor", "vector", "scalar", "gpsimd")
E_T, E_V, E_S, E_G, E_SY = "tensor", "vector", "scalar", "gpsimd", "sync"


def bAP(t_ap, offset, dims):
    return bass.AP(tensor=t_ap.tensor, offset=offset,
                   ap=[list(d) for d in dims])


class Ctx:
    def __init__(self):
        self.nc = bass.Bass("TRN2", target_bir_lowering=False, debug=False)
        self.q = {e: [] for e in ENGINES}
        self.cnt = {}
        self.sems = {}
        self._sbuf_stack = []
        self.psg = {}
        self.lastw = {}

    def sem(self, name):
        h = self.nc.semaphore(name).__enter__()
        self.sems[name] = h
        self.cnt[name] = 0

    def emit(self, eng, fn, inc=None, by=1):
        if inc is not None:
            self.cnt[inc] += by
            sem = self.sems[inc]

            def run(e, fn=fn, sem=sem, by=by):
                fn(e).then_inc(sem, by)
            self.q[eng].append(run)
            return self.cnt[inc]
        self.q[eng].append(fn)
        return None

    def wait(self, eng, sem_name, val):
        if val is None or val <= 0:
            return
        key = (eng, sem_name)
        if self.lastw.get(key, 0) >= val:
            return
        self.lastw[key] = val
        sem = self.sems[sem_name]
        self.q[eng].append(lambda e, s=sem, v=val: e.wait_ge(s, v))

    def sbuf(self, name, shape, dt):
        cm = self.nc.sbuf_tensor(name, list(shape), dt)
        h = cm.__enter__()
        self._sbuf_stack.append((name, cm))
        return h.ap() if hasattr(h, "ap") else h

    def sbuf_mark(self):
        return len(self._sbuf_stack)

    def sbuf_free_to(self, mark):
        while len(self._sbuf_stack) > mark:
            _, cm = self._sbuf_stack.pop()
            cm.__exit__(None, None, None)

    def ps_wait(self, eng, name):
        g = self.psg.get(name)
        if g is not None:
            self.wait(eng, g[0], g[1])

    def ps_consume(self, name, sem, val):
        self.psg[name] = (sem, val)

    def dma(self, queue, out, in_, sem=None):
        sem = sem or ("DG" if queue == E_G else ("DA" if queue == E_S else "DS"))
        return self.emit(queue, lambda e, o=out, i=in_: e.dma_start(out=o, in_=i),
                         inc=sem, by=16)

    def mm(self, psum, lhsT, rhs, start, stop, inc=None):
        return self.emit(E_T, lambda e, p=psum, l=lhsT, r=rhs, st=start, sp=stop:
                         e.matmul(p, l, r, start=st, stop=sp), inc=inc)

    def barrier(self):
        marks = {s: self.cnt[s] for s in ("T", "V", "S", "P", "DS", "DG", "DA", "DO")}
        for eng in (E_T, E_V, E_S, E_G, E_SY):
            for s, v in marks.items():
                self.wait(eng, s, v)

    def build(self):
        nc = self.nc
        with nc.Block() as block:
            @block.sync
            def _(e):
                for fn in self.q["sync"]:
                    fn(e)

            @block.tensor
            def _(e):
                for fn in self.q["tensor"]:
                    fn(e)

            @block.vector
            def _(e):
                for fn in self.q["vector"]:
                    fn(e)

            @block.scalar
            def _(e):
                for fn in self.q["scalar"]:
                    fn(e)

            @block.gpsimd
            def _(e):
                for fn in self.q["gpsimd"]:
                    fn(e)
        return nc


def host_constants():
    c = {}
    ident = np.eye(128, dtype=np.float32)
    c["ident_b"] = ident.astype(ml_dtypes.bfloat16)
    c["ident_f"] = ident
    c["onesc_r"] = np.ones((128, 1), dtype=np.float32)
    c["ones1_r"] = np.ones((1, 128), dtype=np.float32)
    a = np.arange(128)
    c["m_ul"] = np.where(a[:, None] < a[None, :], NEG, 0.0).astype(ml_dtypes.bfloat16)
    c["m_ll"] = np.where(a[:, None] > a[None, :], NEG, 0.0).astype(ml_dtypes.bfloat16)
    c["m_full"] = np.full((128, 128), NEG, dtype=ml_dtypes.bfloat16)
    hpat = np.zeros((128, 32), dtype=np.float32)
    for it in range(4):
        for p_ in range(128):
            hpat[p_, 8 * it + 2 * it + (p_ // 64)] = 1.0
    c["hpat"] = hpat
    hm8 = np.zeros((8, 512), dtype=np.float32)
    hm1 = np.zeros((8, 512), dtype=np.float32)
    for it in range(4):
        for iloc in range(128):
            h = 2 * it + (iloc // 64)
            hm8[h, 128 * it + iloc] = 8.0
            hm1[h, 128 * it + iloc] = 1.0
    c["hm8"] = hm8
    c["hm1"] = hm1
    bcp = np.zeros((16, 16 * 128), dtype=np.float32)
    for n in range(16):
        bcp[n, n * 128:(n + 1) * 128] = 1.0
    c["bcp"] = bcp
    c["fconst"] = np.tile(np.array([[EPS, -0.5, 1.0, -1.0]], dtype=np.float32),
                          (128, 1))
    half = np.arange(0, DH, 2, dtype=np.float64)
    invf = 1.0 / 10000.0 ** (half / DH)
    t = np.arange(T, dtype=np.float64)
    freqs = t[:, None] * invf[None, :]
    base = (half + 0.4 * DH) / (1.4 * DH)
    power = (t - T // 2) / float(WIN // 2)
    scale = base[None, :] ** power[:, None]
    cosf = np.concatenate([np.cos(freqs)] * 2, axis=1)
    sinf = np.concatenate([np.sin(freqs)] * 2, axis=1)
    scf = np.concatenate([scale] * 2, axis=1)
    f1q = (cosf * scf).T
    f1k = (cosf / scf).T
    g2q = (sinf * scf).T.copy()
    g2k = (sinf / scf).T.copy()
    g2q[0:32, :] *= -1.0
    g2k[0:32, :] *= -1.0
    rep = lambda m_: np.concatenate([m_, m_], axis=0).astype(np.float32)
    c["rot_f1q"] = rep(f1q)
    c["rot_f1k"] = rep(f1k)
    c["rot_g2q"] = rep(g2q)
    c["rot_g2k"] = rep(g2k)
    return c


def reorg_params(inp):
    d = {}
    d["w_in"] = np.ascontiguousarray(inp["L0_attn_W_in"], dtype=np.float32)
    wx_ = np.asarray(inp["L0_attn_W_x"], dtype=np.float32)
    d["w_xa"] = np.ascontiguousarray(wx_[:, 0:80])
    d["w_xc"] = np.ascontiguousarray(wx_[:, 80:96])
    d["w_dt"] = np.ascontiguousarray(inp["L0_attn_W_dt"], dtype=np.float32)
    d["w_out"] = np.ascontiguousarray(inp["L0_attn_W_out"], dtype=np.float32)
    cw = np.asarray(inp["L0_attn_conv_w"], dtype=np.float32).reshape(E, 4)
    cwl = np.zeros((128, 64), dtype=np.float32)
    for pb in range(16):
        for k in range(4):
            cwl[:, pb * 4 + k] = cw[pb * 128:(pb + 1) * 128, k]
    d["convw_l"] = cwl
    pcol = lambda v, nb: np.ascontiguousarray(
        np.asarray(v, dtype=np.float32).reshape(nb, 128).T)
    d["convb_l"] = pcol(inp["L0_attn_conv_b"], 16)
    d["Dvec_l"] = pcol(inp["L0_attn_D"], 16)
    d["bdt_l"] = pcol(inp["L0_attn_b_dt"], 16)
    d["alog"] = np.asarray(inp["L0_attn_A_log"], dtype=np.float32).reshape(1, NS)
    for l in range(2):
        d[f"wqkv{l}"] = np.ascontiguousarray(inp["L0_ffn_wqkv"][l], dtype=np.float32)
        d[f"wo{l}"] = np.ascontiguousarray(inp["L0_ffn_wo"][l], dtype=np.float32)
        d[f"w1{l}"] = np.ascontiguousarray(inp["L0_ffn_w1"][l], dtype=np.float32)
        d[f"w2{l}"] = np.ascontiguousarray(inp["L0_ffn_w2"][l], dtype=np.float32)
        d[f"qs{l}"] = np.asarray(inp["L0_ffn_q_scale"][l], dtype=np.float32).reshape(1, DH)
        d[f"ks{l}"] = np.asarray(inp["L0_ffn_k_scale"][l], dtype=np.float32).reshape(1, DH)
        d[f"lng{l}"] = pcol(inp["L0_ffn_ln_g"][l], 8)
        d[f"lnb{l}"] = pcol(inp["L0_ffn_ln_b"][l], 8)
        b1 = np.asarray(inp["L0_ffn_b1"][l], dtype=np.float32)
        b1a = np.zeros((128, NIT1), dtype=np.float32)
        b1g = np.zeros((128, NIT1), dtype=np.float32)
        for it in range(NIT1):
            lo = it * 128
            hi = min(lo + 128, HID)
            b1a[0:hi - lo, it] = b1[lo:hi]
            b1g[0:hi - lo, it] = b1[HID + lo:HID + hi]
        d[f"b1a{l}"] = b1a
        d[f"b1g{l}"] = b1g
        d[f"b2{l}"] = pcol(inp["L0_ffn_b2"][l], 8)
    return d


def build_program():
    c = Ctx()
    nc = c.nc

    def din(name, shape, dt):
        return nc.dram_tensor(name, list(shape), dt, kind="ExternalInput").ap()

    x_in = din("x", (T, D), F32)
    w_in = din("w_in", (D, 2 * E), F32R)
    w_xa = din("w_xa", (E, 80), F32R)
    w_xc = din("w_xc", (E, 16), F32R)
    w_dt = din("w_dt", (RK, E), F32R)
    w_out = din("w_out", (E, D), F32R)
    convw_d = din("convw_l", (128, 64), F32)
    convb_d = din("convb_l", (128, 16), F32)
    Dvec_d = din("Dvec_l", (128, 16), F32)
    bdt_d = din("bdt_l", (128, 16), F32)
    alog_d = din("alog", (1, NS), F32R)
    wqkv_d = [din(f"wqkv{l}", (D, 3 * INNER), F32R) for l in range(2)]
    wo_d = [din(f"wo{l}", (INNER, D), F32R) for l in range(2)]
    w1_d = [din(f"w1{l}", (D, 2 * HID), F32R) for l in range(2)]
    w2_d = [din(f"w2{l}", (HID, D), F32R) for l in range(2)]
    qs_d = [din(f"qs{l}", (1, DH), F32R) for l in range(2)]
    ks_d = [din(f"ks{l}", (1, DH), F32R) for l in range(2)]
    lng_d = [din(f"lng{l}", (128, 8), F32) for l in range(2)]
    lnb_d = [din(f"lnb{l}", (128, 8), F32) for l in range(2)]
    b1a_d = [din(f"b1a{l}", (128, NIT1), F32) for l in range(2)]
    b1g_d = [din(f"b1g{l}", (128, NIT1), F32) for l in range(2)]
    b2_d = [din(f"b2{l}", (128, 8), F32) for l in range(2)]
    identb_d = din("ident_b", (128, 128), BF16)
    identf_d = din("ident_f", (128, 128), F32)
    onesc_d = din("onesc_r", (128, 1), F32R)
    ones1_d = din("ones1_r", (1, 128), F32R)
    mul_d = din("m_ul", (128, 128), BF16)
    mll_d = din("m_ll", (128, 128), BF16)
    mfull_d = din("m_full", (128, 128), BF16)
    hpat_d = din("hpat", (128, 32), F32R)
    hm8_d = din("hm8", (8, 512), F32R)
    hm1_d = din("hm1", (8, 512), F32R)
    bcp_d = din("bcp", (16, 16 * 128), F32R)
    fconst_d = din("fconst", (128, 4), F32)
    rotf_d = [din("rot_f1q", (128, T), F32), din("rot_f1k", (128, T), F32)]
    rotg_d = [din("rot_g2q", (128, T), F32), din("rot_g2k", (128, T), F32)]

    out_d = nc.dram_tensor("out", [T, D], F32, kind="ExternalOutput").ap()
    dbg_uc = dbg_dt = None
    dbg_xa = dbg_oT = dbg_hn = None
    dbg_qh = dbg_kh = None
    dbg_at = dbg_nm = None
    if DBG2:
        dbg_uc = nc.dram_tensor("dbg_uc", [128, NTCH * 16 * TC], F32R,
                                kind="ExternalOutput").ap()
        dbg_dt = nc.dram_tensor("dbg_dt", [128, NTCH * 16 * TC], F32,
                                kind="ExternalOutput").ap()
    if DBG3:
        dbg_xa = nc.dram_tensor("dbg_xa", [128, 8 * T], F32R, kind="ExternalOutput").ap()
        dbg_oT = nc.dram_tensor("dbg_oT", [128, 4 * T], F32R, kind="ExternalOutput").ap()
        dbg_hn = nc.dram_tensor("dbg_hn", [128, 8 * T], F32R, kind="ExternalOutput").ap()
        dbg_qh = nc.dram_tensor("dbg_qh", [128, 4 * T], BF16, kind="ExternalOutput").ap()
        dbg_kh = nc.dram_tensor("dbg_kh", [128, 4 * T], BF16, kind="ExternalOutput").ap()
    if DBG4:
        dbg_oT = nc.dram_tensor("dbg_oT", [128, 4 * T], F32R, kind="ExternalOutput").ap()
        dbg_at = nc.dram_tensor("dbg_at", [128, 4 * 640], BF16, kind="ExternalOutput").ap()
        dbg_nm = nc.dram_tensor("dbg_nm", [128, 64], F32, kind="ExternalOutput").ap()
        dbg_vt = nc.dram_tensor("dbg_vt", [128, 8 * WIN], BF16, kind="ExternalOutput").ap()
        dbg_att = nc.dram_tensor("dbg_att", [128, 5 * 128], BF16, kind="ExternalOutput").ap()
    kind_dbg = "ExternalOutput" if DBG else "Internal"
    xct_h = nc.dram_tensor("xct_h", [D, T], F32R, kind=kind_dbg).ap()
    sa_h = nc.dram_tensor("sa_h", [D, T + 8], F32R, kind=kind_dbg).ap()
    sb_h = nc.dram_tensor("sb_h", [D, T + 8], F32R, kind=kind_dbg).ap()

    identb_s = c.sbuf("identb_s", (128, 128), BF16)
    identf_s = c.sbuf("identf_s", (128, 128), F32)
    onesc_s = c.sbuf("onesc_s", (128, 1), F32R)
    ones1_s = c.sbuf("ones1_s", (1, 128), F32R)
    mul_s = c.sbuf("mul_s", (128, 128), BF16)
    mll_s = c.sbuf("mll_s", (128, 128), BF16)
    mfull_s = c.sbuf("mfull_s", (128, 128), BF16)
    hpat_s = c.sbuf("hpat_s", (128, 32), F32R)
    convw_s = c.sbuf("convw_s", (128, 64), F32)
    convb_s = c.sbuf("convb_s", (128, 16), F32)
    Dvec_s = c.sbuf("Dvec_s", (128, 16), F32)
    bdt_s = c.sbuf("bdt_s", (128, 16), F32)
    alog_s = c.sbuf("alog_s", (1, NS), F32R)
    acol_s = c.sbuf("acol_s", (128, NS), F32)
    lng_s = [c.sbuf(f"lng_s{l}", (128, 8), F32) for l in range(2)]
    lnb_s = [c.sbuf(f"lnb_s{l}", (128, 8), F32) for l in range(2)]
    b1a_s = [c.sbuf(f"b1a_s{l}", (128, NIT1), F32) for l in range(2)]
    b1g_s = [c.sbuf(f"b1g_s{l}", (128, NIT1), F32) for l in range(2)]
    b2_s = [c.sbuf(f"b2_s{l}", (128, 8), F32) for l in range(2)]
    qs_s = [c.sbuf(f"qs_s{l}", (1, DH), F32R) for l in range(2)]
    ks_s = [c.sbuf(f"ks_s{l}", (1, DH), F32R) for l in range(2)]
    zero_s = c.sbuf("zero_s", (128, 16), F32)
    fconst_s = c.sbuf("fconst_s", (128, 4), F32)
    fc_eps = fconst_s[:, 0:1]
    fc_mhalf = fconst_s[:, 1:2]
    fc_one = fconst_s[:, 2:3]
    fc_mone = fconst_s[:, 3:4]
    qsr_s = c.sbuf("qsr_s", (1, 512), F32R)
    stat_s = c.sbuf("stat_s", (1, 3 * TC), F32)
    prow_s = c.sbuf("prow_s", (1, TC), F32R)
    qrow_s = c.sbuf("qrow_s", (1, TC), F32R)
    lnsq_s = c.sbuf("lnsq_s", (128, TC), F32R)
    fence_s = c.sbuf("fence_s", (128, 8), F32)

    psD = nc.psum_tensor("psD", [128, 1024], F32).__enter__().ap()
    psD2 = nc.psum_tensor("psD2", [128, 1024], F32).__enter__().ap()
    p4 = nc.psum_tensor("p4", [128, 512], F32).__enter__().ap()
    p5 = nc.psum_tensor("p5", [128, 512], F32).__enter__().ap()
    p6 = nc.psum_tensor("p6", [128, 512], F32).__enter__().ap()
    p7 = nc.psum_tensor("p7", [128, 512], F32).__enter__().ap()
    psD_a, psD_b = psD[:, 0:512], psD[:, 512:1024]
    psD2_a, psD2_b = psD2[:, 0:512], psD2[:, 512:1024]

    for s in ("DS", "DG", "DA", "DO", "T", "V", "S", "P"):
        c.sem(s)

    def layer_norm(src_fn, dst_fn, npb, width, sqF, ready_V, ready_T=0, gb=None):
        sq = lnsq_s
        """src_fn(pb)->AP read, dst_fn(pb)->AP write; normalizes over npb*128
        channels.  Uses p6/p7 for sums, psD_a/psD_b for P/Q bcast.
        Returns final V mark."""
        nd = npb * 128
        c.wait(E_V, "S", c.cnt["S"])
        c.wait(E_S, "T", c.cnt["T"])
        if ready_T:
            c.wait(E_V, "T", ready_T)
        c.ps_wait(E_T, "b6")
        c.ps_wait(E_T, "b7")
        c.wait(E_T, "V", ready_V)
        if ready_T:
            c.wait(E_T, "T", ready_T)
        t1 = t2 = None
        for pb in range(npb):
            t1 = c.mm(p6[0:1, 0:width], onesc_s[:], src_fn(pb),
                      pb == 0, pb == npb - 1, inc="T" if pb == npb - 1 else None)
        t2m = 0
        for pb in range(npb):
            c.wait(E_S, "V", ready_V)
            c.wait(E_S, "T", t2m)
            sv = c.emit(E_S, lambda e, pb=pb: e.activation(
                out=sq[:, 0:width], in_=src_fn(pb),
                func=AF.Square), inc="S")
            c.wait(E_T, "S", sv)
            t2 = c.mm(p7[0:1, 0:width], onesc_s[:], sq[:, 0:width],
                      pb == 0, pb == npb - 1, inc="T")
            t2m = t2
        c.wait(E_V, "T", t2)
        c.emit(E_V, lambda e: e.tensor_scalar_mul(
            out=stat_s[:, 0:width], in0=p6[0:1, 0:width], scalar1=1.0 / nd))
        v = c.emit(E_V, lambda e: e.tensor_scalar_mul(
            out=stat_s[:, width:2 * width], in0=p7[0:1, 0:width],
            scalar1=1.0 / nd), inc="V")
        c.ps_consume("b6", "V", v)
        c.ps_consume("b7", "V", v)
        c.emit(E_V, lambda e: e.scalar_tensor_tensor(
            out=stat_s[:, 2 * width:3 * width], in0=stat_s[:, 0:width],
            scalar=-1.0, in1=stat_s[:, 0:width], op0=OP.mult, op1=OP.mult))
        v = c.emit(E_V, lambda e: e.tensor_tensor(
            out=stat_s[:, 2 * width:3 * width], in0=stat_s[:, width:2 * width],
            in1=stat_s[:, 2 * width:3 * width], op=OP.add), inc="V")
        c.wait(E_S, "V", v)
        c.emit(E_S, lambda e: e.activation(
            out=stat_s[:, 2 * width:3 * width],
            in_=stat_s[:, 2 * width:3 * width], func=AF.Ln, bias=fc_eps[0:1, :]))
        s = c.emit(E_S, lambda e: e.activation(
            out=prow_s[:, 0:width],
            in_=stat_s[:, 2 * width:3 * width], func=AF.Exp, scale=fc_mhalf[0:1, :]),
            inc="S")
        c.wait(E_V, "S", s)
        v = c.emit(E_V, lambda e: e.tensor_tensor(
            out=qrow_s[:, 0:width], in0=stat_s[:, 0:width],
            in1=prow_s[:, 0:width].bitcast(F32), op=OP.mult), inc="V")
        c.wait(E_T, "V", v)
        c.ps_wait(E_T, "b0")
        c.mm(psD_a[:, 0:width], ones1_s[:], prow_s[:, 0:width], True, True)
        c.ps_wait(E_T, "b1")
        tQ = c.mm(psD_b[:, 0:width], ones1_s[:], qrow_s[:, 0:width], True, True,
                  inc="T")
        c.wait(E_V, "T", tQ)
        out_v = 0
        for pb in range(npb):
            c.emit(E_V, lambda e, pb=pb: e.tensor_tensor(
                out=sqF[:, 0:width], in0=src_fn(pb),
                in1=psD_a[:, 0:width], op=OP.mult))
            if gb is None:
                out_v = c.emit(E_V, lambda e, pb=pb: e.tensor_tensor(
                    out=dst_fn(pb), in0=sqF[:, 0:width],
                    in1=psD_b[:, 0:width], op=OP.subtract), inc="V")
            else:
                g_ap, b_ap = gb
                c.emit(E_V, lambda e, pb=pb: e.tensor_tensor(
                    out=sqF[:, 0:width],
                    in0=sqF[:, 0:width],
                    in1=psD_b[:, 0:width], op=OP.subtract))
                out_v = c.emit(E_V, lambda e, pb=pb, g=g_ap, b=b_ap: e.tensor_scalar(
                    out=dst_fn(pb), in0=sqF[:, 0:width],
                    scalar1=g[:, pb:pb + 1], scalar2=b[:, pb:pb + 1],
                    op0=OP.mult, op1=OP.add), inc="V")
        c.ps_consume("b0", "V", out_v)
        c.ps_consume("b1", "V", out_v)
        return out_v

    consts_ld = None
    for (dst, src) in (
        (identb_s, identb_d), (identf_s, identf_d), (onesc_s, onesc_d),
        (ones1_s, ones1_d), (mul_s, mul_d), (mll_s, mll_d), (mfull_s, mfull_d),
        (hpat_s, hpat_d), (convw_s, convw_d), (convb_s, convb_d), (Dvec_s, Dvec_d),
        (bdt_s, bdt_d), (alog_s, alog_d), (fconst_s, fconst_d),
    ):
        consts_ld = c.dma(E_SY, dst[:], src[:])
    for l in range(2):
        for (dst, src) in ((lng_s[l], lng_d[l]), (lnb_s[l], lnb_d[l]),
                           (b1a_s[l], b1a_d[l]), (b1g_s[l], b1g_d[l]),
                           (b2_s[l], b2_d[l]), (qs_s[l], qs_d[l]),
                           (ks_s[l], ks_d[l])):
            consts_ld = c.dma(E_SY, dst[:], src[:])

    c.emit(E_V, lambda e: e.memset(zero_s[:], 0.0), inc="V")
    for eng in (E_T, E_V, E_S, E_G):
        c.wait(eng, "DS", consts_ld)

    c.wait(E_T, "DS", consts_ld)
    c.mm(psD[:, 0:NS], ones1_s[:], alog_s[:], True, True, inc="T")
    c.wait(E_S, "T", c.cnt["T"])
    c.emit(E_S, lambda e: e.activation(out=acol_s[:], in_=psD[:, 0:NS],
                                       func=AF.Exp), inc="S")
    c.wait(E_V, "S", c.cnt["S"])
    v = c.emit(E_V, lambda e: e.tensor_scalar_mul(out=acol_s[:], in0=acol_s[:],
                                                  scalar1=-1.0), inc="V")
    c.ps_consume("b0", "V", v)

    if STOP == "p0a":
        return c.build()
    p0mark = c.sbuf_mark()
    xin_s = c.sbuf("xin_s", (128, 2 * D), F32)
    otile_s = c.sbuf("otile_s", (128, 2 * D), F32R)
    tp_banks = [psD_a, psD_b, psD2_a, psD2_b]
    tp_ev = []
    tp_dma = []
    tp_tmk = {}
    c.ps_wait(E_T, "b0")
    c.ps_wait(E_T, "b1")
    for tt in range(16):
        ibuf = xin_s[:, (tt % 2) * D:(tt % 2 + 1) * D]
        obuf = otile_s[:, (tt % 2) * D:(tt % 2 + 1) * D]
        if tt >= 2:
            c.wait(E_SY, "T", tp_tmk[tt - 2])
        ld = c.dma(E_SY, ibuf, x_in[tt * 128:(tt + 1) * 128, :])
        if tt >= 2:
            c.wait(E_V, "DG", tp_dma[tt - 2])
        c.wait(E_T, "DS", ld)
        for dtile in range(8):
            slot = dtile % 4
            if len(tp_ev) >= 4:
                c.wait(E_T, "V", tp_ev[-4])
            mk = c.emit(E_T, lambda e, b=tp_banks[slot], i=ibuf, d=dtile:
                        e.transpose(b[:, 0:128], i[:, d * 128:(d + 1) * 128],
                                    identf_s[:]), inc="T")
            if dtile == 7:
                tp_tmk[tt] = mk
            c.wait(E_V, "T", mk)
            ev = c.emit(E_V, lambda e, b=tp_banks[slot], o=obuf, d=dtile:
                        e.tensor_copy(out=o[:, d * 128:(d + 1) * 128],
                                      in_=b[:, 0:128]), inc="V")
            tp_ev.append(ev)
        c.wait(E_G, "V", tp_ev[-1])
        dm = c.dma(E_G, bAP(xct_h, tt * 128, [[T, 128], [128 * T, 8], [1, 128]]),
                   obuf.rearrange("p (a b) -> p a b", b=128))
        tp_dma.append(dm)
    for bn in ("b0", "b1", "b2", "b3"):
        c.ps_consume(bn, "V", tp_ev[-1])

    if STOP == "p0b":
        return c.build()
    zv = c.cnt["V"]
    c.wait(E_G, "V", zv)
    c.dma(E_G, bAP(sa_h, 0, [[T + 8, 128], [128 * (T + 8), 8], [1, 8]]),
          bAP(zero_s, zero_s.offset, [zero_s.ap[0], [0, 8], [1, 8]]))
    c.dma(E_G, bAP(sb_h, 0, [[T + 8, 128], [128 * (T + 8), 8], [1, 8]]),
          bAP(zero_s, zero_s.offset, [zero_s.ap[0], [0, 8], [1, 8]]))

    if STOP == "p0c":
        return c.build()
    c.barrier()
    c.sbuf_free_to(p0mark)
    if STOP == "p0":
        return c.build()

    mamba_mark = c.sbuf_mark()
    bcp_s = c.sbuf("bcp_s", (16, 16 * 128), F32R)
    uc_s = c.sbuf("uc_s", (128, 16 * TC), F32R)
    dt_s = c.sbuf("dt_s", (128, 16 * TC), F32)
    sz_s = c.sbuf("sz_s", (128, 16 * TC), BF16)
    bbc_s = c.sbuf("bbc_s", (128, 8 * TC), BF16)
    cbc_s = c.sbuf("cbc_s", (128, 8 * TC), BF16)
    hall_s = c.sbuf("hall_s", (128, 16 * TC), BF16)
    abar_s = c.sbuf("abar_s", (128, 2 * TC), F32)
    bb_s = c.sbuf("bb_s", (128, 2 * TC), BF16)
    carry_s = c.sbuf("carry_s", (128, 16 * NS), F32)
    upre_s = c.sbuf("upre_s", (128, 3 + TC), F32)
    uhalo_s = c.sbuf("uhalo_s", (128, 16 * 3), F32)
    brow_s = c.sbuf("brow_s", (16, TC), F32R)
    crow_s = c.sbuf("crow_s", (16, TC), F32R)
    dtlr_s = c.sbuf("dtlr_s", (64, TC), F32R)
    g_s = c.sbuf("g_s", (128, TC), BF16)
    y2_s = c.sbuf("y2_s", (128, 2 * TC), F32)
    xw_s = c.sbuf("xw_s", (128, 8 * TC), F32R)
    wx_s = c.sbuf("wx_s", (128, 16 * 80), F32R)
    wxc_s = c.sbuf("wxc_s", (128, 16 * 16), F32R)
    wdt_s = c.sbuf("wdt_s", (64, E), F32R)
    wbufA = c.sbuf("wbufA", (128, 2048), F32R)
    wbufB = c.sbuf("wbufB", (128, 2048), F32R)
    sq_s = c.sbuf("sq_m", (128, TC), F32)

    c.emit(E_V, lambda e: e.memset(carry_s[:], 0.0))
    c.emit(E_V, lambda e: e.memset(uhalo_s[:], 0.0), inc="V")
    zinit = c.cnt["V"]

    c.dma(E_SY, bcp_s[:], bcp_d[:])
    wx_ld = None
    for pb in range(16):
        c.dma(E_SY, wx_s[:, pb * 80:(pb + 1) * 80],
              w_xa[pb * 128:(pb + 1) * 128, :])
        wx_ld = c.dma(E_SY, wxc_s[:, pb * 16:(pb + 1) * 16],
                      w_xc[pb * 128:(pb + 1) * 128, :])
    wdt_ld = c.dma(E_SY, wdt_s[:], w_dt[:])

    winbuf_T = [0, 0]
    uc_wr_S = [0] * 16
    abar_free = [0, 0]
    uc_rd_V = [0] * 16
    sz_rd_V = [0] * 16
    dt_rd_S = 0
    dt_rd_V = 0
    upre_rd_V = 0
    xw_free_V = 0
    xw_dma_G = 0
    hall_rd_T = 0
    bc_rd_V = 0
    sa_dma = []

    for tc in range(NTCH):
        t0 = tc * TC
        c.wait(E_SY, "DG", tp_dma[-1])
        if tc > 0:
            c.wait(E_SY, "V", xw_free_V)
            c.wait(E_SY, "DG", xw_dma_G)
        xld = c.dma(E_SY, xw_s[:].rearrange("p (a b) -> p a b", b=TC),
                    bAP(xct_h, t0, [[T, 128], [128 * T, 8], [1, TC]]))
        for it in range(32):
            wb = (wbufA, wbufB)[it % 2]
            c.wait(E_SY, "T", winbuf_T[it % 2])
            wld = c.dma(E_SY, wb[:, 0:1024].rearrange("p (a b) -> p a b", b=128),
                        bAP(w_in, it * 128, [[2 * E, 128], [128 * 2 * E, 8], [1, 128]]))
            c.wait(E_T, "DS", max(wld, xld))
            bank = (p4, p5)[it % 2]
            c.ps_wait(E_T, f"b{4 + it % 2}")
            for kp in range(8):
                tm = c.mm(bank[:], wb[:, kp * 128:(kp + 1) * 128],
                          xw_s[:, kp * TC:(kp + 1) * TC],
                          kp == 0, kp == 7, inc="T" if kp == 7 else None)
            winbuf_T[it % 2] = tm
            if it < 16:
                dblk = it
                c.wait(E_S, "T", tm)
                c.wait(E_S, "V", upre_rd_V)
                sv = c.emit(E_S, lambda e, b=bank: e.copy(
                    out=upre_s[:, 3:3 + TC], in_=b[:]), inc="S")
                c.ps_consume(f"b{4 + it % 2}", "S", sv)
                c.wait(E_V, "V", zinit)
                c.wait(E_V, "S", sv)
                c.emit(E_V, lambda e, d=dblk: e.tensor_copy(
                    out=upre_s[:, 0:3], in_=uhalo_s[:, d * 3:d * 3 + 3]))
                c.emit(E_V, lambda e, d=dblk: e.tensor_copy(
                    out=uhalo_s[:, d * 3:d * 3 + 3], in_=upre_s[:, TC:TC + 3]))
                c.emit(E_V, lambda e, d=dblk: e.tensor_scalar(
                    out=sq_s[:], in0=upre_s[:, 0:TC],
                    scalar1=convw_s[:, d * 4:d * 4 + 1], scalar2=None,
                    op0=OP.mult))
                for k in range(1, 4):
                    v1 = c.emit(E_V, lambda e, k=k, d=dblk: e.scalar_tensor_tensor(
                        out=sq_s[:], in0=upre_s[:, k:k + TC],
                        scalar=convw_s[:, d * 4 + k:d * 4 + k + 1], in1=sq_s[:],
                        op0=OP.mult, op1=OP.add),
                        inc="V" if k == 3 else None)
                upre_rd_V = v1
                c.wait(E_S, "V", v1)
                c.wait(E_S, "V", uc_rd_V[dblk])
                sv = c.emit(E_S, lambda e, d=dblk: e.activation(
                    out=uc_s[:, d * TC:(d + 1) * TC], in_=sq_s[:],
                    func=AF.Silu, bias=convb_s[:, d:d + 1]), inc="S")
                uc_wr_S[dblk] = sv
            else:
                dblk = it - 16
                c.wait(E_S, "T", tm)
                c.wait(E_S, "V", sz_rd_V[dblk])
                sv = c.emit(E_S, lambda e, b=bank, d=dblk: e.activation(
                    out=sz_s[:, d * TC:(d + 1) * TC], in_=b[:], func=AF.Silu),
                    inc="S")
                c.ps_consume(f"b{4 + it % 2}", "S", sv)
        sz_wr_S = c.cnt["S"]

        if STOP == "m_pre":
            break
        c.wait(E_T, "DS", wx_ld)
        c.ps_wait(E_T, "b2")
        c.ps_wait(E_T, "b3")
        for pb in range(16):
            c.wait(E_T, "S", uc_wr_S[pb])
            c.mm(psD2_a[0:80, :], wx_s[:, pb * 80:(pb + 1) * 80],
                 uc_s[:, pb * TC:(pb + 1) * TC], pb == 0, pb == 15)
        for pb in range(16):
            tm = c.mm(psD2_b[0:16, :], wxc_s[:, pb * 16:(pb + 1) * 16],
                      uc_s[:, pb * TC:(pb + 1) * TC], pb == 0, pb == 15,
                      inc="T" if pb == 15 else None)
        c.wait(E_V, "T", tm)
        c.emit(E_V, lambda e: e.tensor_copy(out=dtlr_s[:], in_=psD2_a[0:64, :]))
        c.emit(E_V, lambda e: e.tensor_copy(out=brow_s[:], in_=psD2_a[64:80, :]))
        v2 = c.emit(E_V, lambda e: e.tensor_copy(out=crow_s[:],
                                                 in_=psD2_b[0:16, :]), inc="V")
        c.ps_consume("b2", "V", v2)
        c.ps_consume("b3", "V", v2)
        xdbl_V = v2

        c.wait(E_T, "DS", wdt_ld)
        c.wait(E_T, "V", xdbl_V)
        dt_wr_V = [0] * 16
        dt_sq_V = 0
        for dblk in range(16):
            bank = (p6, p7)[dblk % 2]
            c.ps_wait(E_T, f"b{6 + dblk % 2}")
            tm = c.mm(bank[:], wdt_s[:, dblk * 128:(dblk + 1) * 128],
                      dtlr_s[:], True, True, inc="T")
            c.wait(E_S, "T", tm)
            c.wait(E_S, "V", dt_rd_V)
            c.wait(E_S, "V", dt_sq_V)
            c.emit(E_S, lambda e, b=bank, d=dblk: e.activation(
                out=sq_s[:], in_=b[:], func=AF.Abs, bias=bdt_s[:, d:d + 1]))
            c.emit(E_S, lambda e: e.activation(
                out=sq_s[:], in_=sq_s[:], func=AF.Exp, scale=fc_mone))
            s3 = c.emit(E_S, lambda e: e.activation(
                out=sq_s[:], in_=sq_s[:], func=AF.Ln, bias=fc_one), inc="S")
            c.wait(E_V, "S", s3)
            v1 = c.emit(E_V, lambda e, b=bank, d=dblk: e.tensor_scalar(
                out=y2_s[:, 0:TC], in0=b[:], scalar1=bdt_s[:, d:d + 1],
                scalar2=0.0, op0=OP.add, op1=OP.max), inc="V")
            dt_wr_V[dblk] = c.emit(E_V, lambda e, d=dblk: e.tensor_tensor(
                out=dt_s[:, d * TC:(d + 1) * TC], in0=y2_s[:, 0:TC],
                in1=sq_s[:], op=OP.add), inc="V")
            dt_sq_V = dt_wr_V[dblk]
            c.ps_consume(f"b{6 + dblk % 2}", "V", v1)

        if DBG2:
            c.wait(E_G, "S", c.cnt["S"])
            c.wait(E_G, "V", c.cnt["V"])
            c.dma(E_G, dbg_uc[:, tc * 16 * TC:(tc + 1) * 16 * TC], uc_s[:], sem="DO")
            c.dma(E_G, dbg_dt[:, tc * 16 * TC:(tc + 1) * 16 * TC], dt_s[:], sem="DO")
            c.wait(E_G, "DO", c.cnt["DO"])
        if STOP == "m_dt_tap" and tc == NTCH - 1:
            break
        if STOP == "m_dt":
            break
        y3_wr_V = [0] * 16
        for half in range(2):
            for j, n in enumerate(range(8 * half, 8 * half + 8)):
                for (rows_s, dstbuf) in ((brow_s, bbc_s), (crow_s, cbc_s)):
                    c.ps_wait(E_T, "b3")
                    c.wait(E_T, "V", max(xdbl_V, bc_rd_V))
                    tm = c.mm(psD2_b[:], bcp_s[:, n * 128:(n + 1) * 128],
                              rows_s[:], True, True, inc="T")
                    c.wait(E_V, "T", tm)
                    ev = c.emit(E_V, lambda e, d=dstbuf, j=j: e.tensor_copy(
                        out=d[:, j * TC:(j + 1) * TC], in_=psD2_b[:]), inc="V")
                    c.ps_consume("b3", "V", ev)

            for dblk in range(16):
                c.wait(E_V, "S", uc_wr_S[dblk])
                gv = c.emit(E_V, lambda e, d=dblk: e.tensor_tensor(
                    out=g_s[:], in0=dt_s[:, d * TC:(d + 1) * TC],
                    in1=uc_s[:, d * TC:(d + 1) * TC], op=OP.mult), inc="V")
                dt_rd_V = gv
                ybank = (p4, p5)[dblk % 2]
                if half == 0:
                    c.ps_wait(E_T, f"b{4 + dblk % 2}")
                scan_mk = [0] * 8
                for j, n in enumerate(range(8 * half, 8 * half + 8)):
                    pp = j % 2
                    c.wait(E_S, "V", abar_free[pp])
                    se = c.emit(E_S, lambda e, d=dblk, n=n, pp=pp: e.activation(
                        out=abar_s[:, pp * TC:(pp + 1) * TC],
                        in_=dt_s[:, d * TC:(d + 1) * TC], func=AF.Exp,
                        scale=acol_s[:, n:n + 1]), inc="S")
                    bv = c.emit(E_V, lambda e, j=j, pp=pp: e.tensor_tensor(
                        out=bb_s[:, pp * TC:(pp + 1) * TC], in0=g_s[:],
                        in1=bbc_s[:, j * TC:(j + 1) * TC], op=OP.mult))
                    c.wait(E_V, "S", se)
                    c.wait(E_V, "T", hall_rd_T)
                    init = carry_s[:, dblk * NS + n:dblk * NS + n + 1] \
                        if tc > 0 else 0.0
                    sv = c.emit(E_V, lambda e, n=n, pp=pp, init=init:
                                e.tensor_tensor_scan(
                                    out=hall_s[:, n * TC:(n + 1) * TC],
                                    data0=abar_s[:, pp * TC:(pp + 1) * TC],
                                    data1=bb_s[:, pp * TC:(pp + 1) * TC],
                                    initial=init, op0=OP.mult, op1=OP.add),
                                inc="V")
                    scan_mk[j] = sv
                    abar_free[pp] = sv
                if tc < NTCH - 1:
                    n0 = 8 * half
                    c.emit(E_V, lambda e, d=dblk, n0=n0: e.tensor_copy(
                        out=carry_s[:, d * NS + n0:d * NS + n0 + 8],
                        in_=bAP(hall_s, hall_s.offset + n0 * TC + TC - 1,
                                [hall_s.ap[0], [TC, 8]])))
                hvs = []
                for j, n in enumerate(range(8 * half, 8 * half + 8)):
                    hv = c.emit(E_V, lambda e, n=n, j=j: e.tensor_tensor(
                        out=hall_s[:, n * TC:(n + 1) * TC],
                        in0=hall_s[:, n * TC:(n + 1) * TC],
                        in1=cbc_s[:, j * TC:(j + 1) * TC], op=OP.mult), inc="V")
                    hvs.append(hv)
                fence = c.emit(E_V, lambda e: e.memset(fence_s[:, 1:2], 0.0),
                               inc="V")
                for j, n in enumerate(range(8 * half, 8 * half + 8)):
                    c.wait(E_T, "V", min(hvs[j] + 1, fence))
                    tm = c.mm(ybank[:], identb_s[:],
                              hall_s[:, n * TC:(n + 1) * TC],
                              half == 0 and j == 0, half == 1 and j == 7,
                              inc="T")
                if half == 1:
                    hall_rd_T = tm
                    c.wait(E_V, "T", tm)
                    c.wait(E_V, "S", sz_wr_S)
                    c.emit(E_V, lambda e, d=dblk, b=ybank: e.scalar_tensor_tensor(
                        out=y2_s[:, (d % 2) * TC:(d % 2 + 1) * TC],
                        in0=uc_s[:, d * TC:(d + 1) * TC],
                        scalar=Dvec_s[:, d:d + 1], in1=b[:],
                        op0=OP.mult, op1=OP.add))
                    v2 = c.emit(E_V, lambda e, d=dblk: e.tensor_tensor(
                        out=uc_s[:, d * TC:(d + 1) * TC],
                        in0=y2_s[:, (d % 2) * TC:(d % 2 + 1) * TC],
                        in1=sz_s[:, d * TC:(d + 1) * TC], op=OP.mult), inc="V")
                    c.ps_consume(f"b{4 + dblk % 2}", "V", v2)
                    y3_wr_V[dblk] = v2
                    uc_rd_V[dblk] = v2
                    sz_rd_V[dblk] = v2
            bc_rd_V = c.cnt["V"]
        dt_rd_S = c.cnt["S"]
        dt_rd_V = c.cnt["V"]

        if STOP == "m_scan":
            break
        if STOP == "m_loop":
            continue
        for i in range(8):
            wb = (wbufA, wbufB)[i % 2]
            c.wait(E_SY, "T", winbuf_T[i % 2])
            wld = c.dma(E_SY, wb[:].rearrange("p (a b) -> p a b", b=128),
                        bAP(w_out, i * 128, [[D, 128], [128 * D, 16], [1, 128]]))
            c.wait(E_T, "DS", wld)
            bank = (p6, p7)[i % 2]
            c.ps_wait(E_T, f"b{6 + i % 2}")
            c.wait(E_T, "V", max(y3_wr_V))
            for kp in range(16):
                tm = c.mm(bank[:], wb[:, kp * 128:(kp + 1) * 128],
                          uc_s[:, kp * TC:(kp + 1) * TC], kp == 0, kp == 15,
                          inc="T" if kp == 15 else None)
            winbuf_T[i % 2] = tm
            c.wait(E_V, "T", tm)
            rv = c.emit(E_V, lambda e, i=i, b=bank: e.tensor_tensor(
                out=xw_s[:, i * TC:(i + 1) * TC],
                in0=xw_s[:, i * TC:(i + 1) * TC], in1=b[:], op=OP.add), inc="V")
            c.ps_consume(f"b{6 + i % 2}", "V", rv)
        x1pre_V = c.cnt["V"]

        if STOP == "m_wout":
            break
        lnm = layer_norm(lambda pb: xw_s[:, pb * TC:(pb + 1) * TC],
                         lambda pb: xw_s[:, pb * TC:(pb + 1) * TC],
                         8, TC, sq_s, x1pre_V)
        c.wait(E_G, "V", lnm)
        sd = c.dma(E_G, bAP(sa_h, 8 + t0, [[T + 8, 128], [128 * (T + 8), 8], [1, TC]]),
                   xw_s[:].rearrange("p (a b) -> p a b", b=TC))
        sa_dma.append(sd)
        xw_free_V = lnm
        xw_dma_G = sd

    if STOP in ("m_pre", "m_dt", "m_dt_tap", "m_scan", "m_loop", "m_wout", "mamba"):
        return c.build()
    c.barrier()
    if STOP == "mamba2":
        return c.build()
    c.sbuf_free_to(mamba_mark)

    xw2_s = c.sbuf("xw2_s", (128, 8 * (WIN + 1)), F32R)
    qraw_s = c.sbuf("qraw_s", (128, 4 * WIN), F32)
    qh_s = c.sbuf("qh_s", (128, 4 * WIN), BF16)
    kpair_s = c.sbuf("kpair_s", (128, 4 * 2 * WIN), BF16)
    vcur_s = c.sbuf("vcur_s", (128, 4 * WIN), BF16)
    vT_s = c.sbuf("vT_s", (128, 8 * WIN), BF16)
    attn_s = c.sbuf("attn_s", (128, 4 * 640), BF16)
    attnT_s = c.sbuf("attnT_s", (128, 2 * 640), BF16)
    oT_s = c.sbuf("oT_s", (128, 4 * WIN), F32R)
    hn_s = c.sbuf("hn_s", (128, 8 * WIN), F32R)
    h2_s = c.sbuf("h2_s", (128, NIT1 * WIN), BF16)
    map_s = c.sbuf("map_s", (128, 2 * WIN), F32)
    t1_s = c.sbuf("t1_s", (128, 2 * WIN), F32)
    rn_s = c.sbuf("rn_s", (8, WIN), F32R)
    rn2_s = c.sbuf("rn2_s", (8, WIN), F32)
    nm_s = c.sbuf("nm_s", (128, 32), F32)
    rs_s = c.sbuf("rs_s", (128, 32), F32)
    rr_s = c.sbuf("rr_s", (128, 32), F32)
    halo2_s = c.sbuf("halo2_s", (128, 16), F32)
    rotf_s = c.sbuf("rotf_s", (128, WIN), F32)
    rotg_s = c.sbuf("rotg_s", (128, WIN), F32)
    sq2_s = c.sbuf("sq2_s", (128, WIN), F32)
    wb1_s = c.sbuf("wb1_s", (128, 1024), F32R)
    wb2_s = c.sbuf("wb2_s", (128, 1024), F32R)
    w2bufA = c.sbuf("w2bufA", (128, 2816), BF16)
    w2bufB = c.sbuf("w2bufB", (128, 2816), BF16)
    hm8_s = c.sbuf("hm8_s", (8, 512), F32R)
    hm1_s = c.sbuf("hm1_s", (8, 512), F32R)
    qpat_s = [c.sbuf(f"qpat_s{l}", (8, 512), F32R) for l in range(2)]
    kpat_s = [c.sbuf(f"kpat_s{l}", (8, 512), F32R) for l in range(2)]
    qpatS_s = [c.sbuf(f"qpatS_s{l}", (8, 512), F32R) for l in range(2)]
    kpatS_s = [c.sbuf(f"kpatS_s{l}", (8, 512), F32R) for l in range(2)]
    x3_s = hn_s
    ot2_s = qraw_s

    pld = c.dma(E_SY, hm8_s[:], hm8_d[:])
    pld = c.dma(E_SY, hm1_s[:], hm1_d[:])
    c.wait(E_V, "DS", pld)
    for l in range(2):
        for (src_, dstN, dstS, hm) in (
            (qs_s[l], qpat_s[l], qpatS_s[l], hm8_s),
            (ks_s[l], kpat_s[l], kpatS_s[l], hm1_s),
        ):
            for swapped, dst in ((False, dstN), (True, dstS)):
                if not swapped:
                    vv = c.emit(E_V, lambda e, s=src_: e.tensor_copy(
                        out=qsr_s[:].rearrange("p (a b) -> p a b", b=DH),
                        in_=bAP(s, s.offset, [s.ap[0], [0, 8], [1, DH]])),
                        inc="V")
                else:
                    c.emit(E_V, lambda e, s=src_: e.tensor_copy(
                        out=bAP(qsr_s, qsr_s.offset, [qsr_s.ap[0], [64, 8], [1, 32]]),
                        in_=bAP(s, s.offset + 32, [s.ap[0], [0, 8], [1, 32]])))
                    vv = c.emit(E_V, lambda e, s=src_: e.tensor_copy(
                        out=bAP(qsr_s, qsr_s.offset + 32, [qsr_s.ap[0], [64, 8], [1, 32]]),
                        in_=bAP(s, s.offset, [s.ap[0], [0, 8], [1, 32]])),
                        inc="V")
                c.wait(E_T, "V", vv)
                c.ps_wait(E_T, "b0")
                tv = c.mm(psD_a[0:8, :], ones1_s[:, 0:8], qsr_s[:], True, True,
                          inc="T")
                c.wait(E_V, "T", tv)
                vv = c.emit(E_V, lambda e, d=dst, h=hm: e.tensor_tensor(
                    out=d[:], in0=psD_a[0:8, :], in1=h[:], op=OP.mult), inc="V")
                c.ps_consume("b0", "V", vv)

    out_dma = []

    def ffn_layer(l, src_h, dst_h, final):
        nonlocal out_dma
        kv_init = c.emit(E_V, lambda e: e.memset(kpair_s[:], 0.0))
        kv_init = c.emit(E_V, lambda e: e.memset(vT_s[:], 0.0))
        kv_init = c.emit(E_V, lambda e: e.memset(halo2_s[:], 0.0), inc="V")

        wbuf_T = [0, 0]
        w2buf_T = [0, 0]
        ev4 = [0, 0, 0, 0]
        odma = []
        xw2_free = kv_init
        qraw_free_V = 0
        qraw_free_S = 0
        vT_T = 0
        rot_rd_V = 0
        nm_free_S = 0
        qh_rd_T = 0
        kv_rd_T = 0
        attn_free = [0, 0, 0, 0]
        attnT_free = [0, 0]
        oT_rd_T = 0
        hn_rd_T = 0
        h2_rd_T = 0
        x3_free = [0, 0]

        for w in range(NW):
            t0 = w * WIN
            c.wait(E_SY, "V", xw2_free)
            if final and w > 0:
                c.wait(E_V, "DO", c.cnt["DO"])
            xld = c.dma(E_SY, xw2_s[:].rearrange("p (a b) -> p a b", b=WIN + 1),
                        bAP(src_h, t0 + 7, [[T + 8, 128], [128 * (T + 8), 8], [1, WIN + 1]]))
            shift_ap = lambda pb, w_=0: (
                xw2_s[:, pb * (WIN + 1) + 1: pb * (WIN + 1) + 1 + WIN] if pb < 4
                else xw2_s[:, (pb - 0) * (WIN + 1): (pb) * (WIN + 1) + WIN])

            c.wait(E_V, "T", kv_rd_T)
            for pb in range(4):
                c.emit(E_V, lambda e, pb=pb: e.tensor_copy(
                    out=kpair_s[:, pb * 1024:pb * 1024 + WIN],
                    in_=kpair_s[:, pb * 1024 + WIN:pb * 1024 + 2 * WIN]))
            kvs = c.emit(E_V, lambda e: e.tensor_copy(
                out=vT_s[:, 0:4 * WIN], in_=vT_s[:, 4 * WIN:8 * WIN]), inc="V")

            for part, patN, patS, rf_d, rg_d in (
                (0, qpat_s[l], qpatS_s[l], rotf_d[0], rotg_d[0]),
                (1, kpat_s[l], kpatS_s[l], rotf_d[1], rotg_d[1]),
            ):
                c.wait(E_SY, "V", rot_rd_V)
                rld1 = c.dma(E_SY, rotf_s[:], rf_d[:, t0:t0 + WIN])
                rld2 = c.dma(E_SY, rotg_s[:], rg_d[:, t0:t0 + WIN])
                c.ps_wait(E_T, "b6")
                tnq = 0
                for iloc in range(4):
                    it = part * 4 + iloc
                    wb = (wb1_s, wb2_s)[iloc % 2]
                    c.wait(E_SY, "T", wbuf_T[iloc % 2])
                    wld = c.dma(E_SY,
                                bAP(wb, 0, [[1024, 128], [128, 8], [1, 128]]),
                                bAP(wqkv_d[l], it * 128,
                                    [[3 * INNER, 128], [128 * 3 * INNER, 8], [1, 128]]))
                    c.wait(E_T, "DS", max(wld, xld))
                    bank = (p4, p5)[iloc % 2]
                    c.ps_wait(E_T, f"b{4 + iloc % 2}")
                    for kp in range(8):
                        tm = c.mm(bank[:], wb[:, kp * 128:(kp + 1) * 128],
                                  shift_ap(kp), kp == 0, kp == 7,
                                  inc="T" if kp == 7 else None)
                    wbuf_T[iloc % 2] = tm
                    c.wait(E_V, "V", qraw_free_V)
                    c.wait(E_V, "S", qraw_free_S)
                    c.wait(E_V, "T", tm)
                    ev = c.emit(E_V, lambda e, b=bank, iloc=iloc: e.tensor_copy(
                        out=qraw_s[:, iloc * WIN:(iloc + 1) * WIN], in_=b[:]),
                        inc="V")
                    c.ps_consume(f"b{4 + iloc % 2}", "V", ev)
                    c.wait(E_S, "V", ev)
                    c.wait(E_S, "T", tnq)
                    sv = c.emit(E_S, lambda e, iloc=iloc: e.activation(
                        out=lnsq_s[:, 0:WIN],
                        in_=qraw_s[:, iloc * WIN:(iloc + 1) * WIN],
                        func=AF.Square), inc="S")
                    c.wait(E_T, "S", sv)
                    tnq = c.mm(p6[0:8, :], hpat_s[:, iloc * 8:(iloc + 1) * 8],
                               lnsq_s[:, 0:WIN], iloc == 0, iloc == 3,
                               inc="T" if iloc == 3 else None)
                qkv_ev_V = c.cnt["V"]
                qraw_free_S = c.cnt["S"]
                c.wait(E_V, "T", tnq)
                c.wait(E_V, "S", c.cnt["S"])
                vv = c.emit(E_V, lambda e: e.tensor_scalar_max(
                    out=rn2_s[:], in0=p6[0:8, :], scalar1=1e-24),
                    inc="V")
                c.ps_consume("b6", "V", vv)
                c.wait(E_S, "V", vv)
                c.emit(E_S, lambda e: e.activation(
                    out=rn2_s[:], in_=rn2_s[:], func=AF.Ln))
                sv = c.emit(E_S, lambda e: e.activation(
                    out=rn_s[:], in_=rn2_s[:], func=AF.Exp,
                    scale=fc_mhalf[0:8, :]), inc="S")
                c.wait(E_T, "S", sv)
                c.wait(E_T, "DS", max(rld1, rld2))
                for iloc in range(4):
                    it = part * 4 + iloc
                    wb = (wb1_s, wb2_s)[iloc % 2]
                    c.wait(E_SY, "T", wbuf_T[iloc % 2])
                    for (oofs, iofs) in ((0, 32), (32, 0), (64, 96), (96, 64)):
                        wldS = c.dma(E_SY,
                                     bAP(wb, oofs, [[1024, 128], [128, 8], [1, 32]]),
                                     bAP(wqkv_d[l], it * 128 + iofs,
                                         [[3 * INNER, 128],
                                          [128 * 3 * INNER, 8], [1, 32]]))
                    c.wait(E_T, "DS", wldS)
                    c.ps_wait(E_T, "b6")
                    for kp in range(8):
                        tsw = c.mm(p6[:], wb[:, kp * 128:(kp + 1) * 128],
                                   shift_ap(kp), kp == 0, kp == 7,
                                   inc="T" if kp == 7 else None)
                    wbuf_T[iloc % 2] = tsw
                    c.ps_wait(E_T, "b7")
                    tS1 = c.mm(p7[:], patN[:, iloc * 128:(iloc + 1) * 128],
                               rn_s[:], True, True, inc="T")
                    c.wait(E_V, "T", tS1)
                    mv1 = c.emit(E_V, lambda e: e.tensor_tensor(
                        out=map_s[:, 0:WIN], in0=p7[:], in1=rotf_s[:],
                        op=OP.mult), inc="V")
                    c.ps_consume("b7", "V", mv1)
                    c.ps_wait(E_T, "b7")
                    tS2 = c.mm(p7[:], patS[:, iloc * 128:(iloc + 1) * 128],
                               rn_s[:], True, True, inc="T")
                    c.wait(E_V, "T", tS2)
                    mv2 = c.emit(E_V, lambda e: e.tensor_tensor(
                        out=map_s[:, WIN:2 * WIN], in0=p7[:], in1=rotg_s[:],
                        op=OP.mult), inc="V")
                    c.ps_consume("b7", "V", mv2)
                    c.emit(E_V, lambda e, iloc=iloc: e.tensor_tensor(
                        out=t1_s[:, 0:WIN],
                        in0=qraw_s[:, iloc * WIN:(iloc + 1) * WIN],
                        in1=map_s[:, 0:WIN], op=OP.mult))
                    c.wait(E_V, "T", tsw)
                    c.emit(E_V, lambda e: e.tensor_tensor(
                        out=t1_s[:, WIN:2 * WIN], in0=p6[:],
                        in1=map_s[:, WIN:2 * WIN], op=OP.mult))
                    if part == 0:
                        dst = qh_s[:, iloc * WIN:(iloc + 1) * WIN]
                    else:
                        dst = kpair_s[:, iloc * 1024 + WIN:iloc * 1024 + 2 * WIN]
                    c.wait(E_V, "T", qh_rd_T if part == 0 else kv_rd_T)
                    qv = c.emit(E_V, lambda e, d=dst: e.tensor_tensor(
                        out=d, in0=t1_s[:, 0:WIN], in1=t1_s[:, WIN:2 * WIN],
                        op=OP.add), inc="V")
                    c.ps_consume("b6", "V", qv)
                qraw_free_V = c.cnt["V"]
                rot_rd_V = c.cnt["V"]
            qk_done_V = c.cnt["V"]

            if STOP == "f_qkv":
                break
            if DBG3 and l == DBGL:
                c.wait(E_G, "V", qk_done_V)
                c.dma(E_G, dbg_qh[:, w * 4 * WIN:(w + 1) * 4 * WIN], qh_s[:], sem="DO")
                c.dma(E_G, bAP(dbg_kh, dbg_kh.offset + w * 4 * WIN,
                               [dbg_kh.ap[0], [WIN, 4], [1, WIN]]),
                      bAP(kpair_s, kpair_s.offset + WIN,
                          [kpair_s.ap[0], [1024, 4], [1, WIN]]), sem="DO")
                c.wait(E_G, "DO", c.cnt["DO"])
                for _e in (E_V, E_S, E_T, E_SY):
                    c.wait(_e, "DO", c.cnt["DO"])
            for iloc in range(4):
                it = 8 + iloc
                wb = (wb1_s, wb2_s)[iloc % 2]
                c.wait(E_SY, "T", wbuf_T[iloc % 2])
                wld = c.dma(E_SY, bAP(wb, 0, [[1024, 128], [128, 8], [1, 128]]),
                            bAP(wqkv_d[l], it * 128,
                                [[3 * INNER, 128], [128 * 3 * INNER, 8], [1, 128]]))
                c.wait(E_T, "DS", max(wld, xld))
                bank = (p4, p5)[iloc % 2]
                c.ps_wait(E_T, f"b{4 + iloc % 2}")
                for kp in range(8):
                    tm = c.mm(bank[:], wb[:, kp * 128:(kp + 1) * 128],
                              shift_ap(kp), kp == 0, kp == 7,
                              inc="T" if kp == 7 else None)
                wbuf_T[iloc % 2] = tm
                c.wait(E_S, "T", tm)
                c.wait(E_S, "T", vT_T)
                c.wait(E_S, "V", kvs)
                sv = c.emit(E_S, lambda e, b=bank, iloc=iloc: e.copy(
                    out=vcur_s[:, iloc * WIN:(iloc + 1) * WIN], in_=b[:]), inc="S")
                c.ps_consume(f"b{4 + iloc % 2}", "S", sv)
            vcur_S = c.cnt["S"]
            c.wait(E_T, "S", vcur_S)
            for bn in ("b2", "b3", "b6", "b7"):
                c.ps_wait(E_T, bn)
            tp_ev2 = []
            for tch in range(4):
                for pb in range(4):
                    slot = (tch * 4 + pb) % 4
                    bankap = (psD2_a[:, 0:64], psD2_b[:, 0:64],
                              p6[:, 0:64], p7[:, 0:64])[slot]
                    if len(tp_ev2) >= 4:
                        c.wait(E_T, "V", tp_ev2[-4])
                    mk = c.emit(E_T, lambda e, b=bankap, tch=tch, pb=pb:
                                e.transpose(b.bitcast(BF16),
                                            vcur_s[:, pb * WIN + tch * 128:
                                                   pb * WIN + (tch + 1) * 128],
                                            identb_s[:]), inc="T")
                    c.wait(E_V, "T", mk)
                    ev = c.emit(E_V, lambda e, b=bankap, tch=tch, pb=pb:
                                e.tensor_copy(
                        out=vT_s[:, (4 + tch) * WIN + pb * 128:
                                 (4 + tch) * WIN + (pb + 1) * 128],
                        in_=b.bitcast(BF16)), inc="V")
                    tp_ev2.append(ev)
            vT_V = c.cnt["V"]
            vT_T = c.cnt["T"]
            for bn in ("b2", "b3", "b6", "b7"):
                c.ps_consume(bn, "V", vT_V)

            if STOP == "f_v":
                break
            if DBG4 and l == DBGL and w == 1:
                c.wait(E_G, "V", vT_V)
                c.dma(E_G, dbg_vt[:], vT_s[:], sem="DO")
                c.wait(E_G, "DO", c.cnt["DO"])
                for _e in (E_V, E_S, E_T, E_SY):
                    c.wait(_e, "DO", c.cnt["DO"])
            for h in range(H):
                hp = (h % 2) * 64
                hb = (h // 2)
                for qt in range(4):
                    gslot = qt
                    c.ps_wait(E_T, "b0")
                    c.ps_wait(E_T, "b1")
                    c.wait(E_T, "V", qk_done_V)
                    c.wait(E_T, "V", attn_free[gslot])
                    lq = qh_s[hp:hp + 64, hb * WIN + qt * 128:hb * WIN + (qt + 1) * 128]
                    rk = kpair_s[hp:hp + 64, hb * 1024 + qt * 128:hb * 1024 + qt * 128 + 640]
                    nfull = max(0, (512 - 128 * qt) // 128) if w == 0 else 0
                    ops0 = [(psD[:, 0:512], lq,
                             kpair_s[hp:hp + 64,
                                     hb * 1024 + qt * 128:hb * 1024 + qt * 128 + 512]),
                            (psD[:, 0:128], mul_s[:], identb_s[:])]
                    for ch in range(min(nfull, 4)):
                        ops0.append((psD[:, ch * 128:(ch + 1) * 128], mfull_s[:],
                                     identb_s[:]))
                    for oi, (po, la, ra) in enumerate(ops0):
                        c.mm(po, la, ra, oi == 0, oi == len(ops0) - 1)
                    ops1 = [(psD[:, 512:640], lq,
                             kpair_s[hp:hp + 64,
                                     hb * 1024 + qt * 128 + 512:
                                     hb * 1024 + qt * 128 + 640]),
                            (psD[:, 512:640], mll_s[:], identb_s[:])]
                    if nfull > 4:
                        ops1.append((psD[:, 512:640], mfull_s[:], identb_s[:]))
                    tm = None
                    for oi, (po, la, ra) in enumerate(ops1):
                        tm = c.mm(po, la, ra, oi == 0, oi == len(ops1) - 1,
                                  inc="T" if oi == len(ops1) - 1 else None)
                    col = h * 4 + qt
                    if ALEVEL < 1:
                        continue
                    c.wait(E_V, "T", tm)
                    c.wait(E_V, "S", nm_free_S)
                    vv = c.emit(E_V, lambda e, col=col: e.tensor_reduce(
                        out=nm_s[:, col:col + 1].rearrange("p (a o) -> p a o", o=1),
                        in_=psD[:, 0:640].rearrange("p (a o) -> p a o", o=640),
                        axis=AX.X, op=OP.max, negate=True), inc="V")
                    if ALEVEL < 2:
                        c.ps_consume("b0", "V", vv)
                        c.ps_consume("b1", "V", vv)
                        continue
                    c.wait(E_S, "V", vv)
                    sv = c.emit(E_S, lambda e, col=col, g=gslot: e.activation(
                        out=attn_s[:, g * 640:(g + 1) * 640], in_=psD[:, 0:640],
                        func=AF.Exp, bias=nm_s[:, col:col + 1],
                        accum_out=rs_s[:, col:col + 1]), inc="S")
                    c.ps_consume("b0", "S", sv)
                    c.ps_consume("b1", "S", sv)
                if ALEVEL < 3:
                    continue
                c.wait(E_V, "S", c.cnt["S"])
                rv = c.emit(E_V, lambda e, h=h: e.reciprocal(
                    out=rr_s[:, h * 4:h * 4 + 4], in_=rs_s[:, h * 4:h * 4 + 4]),
                    inc="V")
                c.emit(E_V, lambda e: e.memset(fence_s[:, 0:1], 0.0), inc="V")
                avbank = (p4, p5)[h % 2]
                c.ps_wait(E_T, f"b{4 + h % 2}")
                for qt in range(4):
                    gslot = qt
                    col = h * 4 + qt
                    nv = c.emit(E_V, lambda e, g=gslot, col=col: e.tensor_scalar_mul(
                        out=attn_s[:, g * 640:(g + 1) * 640],
                        in0=attn_s[:, g * 640:(g + 1) * 640],
                        scalar1=rr_s[:, col:col + 1]), inc="V")
                    if ALEVEL < 4:
                        continue
                    ppT = qt % 2
                    c.wait(E_T, "V", nv)
                    c.wait(E_V, "T", attnT_free[ppT])
                    tev = []
                    for bn in ("b2", "b3", "b6", "b7"):
                        c.ps_wait(E_T, bn)
                    for cc in range(5):
                        slotb = (psD2_a[:, 0:64], psD2_b[:, 0:64],
                                 p6[:, 0:64], p7[:, 0:64],
                                 psD2_a[:, 0:64])[cc]
                        if cc == 4:
                            c.wait(E_T, "V", tev[0])
                        mk = c.emit(E_T, lambda e, b=slotb, g=gslot, cc=cc:
                                    e.transpose(b.bitcast(BF16),
                                                attn_s[:, g * 640 + cc * 128:
                                                       g * 640 + (cc + 1) * 128],
                                                identb_s[:]), inc="T")
                        c.wait(E_V, "T", mk)
                        ev = c.emit(E_V, lambda e, b=slotb, ppT=ppT, cc=cc:
                                    e.tensor_copy(
                            out=attnT_s[:, ppT * 640 + cc * 128:
                                        ppT * 640 + (cc + 1) * 128],
                            in_=b.bitcast(BF16)), inc="V")
                        tev.append(ev)
                    if DBG4 and l == DBGL and w == 1 and h == 0 and qt == 0:
                        c.wait(E_G, "V", tev[-1])
                        c.dma(E_G, dbg_att[:], attnT_s[:, ppT * 640:(ppT + 1) * 640], sem="DO")
                        c.dma(E_G, dbg_at[:, 0:640], attn_s[:, 0:640], sem="DO")
                        c.dma(E_G, dbg_nm[:, 0:32], nm_s[:], sem="DO")
                        c.dma(E_G, dbg_nm[:, 32:64], rs_s[:], sem="DO")
                        c.wait(E_G, "DO", c.cnt["DO"])
                        for _e in (E_V, E_S, E_T, E_SY):
                            c.wait(_e, "DO", c.cnt["DO"])
                    attn_free[gslot] = tev[-1]
                    for bn in ("b2", "b3", "b6", "b7"):
                        c.ps_consume(bn, "V", tev[-1])
                    if ALEVEL < 5:
                        continue
                    if qt == 3:
                        tfence = c.emit(E_V, lambda e: e.memset(
                            fence_s[:, 2:3], 0.0), inc="V")
                        c.wait(E_T, "V", tfence)
                    else:
                        c.wait(E_T, "V", tev[-1] + 1)
                    c.wait(E_T, "V", vT_V)
                    for cc in range(5):
                        tm = c.mm(avbank[0:64, qt * 128:(qt + 1) * 128],
                                  vT_s[:, (qt + cc) * WIN + 64 * h:
                                       (qt + cc) * WIN + 64 * h + 64],
                                  attnT_s[:, ppT * 640 + cc * 128:
                                          ppT * 640 + (cc + 1) * 128],
                                  cc == 0, cc == 4, inc="T" if cc == 4 else None)
                    attnT_free[ppT] = tm
                if ALEVEL < 6:
                    continue
                c.wait(E_V, "T", tm)
                c.wait(E_V, "T", oT_rd_T)
                ov = c.emit(E_V, lambda e, b=avbank, hp=hp, hb=hb: e.tensor_copy(
                    out=oT_s[hp:hp + 64, hb * WIN:(hb + 1) * WIN],
                    in_=b[0:64, :]), inc="V")
                c.ps_consume(f"b{4 + h % 2}", "V", ov)
            c.emit(E_V, lambda e: e.memset(fence_s[:, 3:4], 0.0), inc="V")
            oT_V = c.cnt["V"]
            nm_free_S = c.cnt["S"]

            if STOP == "f_attn":
                break
            for i in range(8):
                wb = (wb1_s, wb2_s)[i % 2]
                c.wait(E_SY, "T", wbuf_T[i % 2])
                wld = c.dma(E_SY, bAP(wb, 0, [[1024, 128], [128, 4], [1, 128]]),
                            bAP(wo_d[l], i * 128, [[D, 128], [128 * D, 4], [1, 128]]))
                c.wait(E_T, "DS", wld)
                bank = (p6, p7)[i % 2]
                c.ps_wait(E_T, f"b{6 + i % 2}")
                c.wait(E_T, "V", oT_V)
                for kp in range(4):
                    tm = c.mm(bank[:], wb[:, kp * 128:(kp + 1) * 128],
                              oT_s[:, kp * WIN:(kp + 1) * WIN], kp == 0, kp == 3,
                              inc="T" if kp == 3 else None)
                wbuf_T[i % 2] = tm
                oT_rd_T = tm
                c.wait(E_V, "T", tm)
                rv = c.emit(E_V, lambda e, i=i, b=bank: e.tensor_tensor(
                    out=xw2_s[:, i * (WIN + 1) + 1:i * (WIN + 1) + 1 + WIN],
                    in0=xw2_s[:, i * (WIN + 1) + 1:i * (WIN + 1) + 1 + WIN],
                    in1=b[:], op=OP.add), inc="V")
                c.ps_consume(f"b{6 + i % 2}", "V", rv)
            x2_V = c.cnt["V"]
            if DBG4 and l == DBGL and DBG3A:
                c.wait(E_G, "V", oT_V)
                c.dma(E_G, dbg_oT[:, w * 4 * WIN:(w + 1) * 4 * WIN], oT_s[:], sem="DO")
                c.wait(E_G, "DO", c.cnt["DO"])
                for _e in (E_V, E_S, E_T, E_SY):
                    c.wait(_e, "DO", c.cnt["DO"])
            if DBG3 and l == DBGL and DBG3A:
                c.wait(E_G, "V", x2_V)
                c.dma(E_G, dbg_oT[:, w * 4 * WIN:(w + 1) * 4 * WIN], oT_s[:], sem="DO")
                c.dma(E_G, bAP(dbg_xa, dbg_xa.offset + w * 8 * WIN,
                               [dbg_xa.ap[0], [8 * T // 8, 0]][0:1] + [[WIN, 8], [1, WIN]]),
                      bAP(xw2_s, xw2_s.offset + 1, [xw2_s.ap[0], [WIN + 1, 8], [1, WIN]]),
                      sem="DO")
                c.wait(E_G, "DO", c.cnt["DO"])
                for _e in (E_V, E_S, E_T, E_SY):
                    c.wait(_e, "DO", c.cnt["DO"])

            hs = (w % 2) * 8
            hp_ = ((w + 1) % 2) * 8
            c.emit(E_V, lambda e, hs=hs: e.tensor_copy(
                out=halo2_s[:, hs:hs + 8],
                in_=bAP(xw2_s, xw2_s.offset + WIN, [xw2_s.ap[0], [WIN + 1, 8], [1, 1]])))
            hv = c.emit(E_V, lambda e, hp_=hp_: e.tensor_copy(
                out=bAP(xw2_s, xw2_s.offset, [xw2_s.ap[0], [WIN + 1, 8], [1, 1]]),
                in_=halo2_s[:, hp_:hp_ + 8]), inc="V")

            if STOP == "f_wo":
                break
            lnm = layer_norm(lambda pb: shift_ap(pb),
                             lambda pb: hn_s[:, pb * WIN:(pb + 1) * WIN],
                             8, WIN, sq2_s, hv, ready_T=hn_rd_T,
                             gb=(lng_s[l], lnb_s[l]))

            if STOP == "f_ln":
                break
            if DBG3 and l == DBGL:
                c.wait(E_G, "V", lnm)
                c.dma(E_G, bAP(dbg_hn, dbg_hn.offset + w * 8 * WIN,
                               [dbg_hn.ap[0], [WIN, 8], [1, WIN]]),
                      hn_s[:].rearrange("p (a b) -> p a b", b=WIN), sem="DO")
                c.wait(E_G, "DO", c.cnt["DO"])
                for _e in (E_V, E_S, E_T, E_SY):
                    c.wait(_e, "DO", c.cnt["DO"])
            gelu_free_V = 0
            for it in range(NIT1):
                na = min(128, HID - it * 128)
                wbA, wbB = (wb1_s, wb2_s)
                c.wait(E_SY, "T", wbuf_T[0])
                wldA = c.dma(E_SY, bAP(wbA, 0, [[1024, 128], [128, 8], [1, na]]),
                             bAP(w1_d[l], it * 128,
                                 [[2 * HID, 128], [128 * 2 * HID, 8], [1, na]]))
                c.wait(E_S, "T", wbuf_T[1])
                wldB = c.dma(E_S, bAP(wbB, 0, [[1024, 128], [128, 8], [1, na]]),
                             bAP(w1_d[l], HID + it * 128,
                                 [[2 * HID, 128], [128 * 2 * HID, 8], [1, na]]))
                c.wait(E_T, "DS", wldA)
                c.wait(E_T, "DA", wldB)
                c.wait(E_T, "V", lnm)
                c.ps_wait(E_T, "b4")
                c.ps_wait(E_T, "b5")
                for kp in range(8):
                    tma = c.mm(p4[0:na, :], wbA[:, kp * 128:kp * 128 + na],
                               hn_s[:, kp * WIN:(kp + 1) * WIN], kp == 0, kp == 7,
                               inc="T" if kp == 7 else None)
                for kp in range(8):
                    tmb = c.mm(p5[0:na, :], wbB[:, kp * 128:kp * 128 + na],
                               hn_s[:, kp * WIN:(kp + 1) * WIN], kp == 0, kp == 7,
                               inc="T" if kp == 7 else None)
                wbuf_T[0] = tma
                wbuf_T[1] = tmb
                hn_rd_T = tmb
                c.wait(E_S, "T", tmb)
                c.wait(E_S, "V", gelu_free_V)
                sv = c.emit(E_S, lambda e, it=it, na=na: e.activation(
                    out=sq2_s[0:na, :], in_=p5[0:na, :], func=AF.Gelu,
                    bias=b1g_s[l][0:na, it:it + 1]), inc="S")
                c.ps_consume("b5", "S", sv)
                c.wait(E_V, "S", sv)
                c.wait(E_V, "T", h2_rd_T)
                gv = c.emit(E_V, lambda e, it=it, na=na: e.scalar_tensor_tensor(
                    out=h2_s[0:na, it * WIN:(it + 1) * WIN], in0=p4[0:na, :],
                    scalar=b1a_s[l][0:na, it:it + 1], in1=sq2_s[0:na, :],
                    op0=OP.add, op1=OP.mult), inc="V")
                gelu_free_V = gv
                c.ps_consume("b4", "V", gv)
            h2_V = c.cnt["V"]

            if STOP == "f_w1":
                break
            for i in range(8):
                wb = (w2bufA, w2bufB)[i % 2]
                c.wait(E_G, "T", w2buf_T[i % 2])
                wld = c.dma(E_G, bAP(wb, 0, [[2816, 128], [128, 21], [1, 128]]),
                            bAP(w2_d[l], i * 128, [[D, 128], [128 * D, 21], [1, 128]]))
                wld = c.dma(E_G, bAP(wb, 21 * 128, [[2816, 42], [1, 128]]),
                            bAP(w2_d[l], 21 * 128 * D + i * 128, [[D, 42], [1, 128]]))
                if W2LEVEL < 2:
                    continue
                c.wait(E_T, "DG", wld)
                bank = (p6, p7)[i % 2]
                c.ps_wait(E_T, f"b{6 + i % 2}")
                c.wait(E_T, "V", h2_V)
                for kp in range(NIT1):
                    na = min(128, HID - kp * 128)
                    tm = c.mm(bank[:], wb[0:na, kp * 128:(kp + 1) * 128],
                              h2_s[0:na, kp * WIN:(kp + 1) * WIN],
                              kp == 0, kp == NIT1 - 1,
                              inc="T" if kp == NIT1 - 1 else None)
                w2buf_T[i % 2] = tm
                h2_rd_T = tm
                if W2LEVEL < 3:
                    c.wait(E_S, "T", tm)
                    dmy = c.emit(E_S, lambda e, b=bank: e.copy(
                        out=sq2_s[:], in_=b[:]), inc="S")
                    c.ps_consume(f"b{6 + i % 2}", "S", dmy)
                    continue
                c.wait(E_V, "T", tm)
                rv = c.emit(E_V, lambda e, i=i, b=bank: e.scalar_tensor_tensor(
                    out=xw2_s[:, i * (WIN + 1) + 1:i * (WIN + 1) + 1 + WIN],
                    in0=b[:], scalar=b2_s[l][:, i:i + 1],
                    in1=xw2_s[:, i * (WIN + 1) + 1:i * (WIN + 1) + 1 + WIN],
                    op0=OP.add, op1=OP.add), inc="V")
                c.ps_consume(f"b{6 + i % 2}", "V", rv)
            xout_V = c.cnt["V"]

            kv_rd_T = c.cnt["T"]
            qh_rd_T = c.cnt["T"]

            if not final and W2LEVEL >= 4:
                c.wait(E_G, "V", xout_V)
                sd = c.dma(E_G, bAP(dst_h, 8 + t0,
                                    [[T + 8, 128], [128 * (T + 8), 8], [1, WIN]]),
                           bAP(xw2_s, xw2_s.offset + 1,
                               [xw2_s.ap[0], [WIN + 1, 8], [1, WIN]]))
                c.wait(E_SY, "DG", sd)
                xw2_free = xout_V
            elif not final:
                xw2_free = xout_V if W2LEVEL >= 3 else c.cnt["V"]
            else:
                c.wait(E_SY, "V", c.cnt["V"])
                c.wait(E_SY, "T", c.cnt["T"])
                x1ld = c.dma(E_SY, x3_s[:].rearrange("p (a b) -> p a b", b=WIN),
                             bAP(sa_h, 8 + t0,
                                 [[T + 8, 128], [128 * (T + 8), 8], [1, WIN]]))
                c.wait(E_V, "DS", x1ld)
                av = 0
                for pb in range(8):
                    av = c.emit(E_V, lambda e, pb=pb: e.tensor_tensor(
                        out=x3_s[:, pb * WIN:(pb + 1) * WIN],
                        in0=x3_s[:, pb * WIN:(pb + 1) * WIN],
                        in1=xw2_s[:, pb * (WIN + 1) + 1:pb * (WIN + 1) + 1 + WIN],
                        op=OP.add), inc="V")
                if DBG3 and DBG3B:
                    c.wait(E_G, "V", av)
                    c.dma(E_G, bAP(dbg_hn, dbg_hn.offset + w * 8 * WIN,
                                   [dbg_hn.ap[0], [WIN, 8], [1, WIN]]),
                          x3_s[:].rearrange("p (a b) -> p a b", b=WIN), sem="DO")
                    c.wait(E_G, "DO", c.cnt["DO"])
                    for _e in (E_V, E_S, E_T, E_SY):
                        c.wait(_e, "DO", c.cnt["DO"])
                lnf = layer_norm(lambda pb: x3_s[:, pb * WIN:(pb + 1) * WIN],
                                 lambda pb: x3_s[:, pb * WIN:(pb + 1) * WIN],
                                 8, WIN, sq2_s, av)
                xw2_free = av
                c.ps_wait(E_T, "psD2a")
                c.ps_wait(E_T, "b3")
                c.wait(E_T, "V", lnf)
                for tch in range(4):
                    ob = ot2_s[:, (tch % 2) * D:(tch % 2 + 1) * D]
                    if len(odma) >= 2:
                        c.wait(E_V, "DO", odma[-2])
                    evl = None
                    for pb in range(8):
                        slot = pb % 4
                        bankap = (psD_a[:, 0:128], psD_b[:, 0:128],
                                  psD2_a[:, 0:128], psD2_b[:, 0:128])[slot]
                        if ev4[slot]:
                            c.wait(E_T, "V", ev4[slot])
                        mk = c.emit(E_T, lambda e, b=bankap, pb=pb, tch=tch:
                                    e.transpose(b, x3_s[:, pb * WIN + tch * 128:
                                                        pb * WIN + (tch + 1) * 128]
                                                .bitcast(F32), identf_s[:]),
                                    inc="T")
                        c.wait(E_V, "T", mk)
                        evl = c.emit(E_V, lambda e, b=bankap, ob=ob, pb=pb:
                                     e.tensor_copy(out=ob[:, pb * 128:(pb + 1) * 128],
                                                   in_=b), inc="V")
                        ev4[slot] = evl
                    c.wait(E_G, "V", evl)
                    dm = c.dma(E_G, out_d[t0 + tch * 128:t0 + (tch + 1) * 128, :],
                               ob, sem="DO")
                    odma.append(dm)
                for bn in ("b0", "b1", "b2", "b3"):
                    c.ps_consume(bn, "V", ev4[3])
                hn_rd_T = c.cnt["T"]
                x3_free[w % 2] = lnf
        return

    ffn_layer(0, sa_h, sb_h, final=False)
    if STOP in ("f_qkv", "f_v", "f_attn", "f_wo", "f_ln", "f_w1"):
        return c.build()
    c.barrier()
    if STOP == "ffn0":
        return c.build()
    ffn_layer(1, sb_h, None, final=True)

    c.wait(E_G, "DO", c.cnt["DO"])
    c.wait(E_SY, "DO", c.cnt["DO"])

    nc = c.build()
    return nc


_CACHED = {}


def kernel(**inputs):
    x_full = np.asarray(inputs["L0_attn_x"], dtype=np.float32)
    params = reorg_params(inputs)
    consts = host_constants()
    if "nc" not in _CACHED:
        _CACHED["nc"] = build_program()
    nc = _CACHED["nc"]
    base = {}
    base.update(params)
    base.update(consts)
    in_maps = []
    for cid in range(NCORES):
        m = dict(base)
        m["x"] = np.ascontiguousarray(x_full[cid])
        in_maps.append(m)
    res = run_bass_kernel_spmd(nc, in_maps, core_ids=list(range(NCORES)))
    out = np.stack([np.asarray(r["out"]) for r in res.results], axis=0)
    return out.astype(np.float32)
